# version 01
# baseline (speedup 1.0000x reference)
import math
from contextlib import ExitStack, contextmanager

import numpy as np
import ml_dtypes
import concourse.bass as bass
import concourse.mybir as mybir
from concourse.bass_utils import run_bass_kernel_spmd

F32 = mybir.dt.float32
BF16 = mybir.dt.bfloat16
AF = mybir.ActivationFunctionType
ALU = mybir.AluOpType
AX = mybir.AxisListType.X
ts = bass.ts

T = 2048
D = 2048
NT = T // 128
KCD = D // 128
HEADS, KVH, HD = 32, 8, 64
INTER = 5632
NL1 = 4
NEXP, MI = 8, 1408
MIC = MI // 128

DUMP = ()
MAX_PHASE = 99

Q_PERM = [(2 * c, 2 * c + 1) for c in range(16)]


class Sem:
    __slots__ = ("h", "n")

    def __init__(self, h):
        self.h = h
        self.n = 0

    def inc(self, ins, k=1):
        ins.then_inc(self.h, k)
        self.n += k
        return self.n


class Cx:
    def __init__(self, nc, es):
        self.nc = nc
        self.es = es
        self.cur = es

    @contextmanager
    def phase(self):
        with ExitStack() as ps:
            old = self.cur
            self.cur = ps
            try:
                yield
            finally:
                self.cur = old

    def sem(self, name):
        return Sem(self.es.enter_context(self.nc.semaphore(name)))

    def sbuf(self, name, shape, dtype, persistent=False, scope=None, side=None):
        if scope is None:
            scope = self.es if persistent else self.cur
        return scope.enter_context(self.nc.sbuf_tensor(name, shape, dtype, side=side))

    def psum(self, name, shape, dtype=F32, persistent=False):
        scope = self.es if persistent else self.cur
        return scope.enter_context(self.nc.psum_tensor(name, shape, dtype))



def p0_weight_prep(cx, io):
    nc = cx.nc
    ld = cx.sem("p0ld")
    dv = cx.sem("p0dv")
    st = cx.sem("p0st")
    bufs = [cx.sbuf(f"p0b{i}", [128, 4 * 2048], BF16) for i in range(2)]
    f0 = cx.sbuf("p0f0", [128, INTER], F32)
    fb = [cx.sbuf(f"p0fb{i}", [128, INTER], BF16) for i in range(2)]
    dsc = cx.sbuf("p0dsc", [128, 16], F32)

    jobs = []

    def chunked(src, dst, rows, cols, cpb):
        for r0 in range(0, rows // 128, cpb):
            c = min(cpb, rows // 128 - r0)
            s = src[r0 * 128:(r0 + c) * 128, :].rearrange("(c p) n -> p c n", p=128)
            d = dst[r0 * 128:(r0 + c) * 128, :].rearrange("(c p) n -> p c n", p=128)
            jobs.append((d, s))

    chunked(io["L0_ffn_gate_w"].ap(), io["sc_gate_w"].ap(), INTER, D, 4)
    chunked(io["L0_ffn_up_w"].ap(), io["sc_up_w"].ap(), INTER, D, 4)
    for e in range(NEXP):
        chunked(io["L1_ffn_w1"].ap()[e], io["sc_w1"].ap()[e], 2 * MI, D, 4)
        chunked(io["L1_ffn_w2"].ap()[e], io["sc_w2"].ap()[e], D, MI, 4)

    with nc.Block() as b:
        @b.gpsimd
        def _(g):
            for j, (dst, src) in enumerate(jobs):
                if j >= 2:
                    g.wait_ge(st.h, 16 * (j - 1))
                c, n = dst.shape[1], dst.shape[2]
                buf = bufs[j % 2][:, 0:c * n].rearrange("p (c n) -> p c n", c=c)
                ld.inc(g.dma_start(buf, src), 16)

        @b.sync
        def _(s):
            for j, (dst, src) in enumerate(jobs):
                s.wait_ge(ld.h, 16 * (j + 1))
                c, n = dst.shape[1], dst.shape[2]
                buf = bufs[j % 2][:, 0:c * n].rearrange("p (c n) -> p c n", c=c)
                st.inc(s.dma_start(dst, buf), 16)
            s.dma_start(dsc[:], io["c_down_s_t"].ap()).then_inc(ld.h, 16)
            for r in range(KCD):
                if r >= 1:
                    s.wait_ge(dv.h, r)
                s.dma_start(f0[:], io["L0_ffn_down_w"].ap()[r * 128:(r + 1) * 128, :]) \
                    .then_inc(ld.h, 16)
            ld.n += 16 * (KCD + 1)
            s.wait_ge(st.h, st.n + 16 * KCD)

        st_base = st.n
        ld_base = ld.n - 16 * (KCD + 1)

        @b.vector
        def _(v):
            for r in range(KCD):
                v.wait_ge(ld.h, ld_base + 16 * (r + 2))
                if r >= 2:
                    v.wait_ge(st.h, st_base + 16 * (r - 1))
                dv.inc(v.tensor_scalar_mul(fb[r % 2][:], f0[:], dsc[:, r:r + 1]))

        @b.scalar
        def _(sc):
            for r in range(KCD):
                sc.wait_ge(dv.h, dv.n - KCD + r + 1)
                sc.dma_start(io["sc_down_w"].ap()[r * 128:(r + 1) * 128, :],
                             fb[r % 2][:]).then_inc(st.h, 16)
        st.n += 16 * KCD




def p1_ln0(cx, io):
    nc = cx.nc
    ld = cx.sem("p1ld")
    dv = cx.sem("p1dv")
    ac = cx.sem("p1ac")
    st = cx.sem("p1st")
    xs = [cx.sbuf(f"p1x{i}", [128, D], F32) for i in range(2)]
    sq = cx.sbuf("p1sq", [128, D], F32)
    ssum = cx.sbuf("p1ssum", [128, 1], F32)
    ssq = cx.sbuf("p1ssq", [128, 1], F32)
    nmean = cx.sbuf("p1nmean", [128, 1], F32)
    var = cx.sbuf("p1var", [128, 1], F32)
    rstd = cx.sbuf("p1rstd", [128, 1], F32)
    tmp = cx.sbuf("p1tmp", [128, D], F32)
    xnbs = [cx.sbuf(f"p1xnb{i}", [128, D], BF16) for i in range(2)]

    with nc.Block() as b:
        @b.sync
        def _(s):
            for i in range(NT):
                if i >= 2:
                    s.wait_ge(dv.h, 7 * (i - 1))
                ld.inc(s.dma_start(xs[i % 2][:], io["x"].ap()[ts(i, 128), :]), 16)

        @b.scalar
        def _(sc):
            for i in range(NT):
                sc.wait_ge(ld.h, 16 * (i + 1))
                sc.activation(sq[:], xs[i % 2][:], AF.Square, accum_out=ssq[:]) \
                    .then_inc(ac.h, 1)
                sc.wait_ge(dv.h, 7 * i + 6)
                sc.activation(rstd[:], rstd[:], AF.Sqrt).then_inc(ac.h, 1)
                sc.wait_ge(dv.h, 7 * (i + 1))
        ac.n = 2 * NT

        @b.vector
        def _(v):
            for i in range(NT):
                x = xs[i % 2]
                v.wait_ge(ld.h, 16 * (i + 1))
                v.reduce_sum(ssum[:], x[:], AX).then_inc(dv.h, 1)
                v.wait_ge(dv.h, 7 * i + 1)
                v.wait_ge(ac.h, 2 * i + 1)
                v.tensor_scalar_mul(nmean[:], ssum[:], -1.0 / D).then_inc(dv.h, 1)
                v.wait_ge(dv.h, 7 * i + 2)
                v.tensor_scalar_mul(var[:], ssq[:], 1.0 / D)
                v.tensor_scalar(ssq[:], nmean[:], nmean[:], None, ALU.mult) \
                    .then_inc(dv.h, 1)
                v.wait_ge(dv.h, 7 * i + 3)
                v.tensor_tensor(var[:], var[:], ssq[:], ALU.subtract).then_inc(dv.h, 1)
                v.wait_ge(dv.h, 7 * i + 4)
                v.tensor_scalar(var[:], var[:], 1e-5, None, ALU.add).then_inc(dv.h, 1)
                v.wait_ge(dv.h, 7 * i + 5)
                v.reciprocal(rstd[:], var[:]).then_inc(dv.h, 1)
                v.wait_ge(ac.h, 2 * i + 2)
                if i >= 2:
                    v.wait_ge(st.h, 16 * (i - 1))
                v.tensor_scalar(tmp[:], x[:], nmean[:], None, ALU.add)
                v.tensor_scalar(xnbs[i % 2][:], tmp[:], rstd[:], None, ALU.mult) \
                    .then_inc(dv.h, 1)
        dv.n = 7 * NT

        @b.gpsimd
        def _(g):
            for i in range(NT):
                g.wait_ge(dv.h, 7 * (i + 1))
                st.inc(g.dma_start(io["sc_xn"].ap()[ts(i, 128), :], xnbs[i % 2][:]), 16)
            g.wait_ge(st.h, st.n)



def p2_l0_qkv(cx, io, q_fm):
    nc = cx.nc

    def load_xn_fm(s, ld, xn_fm):
        for bb in range(NT):
            ld.inc(s.dma_start(
                bass.AP(xn_fm, bb * 128, [[KCD * T, 128], [T, KCD], [1, 128]]),
                io["sc_xn"].ap()[ts(bb, 128), :],
                transpose=True), 16)

    wq = io["L0_attn_Wq"].ap()

    def q_cols_ap(nb):
        return (wq[:, nb * 512:(nb + 1) * 512]
                .rearrange("(c p) n -> p c n", p=128))

    with cx.phase():
        ld = cx.sem("p2ald")
        wl = cx.sem("p2awl")
        mm = cx.sem("p2amm")
        ev = cx.sem("p2aev")
        rp = cx.sem("p2arp")
        tr = cx.sem("p2atr")
        xn_fm = cx.sbuf("p2xnfm", [128, KCD * T], BF16)
        cosb = cx.sbuf("p2cos", [128, NT * 64], F32)
        sinb = cx.sbuf("p2sin", [128, NT * 64], F32)
        wbuf = cx.sbuf("p2wq", [128, KCD * 512], BF16)
        qf = [cx.sbuf(f"p2qf{i}", [128, 512], F32) for i in range(2)]
        qsw = cx.sbuf("p2qsw", [128, 512], F32)
        qb = [cx.sbuf(f"p2qb{i}", [128, 512], BF16) for i in range(2)]
        ps = [cx.psum(f"p2ps{i}", [128, 512]) for i in range(8)]

        with nc.Block() as b:
            @b.sync
            def _(s):
                load_xn_fm(s, ld, xn_fm)
                ld.inc(s.dma_start(
                    cosb[:].rearrange("p (i n) -> p i n", i=NT),
                    io["c_cos"].ap().rearrange("(i p) n -> p i n", p=128)), 16)
                ld.inc(s.dma_start(
                    sinb[:].rearrange("p (i n) -> p i n", i=NT),
                    io["c_sin"].ap().rearrange("(i p) n -> p i n", p=128)), 16)
                for nb in range(4):
                    for i in range(NT):
                        it = nb * NT + i
                        s.wait_ge(rp.h, it + 1)
                        dst = bass.AP(q_fm, 4 * nb * T + i * 128,
                                      [[KCD * T, 128], [T, 4], [1, 128]])
                        tr.inc(s.dma_start(dst, qb[it % 2][:], transpose=True), 16)

            @b.gpsimd
            def _(g):
                for nb in range(4):
                    if nb >= 1:
                        g.wait_ge(mm.h, 16 * nb)
                    wl.inc(g.dma_start(
                        wbuf[:].rearrange("p (c n) -> p c n", c=KCD),
                        q_cols_ap(nb)), 16)
                g.wait_ge(tr.h, 16 * 4 * NT)

            @b.tensor
            def _(t):
                t.wait_ge(ld.h, 16 * (KCD + 2))
                for nb in range(4):
                    t.wait_ge(wl.h, 16 * (nb + 1))
                    for i in range(NT):
                        it = nb * NT + i
                        bank = it % 8
                        if it >= 8:
                            t.wait_ge(ev.h, it - 8 + 1)
                        for kc in range(KCD):
                            ins = t.matmul(
                                ps[bank][:],
                                xn_fm[:, kc * T + i * 128: kc * T + (i + 1) * 128],
                                wbuf[:, ts(kc, 512)],
                                start=(kc == 0), stop=(kc == KCD - 1))
                        mm.inc(ins)

            @b.scalar
            def _(sc):
                for it in range(4 * NT):
                    sc.wait_ge(mm.h, it + 1)
                    sc.wait_ge(rp.h, max(0, it - 1))
                    ev.inc(sc.activation(qf[it % 2][:], ps[it % 8][:], AF.Copy,
                                         scale=0.125))

            @b.vector
            def _(v):
                for nb in range(4):
                    for i in range(NT):
                        it = nb * NT + i
                        v.wait_ge(ev.h, it + 1)
                        if it >= 2:
                            v.wait_ge(tr.h, 16 * (it - 1))
                        x = qf[it % 2]
                        co = bass.AP(cosb, i * 64, [[NT * 64, 128], [0, 8], [1, 64]])
                        si = bass.AP(sinb, i * 64, [[NT * 64, 128], [0, 8], [1, 64]])
                        sw = qsw[:].rearrange("p (h two j) -> p h two j", two=2, j=32)
                        xv = x[:].rearrange("p (h two j) -> p h two j", two=2, j=32)
                        v.tensor_copy(sw[:, :, 0, :], xv[:, :, 1, :])
                        v.tensor_copy(sw[:, :, 1, :], xv[:, :, 0, :])
                        x3 = x[:].rearrange("p (h j) -> p h j", j=64)
                        v.tensor_tensor(x3, x3, co, ALU.mult)
                        v.tensor_tensor(qsw[:].rearrange("p (h j) -> p h j", j=64),
                                        qsw[:].rearrange("p (h j) -> p h j", j=64),
                                        si, ALU.mult)
                        rp.inc(v.tensor_tensor(qb[it % 2][:], x[:], qsw[:], ALU.add))

    k_fm = cx.sbuf("k_fm", [128, 8 * T], BF16)
    with cx.phase():
        ld = cx.sem("p2bld")
        wl = cx.sem("p2bwl")
        mm = cx.sem("p2bmm")
        ev = cx.sem("p2bev")
        rp = cx.sem("p2brp")
        tr = cx.sem("p2btr")
        xn_fm = cx.sbuf("p2bxnfm", [128, KCD * T], BF16)
        cosb = cx.sbuf("p2bcos", [128, NT * 64], F32)
        sinb = cx.sbuf("p2bsin", [128, NT * 64], F32)
        wbuf = cx.sbuf("p2bw", [128, KCD * 512], BF16)
        qf = [cx.sbuf(f"p2bqf{i}", [128, 512], F32) for i in range(2)]
        qsw = cx.sbuf("p2bqsw", [128, 512], F32)
        qb = [cx.sbuf(f"p2bqb{i}", [128, 512], BF16) for i in range(2)]
        k2 = [cx.sbuf(f"p2bk2{i}", [128, 1024], BF16) for i in range(2)]
        ps = [cx.psum(f"p2bps{i}", [128, 512]) for i in range(8)]

        with nc.Block() as b:
            @b.sync
            def _(s):
                load_xn_fm(s, ld, xn_fm)
                ld.inc(s.dma_start(
                    cosb[:].rearrange("p (i n) -> p i n", i=NT),
                    io["c_cos"].ap().rearrange("(i p) n -> p i n", p=128)), 16)
                ld.inc(s.dma_start(
                    sinb[:].rearrange("p (i n) -> p i n", i=NT),
                    io["c_sin"].ap().rearrange("(i p) n -> p i n", p=128)), 16)
                for i in range(NT):
                    s.wait_ge(rp.h, 3 * (i + 1))
                    dst = bass.AP(k_fm, i * 128, [[8 * T, 128], [T, 8], [1, 128]])
                    tr.inc(s.dma_start(dst, k2[i % 2][:], transpose=True), 16)

            @b.gpsimd
            def _(g):
                wl.inc(g.dma_start(
                    wbuf[:].rearrange("p (c n) -> p c n", c=KCD),
                    io["L0_attn_Wk"].ap().rearrange("(c p) n -> p c n", p=128)), 16)
                g.wait_ge(tr.h, 16 * NT)

            @b.tensor
            def _(t):
                t.wait_ge(ld.h, 16 * (KCD + 2))
                t.wait_ge(wl.h, 16)
                for i in range(NT):
                    bank = i % 8
                    if i >= 8:
                        t.wait_ge(ev.h, i - 8 + 1)
                    for kc in range(KCD):
                        ins = t.matmul(
                            ps[bank][:],
                            xn_fm[:, kc * T + i * 128: kc * T + (i + 1) * 128],
                            wbuf[:, ts(kc, 512)],
                            start=(kc == 0), stop=(kc == KCD - 1))
                    mm.inc(ins)

            @b.scalar
            def _(sc):
                for i in range(NT):
                    sc.wait_ge(mm.h, i + 1)
                    sc.wait_ge(rp.h, max(0, 3 * (i - 1)))
                    ev.inc(sc.activation(qf[i % 2][:], ps[i % 8][:], AF.Copy))

            @b.vector
            def _(v):
                for i in range(NT):
                    v.wait_ge(ev.h, i + 1)
                    if i >= 2:
                        v.wait_ge(tr.h, 16 * (i - 1))
                    x = qf[i % 2]
                    co = bass.AP(cosb, i * 64, [[NT * 64, 128], [0, 8], [1, 64]])
                    si = bass.AP(sinb, i * 64, [[NT * 64, 128], [0, 8], [1, 64]])
                    sw = qsw[:].rearrange("p (h two j) -> p h two j", two=2, j=32)
                    xv = x[:].rearrange("p (h two j) -> p h two j", two=2, j=32)
                    v.tensor_copy(sw[:, :, 0, :], xv[:, :, 1, :])
                    v.tensor_copy(sw[:, :, 1, :], xv[:, :, 0, :])
                    x3 = x[:].rearrange("p (h j) -> p h j", j=64)
                    v.tensor_tensor(x3, x3, co, ALU.mult)
                    v.tensor_tensor(qsw[:].rearrange("p (h j) -> p h j", j=64),
                                    qsw[:].rearrange("p (h j) -> p h j", j=64),
                                    si, ALU.mult)
                    rp.inc(v.tensor_tensor(qb[i % 2][:], x[:], qsw[:], ALU.add))
                    kk = k2[i % 2][:].rearrange("p (g d x) -> p g d x", d=2, x=64)
                    kv = qb[i % 2][:].rearrange("p (g x) -> p g x", x=64)
                    rp.inc(v.tensor_copy(kk[:, :, 0, :], kv))
                    rp.inc(v.tensor_copy(kk[:, :, 1, :], kv))

    v_tm = cx.sbuf("v_tm", [128, NT * 512], BF16)
    with cx.phase():
        ld = cx.sem("p2cld")
        wl = cx.sem("p2cwl")
        mm = cx.sem("p2cmm")
        ev = cx.sem("p2cev")
        xn_fm = cx.sbuf("p2cxnfm", [128, KCD * T], BF16)
        wbuf = cx.sbuf("p2cw", [128, KCD * 512], BF16)
        ps = [cx.psum(f"p2cps{i}", [128, 512]) for i in range(8)]

        with nc.Block() as b:
            @b.sync
            def _(s):
                load_xn_fm(s, ld, xn_fm)

            @b.gpsimd
            def _(g):
                wl.inc(g.dma_start(
                    wbuf[:].rearrange("p (c n) -> p c n", c=KCD),
                    io["L0_attn_Wv"].ap().rearrange("(c p) n -> p c n", p=128)), 16)

            @b.tensor
            def _(t):
                t.wait_ge(ld.h, 16 * KCD)
                t.wait_ge(wl.h, 16)
                for i in range(NT):
                    bank = i % 8
                    if i >= 8:
                        t.wait_ge(ev.h, i - 8 + 1)
                    for kc in range(KCD):
                        ins = t.matmul(
                            ps[bank][:],
                            xn_fm[:, kc * T + i * 128: kc * T + (i + 1) * 128],
                            wbuf[:, ts(kc, 512)],
                            start=(kc == 0), stop=(kc == KCD - 1))
                    mm.inc(ins)

            @b.scalar
            def _(sc):
                for i in range(NT):
                    sc.wait_ge(mm.h, i + 1)
                    ev.inc(sc.activation(v_tm[:, ts(i, 512)], ps[i % 8][:], AF.Copy))
    return k_fm, v_tm



def p3_l0_attn(cx, io, q_fm, k_fm, v_tm, attn_fm):
    nc = cx.nc
    pe = cx.sem("p3pe")
    dv = cx.sem("p3dv")
    ac = cx.sem("p3ac")
    tr = cx.sem("p3tr")
    ld = cx.sem("p3ld")
    mask = cx.sbuf("p3mask", [128, 128], F32)
    probs = [cx.sbuf(f"p3pr{i}", [128, 2048], BF16) for i in range(2)]
    probsT = [cx.sbuf(f"p3pt{i}", [128, 2048], BF16) for i in range(2)]
    rs = [cx.sbuf(f"p3rs{i}", [128, 1], F32) for i in range(2)]
    rsum = cx.sbuf("p3rsum", [128, 1], F32)
    rinv = cx.sbuf("p3rinv", [128, 1], F32)
    nmax = cx.sbuf("p3nmax", [128, 1], F32)
    mx = [cx.sbuf(f"p3mx{i}", [128, 1], F32) for i in range(2)]
    scp = [cx.psum(f"p3sc{i}", [128, 1024]) for i in range(4)]

    iters = [(h, qt) for h in range(HEADS) for qt in range(NT)]
    NIT = len(iters)

    pool = [0, 0, 0, 0]
    ptr = 0
    pe_n = [0]
    dv_n = [0]
    ac_n = [0]
    plan = []
    for i, (h, qt) in enumerate(iters):
        L = (qt + 1) * 128
        C = 1 if L <= 1024 else 2
        p = {"h": h, "qt": qt, "L": L, "C": C}
        tens = []
        for c in range(C):
            t = ptr % 4
            ptr += 1
            tens.append([t, pool[t], 0])
        tpv = ptr % 4
        ptr += 1
        p["pv"] = [tpv, pool[tpv], 0]
        p["tens"] = tens
        for c in range(C):
            pe_n[0] += 1
            tens[c][2] = pe_n[0]
        pe_n[0] += 1
        p["pv"][2] = pe_n[0]
        dv_n[0] += 1
        p["dv_mask"] = dv_n[0]
        dv_n[0] += C
        if C == 2:
            dv_n[0] += 1
        p["dv_nmax"] = dv_n[0]
        if C == 2:
            dv_n[0] += 1
        dv_n[0] += 1
        p["dv_recip"] = dv_n[0]
        dv_n[0] += 1
        p["dv_norm"] = dv_n[0]
        if i > 0:
            ac_n[0] += 1
            plan[i - 1]["ac_evac"] = ac_n[0]
            ppv = plan[i - 1]["pv"]
            pool[ppv[0]] = ac_n[0]
        for c in range(C):
            ac_n[0] += 1
            p.setdefault("ac_exp", []).append(ac_n[0])
            pool[tens[c][0]] = ac_n[0]
        p["ac_exps"] = ac_n[0]
        plan.append(p)
    ac_n[0] += 1
    plan[-1]["ac_evac"] = ac_n[0]

    with nc.Block() as b:
        @b.tensor
        def _(t):
            t.wait_ge(ld.h, 16)
            for i, p in enumerate(plan):
                h, qt, L, C = p["h"], p["qt"], p["L"], p["C"]
                base = (h % 2) * 64
                g = h // 4
                lhs = q_fm[base:base + 64, (h // 2) * T + qt * 128:
                           (h // 2) * T + (qt + 1) * 128]
                for c in range(C):
                    tid, rel, _ = p["tens"][c]
                    t.wait_ge(ac.h, rel)
                    l0 = c * 1024
                    ncols = min(1024, L - l0)
                    ins = None
                    for n0 in range(0, ncols, 512):
                        w = min(512, ncols - n0)
                        ins = t.matmul(
                            scp[tid][:, n0:n0 + w],
                            lhs,
                            k_fm[base:base + 64,
                                 g * T + l0 + n0: g * T + l0 + n0 + w],
                            start=True, stop=True, tile_position=(base, 0))
                    ins.then_inc(pe.h, 1)
                tpv, relpv, _ = p["pv"]
                t.wait_ge(ac.h, relpv)
                t.wait_ge(tr.h, 16 * (i + 1))
                ins = None
                for kc in range(L // 128):
                    ins = t.matmul(
                        scp[tpv][base:base + 64, 0:128],
                        v_tm[:, kc * 512 + g * 64: kc * 512 + (g + 1) * 64],
                        probsT[i % 2][:, kc * 128:(kc + 1) * 128],
                        start=(kc == 0), stop=(kc == L // 128 - 1),
                        tile_position=(0, base))
                ins.then_inc(pe.h, 1)

        @b.vector
        def _(v):
            v.wait_ge(ld.h, 16)
            for i, p in enumerate(plan):
                qt, L, C = p["qt"], p["L"], p["C"]
                dtid = p["tens"][C - 1][0]
                dcol = (L - 128) - (C - 1) * 1024
                v.wait_ge(pe.h, p["tens"][C - 1][2])
                v.tensor_tensor(scp[dtid][:, dcol:dcol + 128],
                                scp[dtid][:, dcol:dcol + 128],
                                mask[:], ALU.add).then_inc(dv.h, 1)
                for c in range(C):
                    tid = p["tens"][c][0]
                    if c < C - 1:
                        v.wait_ge(pe.h, p["tens"][c][2])
                    else:
                        v.wait_ge(dv.h, p["dv_mask"])
                    ncols = min(1024, L - c * 1024)
                    v.reduce_max(mx[c][:], scp[tid][:, 0:ncols], AX,
                                 negate=True).then_inc(dv.h, 1)
                if C == 2:
                    v.wait_ge(dv.h, p["dv_nmax"] - 1)
                    v.tensor_tensor(nmax[:], mx[0][:], mx[1][:], ALU.min) \
                        .then_inc(dv.h, 1)
                else:
                    nm_src = mx[0]
                v.wait_ge(ac.h, p["ac_exps"])
                if C == 2:
                    v.tensor_tensor(rsum[:], rs[0][:], rs[1][:], ALU.add) \
                        .then_inc(dv.h, 1)
                    v.wait_ge(dv.h, p["dv_recip"] - 1)
                    v.reciprocal(rinv[:], rsum[:]).then_inc(dv.h, 1)
                else:
                    v.reciprocal(rinv[:], rs[0][:]).then_inc(dv.h, 1)
                v.wait_ge(dv.h, p["dv_recip"])
                v.tensor_scalar_mul(probs[i % 2][:, 0:L], probs[i % 2][:, 0:L],
                                    rinv[:]).then_inc(dv.h, 1)

        @b.scalar
        def _(sc):
            for i, p in enumerate(plan):
                if i > 0:
                    q = plan[i - 1]
                    hh, qq = q["h"], q["qt"]
                    bb = (hh % 2) * 64
                    sc.wait_ge(pe.h, q["pv"][2])
                    sc.activation(
                        attn_fm[bb:bb + 64,
                                (hh // 2) * T + qq * 128:(hh // 2) * T + (qq + 1) * 128],
                        scp[q["pv"][0]][bb:bb + 64, 0:128],
                        AF.Copy).then_inc(ac.h, 1)
                L, C = p["L"], p["C"]
                for c in range(C):
                    tid = p["tens"][c][0]
                    ncols = min(1024, L - c * 1024)
                    sc.wait_ge(dv.h, p["dv_nmax"])
                    if i > 0:
                        sc.wait_ge(dv.h, plan[i - 1]["dv_recip"])
                        sc.wait_ge(tr.h, 16 * max(0, i - 1))
                    sc.activation(probs[i % 2][:, c * 1024:c * 1024 + ncols],
                                  scp[tid][:, 0:ncols], AF.Exp,
                                  bias=nmax[:] if C == 2 else mx[0][:],
                                  accum_out=rs[c][:]).then_inc(ac.h, 1)
            q = plan[-1]
            hh, qq = q["h"], q["qt"]
            bb = (hh % 2) * 64
            sc.wait_ge(pe.h, q["pv"][2])
            sc.activation(
                attn_fm[bb:bb + 64,
                        (hh // 2) * T + qq * 128:(hh // 2) * T + (qq + 1) * 128],
                scp[q["pv"][0]][bb:bb + 64, 0:128],
                AF.Copy).then_inc(ac.h, 1)

        @b.sync
        def _(s):
            ld.inc(s.dma_start(mask[:], io["c_causal"].ap()), 16)
            for i, p in enumerate(plan):
                s.wait_ge(dv.h, p["dv_norm"])
                if i >= 2:
                    s.wait_ge(pe.h, plan[i - 2]["pv"][2])
                tr.inc(s.dma_start(
                    probsT[i % 2][:, 0:p["L"]].rearrange("p (c f) -> p c f", f=128),
                    probs[i % 2][:, 0:p["L"]].rearrange("p (c f) -> p c f", f=128),
                    transpose=True), 16)
    pe.n = pe_n[0]
    dv.n = dv_n[0]
    ac.n = ac_n[0]
    return {"sb_p3_probs0": probs[0], "sb_p3_probs1": probs[1],
            "sb_p3_pt0": probsT[0], "sb_p3_pt1": probsT[1],
            "sb_p3_rinv": rinv, "sb_p3_nmax": nmax, "sb_p3_mx0": mx[0],
            "sb_p3_rs0": rs[0], "sb_p3_mask": mask}



def p4_l0_proj(cx, io, attn_fm):
    nc = cx.nc
    wl = cx.sem("p4wl")
    mm = cx.sem("p4mm")
    ev = cx.sem("p4ev")
    st = cx.sem("p4st")
    wo = cx.sbuf("p4wo", [128, KCD * D], BF16)
    sb = [cx.sbuf(f"p4sb{i}", [128, 512], F32) for i in range(4)]
    ps = [cx.psum(f"p4ps{i}", [128, 512]) for i in range(8)]

    with nc.Block() as b:
        @b.gpsimd
        def _(g):
            wl.inc(g.dma_start(
                wo[:].rearrange("p (c n) -> p c n", c=KCD),
                io["L0_attn_Wo"].ap().rearrange("(c p) n -> p c n", p=128)), 16)

        @b.tensor
        def _(t):
            t.wait_ge(wl.h, 16)
            for it in range(NT * 4):
                i, n = divmod(it, 4)
                bank = it % 8
                if it >= 8:
                    t.wait_ge(ev.h, it - 8 + 1)
                for kc in range(KCD):
                    ins = t.matmul(ps[bank][:],
                                   attn_fm[:, kc * T + i * 128: kc * T + (i + 1) * 128],
                                   wo[:, kc * D + n * 512: kc * D + (n + 1) * 512],
                                   start=(kc == 0), stop=(kc == KCD - 1))
                mm.inc(ins)

        @b.scalar
        def _(sc):
            for it in range(NT * 4):
                sc.wait_ge(mm.h, it + 1)
                if it >= 4:
                    sc.wait_ge(st.h, 16 * (it - 4 + 1))
                ev.inc(sc.activation(sb[it % 4][:], ps[it % 8][:], AF.Copy))

        @b.sync
        def _(s):
            for it in range(NT * 4):
                i, n = divmod(it, 4)
                s.wait_ge(ev.h, it + 1)
                st.inc(s.dma_start(
                    io["sc_s"].ap()[ts(i, 128), ts(n, 512)], sb[it % 4][:]), 16)
            s.wait_ge(st.h, st.n)



def p5_l0_ffn(cx, io):
    nc = cx.nc
    ld = cx.sem("p5ld")
    gl = cx.sem("p5gl")
    mm = cx.sem("p5mm")
    dv = cx.sem("p5dv")
    ac = cx.sem("p5ac")
    st = cx.sem("p5st")

    NQ = 4
    QT = 512
    IC = INTER // 128

    xn_q = cx.sbuf("p5xn", [128, KCD * QT], BF16)
    gwb = [cx.sbuf(f"p5gw{i}", [128, KCD * 128], BF16) for i in range(2)]
    uwb = [cx.sbuf(f"p5uw{i}", [128, KCD * 128], BF16) for i in range(2)]
    act_q = cx.sbuf("p5act", [128, IC * QT], BF16)
    dw_n = cx.sbuf("p5dw", [128, IC * QT], BF16)
    gsc = cx.sbuf("p5gsc", [128, IC], F32)
    usc = cx.sbuf("p5usc", [128, IC], F32)
    g_f = cx.sbuf("p5gf", [128, 512], F32)
    gs_ = cx.sbuf("p5gs", [128, 512], F32)
    gbuf = cx.sbuf("p5gbuf", [128, 4 * D], F32)
    xn1f = cx.sbuf("p5xn1f", [128, D], F32)
    xts = [cx.sbuf("p5xt0", [128, D], F32)] * 2
    sts_ = [cx.sbuf("p5st0", [128, D], F32)] * 2
    x1b = cx.sbuf("p5x1", [128, D], F32)
    xn1b = cx.sbuf("p5xn1b", [128, D], BF16)
    xlob = cx.sbuf("p5xlob", [128, D], BF16)
    sq5 = xn1f
    ssum = cx.sbuf("p5ssum", [128, 1], F32)
    ssq = cx.sbuf("p5ssq", [128, 1], F32)
    nmean = cx.sbuf("p5nmean", [128, 1], F32)
    var = cx.sbuf("p5var", [128, 1], F32)
    rstd = cx.sbuf("p5rstd", [128, 1], F32)
    ps = [cx.psum(f"p5ps{i}", [128, 512]) for i in range(8)]

    cnt = {"ld": 2, "gl": 0, "mm": 0, "dv": 0, "ac": 0, "st": 0}
    bank_hist = []
    P = []

    def use_bank():
        i = len(bank_hist)
        w = bank_hist[i - 8] if i >= 8 else None
        bank_hist.append(None)
        return i, w

    for q in range(NQ):
        plan = {"q": q, "oc": [], "bn": [], "tiles": []}
        cnt["ld"] += 4
        plan["ld_xn"] = cnt["ld"]
        for oc in range(IC):
            e = {}
            cnt["ld"] += 2
            e["ld_w"] = cnt["ld"]
            e["gu"], e["gu_w"] = use_bank()
            e["uu"], e["uu_w"] = use_bank()
            cnt["mm"] += 1
            e["mm_g"] = cnt["mm"]
            cnt["mm"] += 1
            e["mm_u"] = cnt["mm"]
            cnt["dv"] += 1
            e["dv_g"] = cnt["dv"]
            bank_hist[e["gu"]] = ("dv", cnt["dv"])
            cnt["ac"] += 1
            e["ac_silu"] = cnt["ac"]
            cnt["dv"] += 1
            e["dv_stt"] = cnt["dv"]
            bank_hist[e["uu"]] = ("dv", cnt["dv"])
            plan["oc"].append(e)
        for n in range(4):
            e = {}
            cnt["ld"] += 4
            e["ld_dw"] = cnt["ld"]
            e["tiles"] = []
            for t in range(4):
                f = {}
                f["bu"], f["bu_w"] = use_bank()
                cnt["mm"] += 1
                f["mm"] = cnt["mm"]
                cnt["ac"] += 1
                f["ac"] = cnt["ac"]
                bank_hist[f["bu"]] = ("ac", cnt["ac"])
                e["tiles"].append(f)
            plan["bn"].append(e)
        for t in range(4):
            f = {}
            cnt["gl"] += 2
            f["gl"] = cnt["gl"]
            base = cnt["dv"]
            f["dv0"] = base
            cnt["dv"] += 9
            f["dv_x1"] = base + 2
            f["dv_red"] = base + 3
            f["dv_mean"] = base + 4
            f["dv_vs"] = base + 5
            f["dv_ve"] = base + 6
            f["dv_rec"] = base + 7
            f["dv_done"] = base + 9
            cnt["ac"] += 2
            f["ac_sq"] = cnt["ac"] - 1
            f["ac_rt"] = cnt["ac"]
            cnt["st"] += 3
            f["st"] = cnt["st"]
            plan["tiles"].append(f)
        P.append(plan)

    with nc.Block() as b:
        @b.sync
        def _(s):
            ld.inc(s.dma_start(gsc[:], io["c_gate_s_t"].ap()), 16)
            ld.inc(s.dma_start(usc[:], io["c_up_s_t"].ap()), 16)
            for q in range(NQ):
                plan = P[q]
                if q > 0:
                    s.wait_ge(mm.h, P[q - 1]["oc"][-1]["mm_u"])
                for bb in range(4):
                    ld.inc(s.dma_start(
                        bass.AP(xn_q, bb * 128, [[KCD * QT, 128], [QT, KCD], [1, 128]]),
                        io["sc_xn"].ap()[ts(4 * q + bb, 128), :],
                        transpose=True), 16)
                for oc in range(IC):
                    e = plan["oc"][oc]
                    if oc >= 2:
                        s.wait_ge(mm.h, plan["oc"][oc - 2]["mm_u"])
                    elif q > 0 and oc < 2:
                        s.wait_ge(mm.h, P[q - 1]["oc"][oc - 2]["mm_u"])
                    ld.inc(s.dma_start(
                        bass.AP(gwb[oc % 2], 0, [[KCD * 128, 128], [128, KCD], [1, 128]]),
                        io["sc_gate_w"].ap()[ts(oc, 128), :], transpose=True), 16)
                    ld.inc(s.dma_start(
                        bass.AP(uwb[oc % 2], 0, [[KCD * 128, 128], [128, KCD], [1, 128]]),
                        io["sc_up_w"].ap()[ts(oc, 128), :], transpose=True), 16)
                for n in range(4):
                    e = plan["bn"][n]
                    if n > 0:
                        s.wait_ge(mm.h, plan["bn"][n - 1]["tiles"][-1]["mm"])
                    elif q > 0:
                        s.wait_ge(mm.h, P[q - 1]["bn"][-1]["tiles"][-1]["mm"])
                    for hb in range(4):
                        ld.inc(s.dma_start(
                            bass.AP(dw_n, hb * 128, [[IC * QT, 128], [QT, IC], [1, 128]]),
                            io["sc_down_w"].ap()[n * 512 + hb * 128:
                                                 n * 512 + (hb + 1) * 128, :],
                            transpose=True), 16)

        @b.gpsimd
        def _(g):
            for q in range(NQ):
                plan = P[q]
                for t in range(4):
                    f = plan["tiles"][t]
                    it = q * 4 + t
                    if it >= 1:
                        g.wait_ge(dv.h, P[(it - 1) // 4]["tiles"][(it - 1) % 4]["dv_x1"])
                    g.dma_start(xts[it % 2][:],
                                io["x"].ap()[ts(it, 128), :]).then_inc(gl.h, 16)
                    g.dma_start(sts_[it % 2][:],
                                io["sc_s"].ap()[ts(it, 128), :]).then_inc(gl.h, 16)
                    g.wait_ge(dv.h, f["dv_x1"])
                    g.dma_start(io["sc_x1"].ap()[ts(it, 128), :], x1b[:]).then_inc(st.h, 16)
                    g.wait_ge(dv.h, f["dv_done"])
                    g.dma_start(io["sc_xn1"].ap()[ts(it, 128), :], xn1b[:]).then_inc(st.h, 16)
                    g.dma_start(io["sc_xlo"].ap()[ts(it, 128), :], xlob[:]).then_inc(st.h, 16)
            g.wait_ge(st.h, 16 * cnt["st"])
        st.n = 16 * cnt["st"]
        gl.n = 16 * cnt["gl"]

        @b.tensor
        def _(t):
            for q in range(NQ):
                plan = P[q]
                t.wait_ge(ld.h, 16 * plan["ld_xn"])
                for oc in range(IC):
                    e = plan["oc"][oc]
                    t.wait_ge(ld.h, 16 * e["ld_w"])
                    for which, bid, wt, mmord in (
                            ("g", e["gu"], e["gu_w"], e["mm_g"]),
                            ("u", e["uu"], e["uu_w"], e["mm_u"])):
                        if wt is not None:
                            t.wait_ge(dv.h if wt[0] == "dv" else ac.h, wt[1])
                        wsrc = gwb[oc % 2] if which == "g" else uwb[oc % 2]
                        for kc in range(KCD):
                            ins = t.matmul(
                                ps[bid % 8][:],
                                wsrc[:, ts(kc, 128)],
                                xn_q[:, ts(kc, QT)],
                                start=(kc == 0), stop=(kc == KCD - 1))
                        ins.then_inc(mm.h, 1)
                t.wait_ge(dv.h, plan["oc"][-1]["dv_stt"])
                for n in range(4):
                    e = plan["bn"][n]
                    t.wait_ge(ld.h, 16 * e["ld_dw"])
                    for ti in range(4):
                        f = e["tiles"][ti]
                        if f["bu_w"] is not None:
                            wt = f["bu_w"]
                            t.wait_ge(dv.h if wt[0] == "dv" else ac.h, wt[1])
                        for ic in range(IC):
                            ins = t.matmul(
                                ps[f["bu"] % 8][:],
                                act_q[:, ic * QT + ti * 128: ic * QT + (ti + 1) * 128],
                                dw_n[:, ts(ic, QT)],
                                start=(ic == 0), stop=(ic == IC - 1))
                        ins.then_inc(mm.h, 1)
        mm.n = cnt["mm"]

        @b.vector
        def _(v):
            for q in range(NQ):
                plan = P[q]
                for oc in range(IC):
                    e = plan["oc"][oc]
                    v.wait_ge(mm.h, e["mm_g"])
                    if oc > 0:
                        v.wait_ge(ac.h, plan["oc"][oc - 1]["ac_silu"])
                    elif q > 0:
                        v.wait_ge(ac.h, P[q - 1]["oc"][-1]["ac_silu"])
                    v.tensor_scalar_mul(g_f[:], ps[e["gu"] % 8][:],
                                        gsc[:, oc:oc + 1]).then_inc(dv.h, 1)
                    v.wait_ge(mm.h, e["mm_u"])
                    v.wait_ge(ac.h, e["ac_silu"])
                    if oc == 0 and q > 0:
                        v.wait_ge(mm.h, P[q - 1]["bn"][-1]["tiles"][-1]["mm"])
                    v.scalar_tensor_tensor(
                        act_q[:, ts(oc, QT)], ps[e["uu"] % 8][:],
                        usc[:, oc:oc + 1], gs_[:], ALU.mult, ALU.mult) \
                        .then_inc(dv.h, 1)
                for t in range(4):
                    f = plan["tiles"][t]
                    it = q * 4 + t
                    v.wait_ge(ac.h, plan["bn"][3]["tiles"][t]["ac"])
                    v.wait_ge(gl.h, 16 * f["gl"])
                    if it >= 1:
                        v.wait_ge(st.h, 16 * P[(it - 1) // 4]["tiles"][(it - 1) % 4]["st"])
                    gb = gbuf[:, ts(t, D)]
                    v.tensor_tensor(x1b[:], sts_[it % 2][:], gb, ALU.mult) \
                        .then_inc(dv.h, 1)
                    v.tensor_tensor(x1b[:], x1b[:], xts[it % 2][:], ALU.add) \
                        .then_inc(dv.h, 1)
                    v.reduce_sum(ssum[:], x1b[:], AX).then_inc(dv.h, 1)
                    v.wait_ge(dv.h, f["dv_red"])
                    v.wait_ge(ac.h, f["ac_sq"])
                    v.tensor_scalar_mul(nmean[:], ssum[:], -1.0 / D).then_inc(dv.h, 1)
                    v.wait_ge(dv.h, f["dv_mean"])
                    v.tensor_scalar_mul(var[:], ssq[:], 1.0 / D)
                    v.tensor_scalar(ssq[:], nmean[:], nmean[:], None, ALU.mult) \
                        .then_inc(dv.h, 1)
                    v.wait_ge(dv.h, f["dv_vs"])
                    v.tensor_tensor(var[:], var[:], ssq[:], ALU.subtract)
                    v.tensor_scalar(var[:], var[:], 1e-5, None, ALU.add) \
                        .then_inc(dv.h, 1)
                    v.wait_ge(dv.h, f["dv_ve"])
                    v.reciprocal(rstd[:], var[:]).then_inc(dv.h, 1)
                    v.wait_ge(ac.h, f["ac_rt"])
                    v.tensor_scalar(xn1f[:], x1b[:], nmean[:], None, ALU.add)
                    v.tensor_scalar(xn1f[:], xn1f[:], rstd[:], None, ALU.mult) \
                        .then_inc(dv.h, 1)
                    v.tensor_copy(xn1b[:], xn1f[:])
                    v.tensor_tensor(xlob[:], xn1f[:], xn1b[:], ALU.subtract) \
                        .then_inc(dv.h, 1)
        dv.n = cnt["dv"]

        @b.scalar
        def _(sc):
            for q in range(NQ):
                plan = P[q]
                for oc in range(IC):
                    e = plan["oc"][oc]
                    sc.wait_ge(dv.h, e["dv_g"])
                    if oc > 0:
                        sc.wait_ge(dv.h, plan["oc"][oc - 1]["dv_stt"])
                    elif q > 0:
                        sc.wait_ge(dv.h, P[q - 1]["oc"][-1]["dv_stt"])
                    sc.activation(gs_[:], g_f[:], AF.Silu).then_inc(ac.h, 1)
                for n in range(4):
                    e = plan["bn"][n]
                    for ti in range(4):
                        f = e["tiles"][ti]
                        sc.wait_ge(mm.h, f["mm"])
                        if q > 0:
                            sc.wait_ge(dv.h, P[q - 1]["tiles"][ti]["dv_x1"])
                        sc.activation(gbuf[:, ti * D + n * 512: ti * D + (n + 1) * 512],
                                      ps[f["bu"] % 8][:], AF.Silu).then_inc(ac.h, 1)
                for t in range(4):
                    f = plan["tiles"][t]
                    sc.wait_ge(dv.h, f["dv_x1"])
                    sc.activation(sq5[:], x1b[:], AF.Square, accum_out=ssq[:]) \
                        .then_inc(ac.h, 1)
                    sc.wait_ge(dv.h, f["dv_rec"])
                    sc.activation(rstd[:], rstd[:], AF.Sqrt).then_inc(ac.h, 1)
        ac.n = cnt["ac"]



def p6_l1_attn(cx, io):
    nc = cx.nc
    ld = cx.sem("p6ld")
    gl = cx.sem("p6gl")
    mm = cx.sem("p6mm")
    dv = cx.sem("p6dv")
    ac = cx.sem("p6ac")
    st = cx.sem("p6st")
    qs = 1.0 / math.sqrt(D)

    bqt = cx.sbuf("p6bq", [128, NL1 * KCD], F32)
    bkt = cx.sbuf("p6bk", [128, NL1 * KCD], F32)
    bot = cx.sbuf("p6bo", [128, NL1 * KCD], F32)
    ones = cx.sbuf("p6ones", [1, 128], BF16)
    bvr = cx.sbuf("p6bv", [1, NL1 * D], BF16)
    with nc.Block() as b:
        @b.sync
        def _(s):
            ld.inc(s.dma_start(bqt[:], io["c_bq_t"].ap()), 16)
            ld.inc(s.dma_start(bkt[:], io["c_bk_t"].ap()), 16)
            ld.inc(s.dma_start(bot[:], io["c_bo_t"].ap()), 16)
            ld.inc(s.dma_start(ones[:], io["c_ones"].ap()), 16)
            ld.inc(s.dma_start(bvr[:], io["c_bv_row"].ap()), 16)
            s.wait_ge(ld.h, ld.n)
        @b.vector
        def _(v):
            v.wait_ge(ld.h, ld.n)

    for l in range(NL1):
        src_fm = io["sc_xn1"] if l == 0 else io["sc_xa%d" % ((l - 1) % 2)]
        src_is_tm = (l == 0)
        dst = io["sc_s"] if l == NL1 - 1 else io["sc_xa%d" % (l % 2)]

        with cx.phase():
            xa = cx.sbuf(f"p6xa{l}", [128, KCD * T], BF16)
            wcol = [cx.sbuf(f"p6wc{l}{i}", [128, KCD * 128], BF16) for i in range(2)]
            wv = cx.sbuf(f"p6wv{l}", [128, KCD * 512], BF16)
            stg = [cx.sbuf(f"p6stg{l}{i}", [128, 512], BF16) for i in range(4)]
            ps = [cx.psum(f"p6ps{l}{i}", [128, 512]) for i in range(8)]

            ld0, gl0, mm0, ac0, st0 = ld.n, gl.n, mm.n, ac.n, st.n
            with nc.Block() as b:
                @b.sync
                def _(s):
                    s.wait_ge(st.h, st0)
                    if src_is_tm:
                        for bb in range(NT):
                            ld.inc(s.dma_start(
                                bass.AP(xa, bb * 128,
                                        [[KCD * T, 128], [T, KCD], [1, 128]]),
                                src_fm.ap()[ts(bb, 128), :], transpose=True), 16)
                    else:
                        for kc in range(KCD):
                            ld.inc(s.dma_start(
                                xa[:].rearrange("p (c t) -> p c t", c=KCD)[:, kc:kc + 1, :],
                                src_fm.ap().rearrange("(c p) t -> p c t", p=128)[:, kc:kc + 1, :]), 16)
                    for j in range(192):
                        s.wait_ge(ac.h, ac0 + j + 1)
                        if j < 64:
                            m, n = divmod(j, 4)
                            dst_ap = io["sc_qf"].ap()[ts(m, 128), ts(n, 512)]
                        elif j < 128:
                            m, n = divmod(j - 64, 4)
                            dst_ap = io["sc_kf"].ap()[ts(m, 128), ts(n, 512)]
                        else:
                            n, t_ = divmod(j - 128, 16)
                            dst_ap = io["sc_v"].ap()[ts(t_, 128), ts(n, 512)]
                        st.inc(s.dma_start(dst_ap, stg[j % 4][:]), 16)

                @b.gpsimd
                def _(g):
                    for proj in range(2):
                        w = io["L1_attn_Wq"] if proj == 0 else io["L1_attn_Wk"]
                        for m in range(KCD):
                            j = proj * 16 + m
                            if j >= 2:
                                g.wait_ge(mm.h, mm0 + (j - 1) * 4)
                            gl.inc(g.dma_start(
                                wcol[m % 2][:].rearrange("p (c f) -> p c f", c=KCD),
                                w.ap()[l].rearrange("(c p) f -> p c f", p=128)
                                 [:, :, ts(m, 128)]), 16)
                    for n in range(4):
                        g.wait_ge(mm.h, mm0 + 128 + n * 16)
                        gl.inc(g.dma_start(
                            wv[:].rearrange("p (c f) -> p c f", c=KCD),
                            io["L1_attn_Wv"].ap()[l].rearrange("(c p) f -> p c f", p=128)
                             [:, :, ts(n, 512)]), 16)

                @b.tensor
                def _(t):
                    t.wait_ge(ld.h, ld0 + 16 * KCD)
                    u = 0
                    for proj in range(2):
                        for m in range(KCD):
                            t.wait_ge(gl.h, gl0 + 16 * (proj * 16 + m + 1))
                            for n in range(4):
                                if u >= 8:
                                    t.wait_ge(ac.h, ac0 + u - 8 + 1)
                                for kc in range(KCD):
                                    ins = t.matmul(
                                        ps[u % 8][:],
                                        wcol[m % 2][:, ts(kc, 128)],
                                        xa[:, kc * T + n * 512: kc * T + (n + 1) * 512],
                                        start=(kc == 0), stop=(kc == KCD - 1))
                                mm.inc(ins)
                                u += 1
                    for n in range(4):
                        t.wait_ge(gl.h, gl0 + 16 * (32 + n + 1))
                        for t_ in range(NT):
                            if u >= 8:
                                t.wait_ge(ac.h, ac0 + u - 8 + 1)
                            for kc in range(KCD):
                                t.matmul(
                                    ps[u % 8][:],
                                    xa[:, kc * T + t_ * 128: kc * T + (t_ + 1) * 128],
                                    wv[:, ts(kc, 512)],
                                    start=(kc == 0), stop=False)
                            ins = t.matmul(ps[u % 8][:], ones[:],
                                           bvr[0:1, l * D + n * 512: l * D + (n + 1) * 512],
                                           start=False, stop=True)
                            mm.inc(ins)
                            u += 1

                @b.vector
                def _(v):
                    for j in range(192):
                        v.wait_ge(mm.h, mm0 + j + 1)
                        if j >= 4:
                            v.wait_ge(st.h, st0 + 16 * (j - 4 + 1))
                        if j < 64:
                            m = j // 4
                            v.tensor_scalar(stg[j % 4][:], ps[j % 8][:],
                                            bqt[:, l * KCD + m: l * KCD + m + 1],
                                            None, ALU.add).then_inc(ac.h, 1)
                        elif j < 128:
                            m = (j - 64) // 4
                            v.tensor_scalar(stg[j % 4][:], ps[j % 8][:],
                                            bkt[:, l * KCD + m: l * KCD + m + 1],
                                            None, ALU.add).then_inc(ac.h, 1)
                        else:
                            v.tensor_copy(stg[j % 4][:], ps[j % 8][:]) \
                                .then_inc(ac.h, 1)
                    ac.n += 192

        with cx.phase():
            k_sb = cx.sbuf(f"p6k{l}", [128, KCD * T], BF16)
            v_sb = cx.sbuf(f"p6v{l}", [128, KCD * D], BF16)
            qt_b = [cx.sbuf(f"p6qt{l}{i}", [128, KCD * 128], BF16) for i in range(2)]
            probs = [cx.sbuf(f"p6pr{l}{i}", [128, T], BF16) for i in range(2)]
            probsT = [cx.sbuf(f"p6pt{l}{i}", [128, T], BF16) for i in range(2)]
            astg = [cx.sbuf(f"p6as{l}{i}", [128, 512], BF16) for i in range(4)]
            nmax = [cx.sbuf(f"p6nm{l}{i}", [128, 1], F32) for i in range(2)]
            rs = [cx.sbuf(f"p6rs{l}{i}", [128, 1], F32) for i in range(2)]
            rinv = [cx.sbuf(f"p6ri{l}{i}", [128, 1], F32) for i in range(2)]
            psA = cx.psum(f"p6A{l}", [128, T])
            psB = cx.psum(f"p6B{l}", [128, T])

            ld0, gl0, mm0, dv0, ac0, st0, tr0 = ld.n, gl.n, mm.n, dv.n, ac.n, st.n, 0
            trs = cx.sem(f"p6tr{l}")
            with nc.Block() as b:
                @b.sync
                def _(s):
                    s.wait_ge(st.h, st0)
                    for kc in range(KCD):
                        ld.inc(s.dma_start(
                            k_sb[:].rearrange("p (c t) -> p c t", c=KCD)[:, kc:kc + 1, :],
                            io["sc_kf"].ap().rearrange("(c p) t -> p c t", p=128)
                             [:, kc:kc + 1, :]), 16)
                    for kc in range(KCD):
                        ld.inc(s.dma_start(
                            v_sb[:].rearrange("p (c t) -> p c t", c=KCD)[:, kc:kc + 1, :],
                            io["sc_v"].ap().rearrange("(c p) t -> p c t", p=128)
                             [:, kc:kc + 1, :]), 16)
                    for qt in range(NT):
                        if qt >= 2:
                            s.wait_ge(mm.h, mm0 + 5 * (qt - 2) + 1)
                        ld.inc(s.dma_start(
                            qt_b[qt % 2][:].rearrange("p (c f) -> p c f", c=KCD),
                            io["sc_qf"].ap().rearrange("(c p) t -> p c t", p=128)
                             [:, :, ts(qt, 128)]), 16)
                        s.wait_ge(ac.h, ac0 + qt + 1)
                        if qt >= 2:
                            s.wait_ge(mm.h, mm0 + 5 * (qt - 2) + 5)
                        trs.inc(s.dma_start(
                            probsT[qt % 2][:].rearrange("p (c f) -> p c f", f=128),
                            probs[qt % 2][:].rearrange("p (c f) -> p c f", f=128),
                            transpose=True), 16)

                @b.tensor
                def _(t):
                    t.wait_ge(ld.h, ld0 + 16 * 2 * KCD)
                    for qt in range(NT):
                        t.wait_ge(ld.h, ld0 + 16 * (2 * KCD + qt + 1))
                        if qt >= 1:
                            t.wait_ge(ac.h, ac0 + qt)
                        for n in range(4):
                            for kc in range(KCD):
                                ins = t.matmul(
                                    psA[:, ts(n, 512)],
                                    qt_b[qt % 2][:, ts(kc, 128)],
                                    k_sb[:, kc * T + n * 512: kc * T + (n + 1) * 512],
                                    start=(kc == 0), stop=(kc == KCD - 1))
                        ins.then_inc(mm.h, 1)
                        mm.n += 1
                        t.wait_ge(trs.h, 16 * (qt + 1))
                        if qt >= 1:
                            t.wait_ge(dv.h, dv0 + 7 * qt)
                        for n in range(4):
                            for kc in range(KCD):
                                ins = t.matmul(
                                    psB[:, ts(n, 512)],
                                    probsT[qt % 2][:, ts(kc, 128)],
                                    v_sb[:, kc * D + n * 512: kc * D + (n + 1) * 512],
                                    start=(kc == 0), stop=(kc == KCD - 1))
                            mm.inc(ins)

                @b.vector
                def _(v):
                    for qt in range(NT):
                        v.wait_ge(mm.h, mm0 + 5 * qt + 1)
                        v.reduce_max(nmax[qt % 2][:], psA[:], AX, negate=True) \
                            .then_inc(dv.h, 1)
                        v.wait_ge(dv.h, dv0 + 7 * qt + 1)
                        v.tensor_scalar_mul(nmax[qt % 2][:], nmax[qt % 2][:], qs) \
                            .then_inc(dv.h, 1)
                        v.wait_ge(ac.h, ac0 + qt + 1)
                        v.reciprocal(rinv[qt % 2][:], rs[qt % 2][:]).then_inc(dv.h, 1)
                        dv.n += 3
                        v.wait_ge(mm.h, mm0 + 5 * qt + 5)
                        v.wait_ge(dv.h, dv0 + 7 * qt + 3)
                        for n in range(4):
                            if 4 * qt + n >= 4:
                                v.wait_ge(st.h, st0 + 16 * (4 * qt + n - 4 + 1))
                            v.tensor_scalar_mul(astg[n % 4][:], psB[:, ts(n, 512)],
                                                rinv[qt % 2][:]).then_inc(dv.h, 1)
                            dv.n += 1

                @b.scalar
                def _(sc):
                    for qt in range(NT):
                        sc.wait_ge(dv.h, dv0 + 7 * qt + 2)
                        if qt >= 2:
                            sc.wait_ge(dv.h, dv0 + 7 * (qt - 1))
                            sc.wait_ge(trs.h, 16 * (qt - 1))
                        sc.activation(probs[qt % 2][:], psA[:], AF.Exp,
                                      bias=nmax[qt % 2][:], scale=qs,
                                      accum_out=rs[qt % 2][:]).then_inc(ac.h, 1)
                        ac.n += 1

                @b.gpsimd
                def _(g):
                    for qt in range(NT):
                        for n in range(4):
                            g.wait_ge(dv.h, dv0 + 7 * qt + 3 + n + 1)
                            st.inc(g.dma_start(
                                io["sc_attn"].ap()[ts(qt, 128), ts(n, 512)],
                                astg[n % 4][:]), 16)
                    g.wait_ge(st.h, st.n)

        last = (l == NL1 - 1)
        with cx.phase():
            at_sb = cx.sbuf(f"p6at{l}", [128, KCD * T], BF16)
            if last:
                wcol = [cx.sbuf(f"p6oc{l}{i}", [128, KCD * 512], BF16)
                        for i in range(2)]
                bor = cx.sbuf(f"p6bor{l}", [1, D], BF16)
            else:
                wcol = [cx.sbuf(f"p6oc{l}{i}", [128, KCD * 128], BF16)
                        for i in range(2)]
            ostg = [cx.sbuf(f"p6os{l}{i}", [128, 512],
                            F32 if last else BF16) for i in range(4)]
            ps = [cx.psum(f"p6cps{l}{i}", [128, 512]) for i in range(8)]
            ld0, gl0, mm0, ac0, st0 = ld.n, gl.n, mm.n, ac.n, st.n
            with nc.Block() as b:
                @b.sync
                def _(s):
                    s.wait_ge(st.h, st0)
                    if last:
                        ld.inc(s.dma_start(bor[:], io["c_bo_row"].ap()), 16)
                    for bb in range(NT):
                        ld.inc(s.dma_start(
                            bass.AP(at_sb, bb * 128,
                                    [[KCD * T, 128], [T, KCD], [1, 128]]),
                            io["sc_attn"].ap()[ts(bb, 128), :], transpose=True), 16)
                    for j in range(64):
                        if last:
                            n, m = divmod(j, 16)
                        else:
                            m, n = divmod(j, 4)
                        dap = dst.ap()[ts(m, 128), ts(n, 512)]
                        s.wait_ge(ac.h, ac0 + j + 1)
                        st.inc(s.dma_start(dap, ostg[j % 4][:]), 16)
                    s.wait_ge(st.h, st.n)

                @b.gpsimd
                def _(g):
                    if last:
                        for n in range(4):
                            if n >= 2:
                                g.wait_ge(mm.h, mm0 + (n - 1) * 16)
                            gl.inc(g.dma_start(
                                wcol[n % 2][:].rearrange("p (c f) -> p c f", c=KCD),
                                io["L1_attn_Wo"].ap()[l]
                                 .rearrange("(c p) f -> p c f", p=128)
                                 [:, :, ts(n, 512)]), 16)
                    else:
                        for m in range(KCD):
                            if m >= 2:
                                g.wait_ge(mm.h, mm0 + (m - 1) * 4)
                            gl.inc(g.dma_start(
                                wcol[m % 2][:].rearrange("p (c f) -> p c f", c=KCD),
                                io["L1_attn_Wo"].ap()[l]
                                 .rearrange("(c p) f -> p c f", p=128)
                                 [:, :, ts(m, 128)]), 16)

                @b.tensor
                def _(t):
                    t.wait_ge(ld.h, ld0 + 16 * (NT + (1 if last else 0)))
                    u = 0
                    if last:
                        for n in range(4):
                            t.wait_ge(gl.h, gl0 + 16 * (n + 1))
                            for t_ in range(NT):
                                if u >= 8:
                                    t.wait_ge(ac.h, ac0 + u - 8 + 1)
                                for kc in range(KCD):
                                    t.matmul(
                                        ps[u % 8][:],
                                        at_sb[:, kc * T + t_ * 128: kc * T + (t_ + 1) * 128],
                                        wcol[n % 2][:, ts(kc, 512)],
                                        start=(kc == 0), stop=False)
                                ins = t.matmul(ps[u % 8][:], ones[:],
                                               bor[0:1, ts(n, 512)],
                                               start=False, stop=True)
                                mm.inc(ins)
                                u += 1
                    else:
                        for m in range(KCD):
                            t.wait_ge(gl.h, gl0 + 16 * (m + 1))
                            for n in range(4):
                                if u >= 8:
                                    t.wait_ge(ac.h, ac0 + u - 8 + 1)
                                for kc in range(KCD):
                                    ins = t.matmul(
                                        ps[u % 8][:],
                                        wcol[m % 2][:, ts(kc, 128)],
                                        at_sb[:, kc * T + n * 512: kc * T + (n + 1) * 512],
                                        start=(kc == 0), stop=(kc == KCD - 1))
                                mm.inc(ins)
                                u += 1

                @b.vector
                def _(v):
                    for j in range(64):
                        v.wait_ge(mm.h, mm0 + j + 1)
                        if j >= 4:
                            v.wait_ge(st.h, st0 + 16 * (j - 4 + 1))
                        if last:
                            v.tensor_copy(ostg[j % 4][:], ps[j % 8][:]) \
                                .then_inc(ac.h, 1)
                        else:
                            m = j // 4
                            v.tensor_scalar(ostg[j % 4][:], ps[j % 8][:],
                                            bot[:, l * KCD + m: l * KCD + m + 1],
                                            None, ALU.add).then_inc(ac.h, 1)
                    ac.n += 64



def p7_gate(cx, io, combine_sb):
    nc = cx.nc
    ld = cx.sem("p7ld")
    dv = cx.sem("p7dv")
    ac = cx.sem("p7ac")
    mm = cx.sem("p7mm")
    gwf = cx.sbuf("p7gwf", [128, KCD * NEXP], F32)
    gwh = cx.sbuf("p7gwh", [128, KCD * NEXP], BF16)
    gwl = cx.sbuf("p7gwl", [128, KCD * NEXP], BF16)
    xh = [cx.sbuf(f"p7xh{i}", [128, KCD * 128], BF16) for i in range(2)]
    xl = [cx.sbuf(f"p7xl{i}", [128, KCD * 128], BF16) for i in range(2)]
    pr = cx.sbuf("p7pr", [128, NEXP], F32)
    pn = cx.sbuf("p7pn", [128, NEXP], F32)
    msk1 = cx.sbuf("p7m1k", [128, NEXP], F32)
    tmp8 = cx.sbuf("p7t8", [128, NEXP], F32)
    nm = cx.sbuf("p7nm", [128, 1], F32)
    rsum = cx.sbuf("p7rs", [128, 1], F32)
    rinv = cx.sbuf("p7ri", [128, 1], F32)
    m1 = cx.sbuf("p7m1", [128, 1], F32)
    m2 = cx.sbuf("p7m2", [128, 1], F32)
    den = cx.sbuf("p7dn", [128, 1], F32)
    w1_ = cx.sbuf("p7w1", [128, 1], F32)
    w2_ = cx.sbuf("p7w2", [128, 1], F32)
    psg = cx.psum("p7ps", [128, NEXP])

    with nc.Block() as b:
        @b.sync
        def _(s):
            ld.inc(s.dma_start(
                gwf[:].rearrange("p (c e) -> p c e", c=KCD),
                io["c_gate_wT"].ap().rearrange("(c p) e -> p c e", p=128)), 16)
            for i in range(NT):
                if i >= 2:
                    s.wait_ge(mm.h, i - 1)
                ld.inc(s.dma_start(
                    bass.AP(xh[i % 2], 0, [[KCD * 128, 128], [128, KCD], [1, 128]]),
                    io["sc_xn1"].ap()[ts(i, 128), :], transpose=True), 16)
                ld.inc(s.dma_start(
                    bass.AP(xl[i % 2], 0, [[KCD * 128, 128], [128, KCD], [1, 128]]),
                    io["sc_xlo"].ap()[ts(i, 128), :], transpose=True), 16)

        @b.vector
        def _(v):
            v.wait_ge(ld.h, 16)
            dv.inc(v.tensor_copy(gwh[:], gwf[:]))
            v.wait_ge(dv.h, dv.n)
            dv.inc(v.tensor_tensor(gwl[:], gwf[:], gwh[:], ALU.subtract))
            for i in range(NT):
                v.wait_ge(mm.h, i + 1)
                dv.inc(v.reduce_max(nm[:], psg[:], AX, negate=True))
                v.wait_ge(dv.h, dv.n)
                v.wait_ge(ac.h, i + 1)
                dv.inc(v.reciprocal(rinv[:], rsum[:]))
                v.wait_ge(dv.h, dv.n)
                dv.inc(v.tensor_scalar_mul(pn[:], pr[:], rinv[:]))
                v.wait_ge(dv.h, dv.n)
                dv.inc(v.reduce_max(m1[:], pn[:], AX))
                v.wait_ge(dv.h, dv.n)
                dv.inc(v.tensor_scalar(msk1[:], pn[:], m1[:], None, ALU.is_equal))
                v.wait_ge(dv.h, dv.n)
                dv.inc(v.scalar_tensor_tensor(tmp8[:], msk1[:], -1e30, pn[:],
                                              ALU.mult, ALU.add))
                v.wait_ge(dv.h, dv.n)
                dv.inc(v.reduce_max(m2[:], tmp8[:], AX))
                v.wait_ge(dv.h, dv.n)
                dv.inc(v.tensor_tensor(den[:], m1[:], m2[:], ALU.add))
                v.wait_ge(dv.h, dv.n)
                dv.inc(v.reciprocal(w1_[:], den[:]))
                v.wait_ge(dv.h, dv.n)
                dv.inc(v.tensor_scalar(w2_[:], m2[:], w1_[:], None, ALU.mult))
                dv.inc(v.tensor_scalar(w1_[:], m1[:], w1_[:], None, ALU.mult))
                v.wait_ge(dv.h, dv.n)
                dv.inc(v.tensor_scalar(msk1[:], msk1[:], w1_[:], None, ALU.mult))
                dv.inc(v.tensor_scalar(tmp8[:], pn[:], m2[:], None, ALU.is_equal))
                v.wait_ge(dv.h, dv.n)
                dv.inc(v.scalar_tensor_tensor(
                    combine_sb[:, ts(i, NEXP)], tmp8[:], w2_[:], msk1[:],
                    ALU.mult, ALU.add))
                v.wait_ge(dv.h, dv.n)

        @b.tensor
        def _(t):
            t.wait_ge(dv.h, 2)
            for i in range(NT):
                t.wait_ge(ld.h, 16 * (2 * i + 3))
                if i >= 1:
                    t.wait_ge(ac.h, i)
                for kc in range(KCD):
                    last = (kc == KCD - 1)
                    t.matmul(psg[:], xh[i % 2][:, ts(kc, 128)],
                             gwh[:, ts(kc, NEXP)], start=(kc == 0), stop=False)
                    t.matmul(psg[:], xl[i % 2][:, ts(kc, 128)],
                             gwh[:, ts(kc, NEXP)], start=False, stop=False)
                    ins = t.matmul(psg[:], xh[i % 2][:, ts(kc, 128)],
                                   gwl[:, ts(kc, NEXP)], start=False, stop=last)
                mm.inc(ins)

        @b.scalar
        def _(sc):
            for i in range(NT):
                sc.wait_ge(dv.h, 2 + 14 * i + 1)
                sc.activation(pr[:], psg[:], AF.Exp, bias=nm[:],
                              accum_out=rsum[:]).then_inc(ac.h, 1)
        ac.n = NT



def p8_moe(cx, io, combine_sb):
    nc = cx.nc
    ld = cx.sem("p8ld")
    gl = cx.sem("p8gl")
    mm = cx.sem("p8mm")
    dv = cx.sem("p8dv")
    ac = cx.sem("p8ac")
    st = cx.sem("p8st")
    HT = 1024
    NTH = HT // 128

    xn_h = cx.sbuf("p8xn", [128, KCD * HT], BF16)
    acc = cx.sbuf("p8acc", [128, NTH * D], F32)
    act_h = cx.sbuf("p8act", [128, MIC * HT], BF16)
    w2t = cx.sbuf("p8w2", [128, MIC * D], BF16)
    w1t = [cx.sbuf(f"p8w1{i}", [128, KCD * 128], BF16) for i in range(4)]
    gs8 = cx.sbuf("p8gs", [128, HT], F32)
    xt8 = cx.sbuf("p8xt", [128, D], F32)
    st8 = cx.sbuf("p8st_", [128, D], F32)
    ps = [cx.psum(f"p8ps{i}", [128, 512]) for i in range(8)]

    cnt = {"ld": 0, "mm": 0, "dv": 0, "ac": 0, "st": 0, "gl": 0}
    bank_hist = []
    P = []

    def use_bank():
        i = len(bank_hist)
        w = bank_hist[i - 8] if i >= 8 else None
        bank_hist.append(None)
        return i, w

    for h in range(2):
        plan = {"pairs": [], "m2": []}
        cnt["ld"] += NTH
        plan["ld_xn"] = cnt["ld"]
        for e in range(NEXP):
            ep = {"pairs": [], "m2": []}
            cnt["ld"] += KCD
            ep["ld_w2"] = cnt["ld"]
            for j in range(MIC):
                pr = {}
                cnt["ld"] += 2
                pr["ld_w1"] = cnt["ld"]
                pr["gu"] = [use_bank() for _ in range(2)]
                cnt["mm"] += 1
                pr["mm_g"] = cnt["mm"]
                pr["uu"] = [use_bank() for _ in range(2)]
                cnt["mm"] += 1
                pr["mm_u"] = cnt["mm"]
                cnt["ac"] += 2
                pr["ac_s"] = cnt["ac"]
                for z in range(2):
                    bank_hist[pr["gu"][z][0]] = ("ac", cnt["ac"] - 1 + z)
                cnt["dv"] += 2
                pr["dv_a"] = cnt["dv"]
                for z in range(2):
                    bank_hist[pr["uu"][z][0]] = ("dv", cnt["dv"] - 1 + z)
                ep["pairs"].append(pr)
            for n in range(4):
                for t2 in range(NTH):
                    f = {}
                    f["bu"], f["bu_w"] = use_bank()
                    cnt["mm"] += 1
                    f["mm"] = cnt["mm"]
                    cnt["dv"] += 1
                    f["dv"] = cnt["dv"]
                    bank_hist[f["bu"]] = ("dv", cnt["dv"])
                    ep["m2"].append(f)
            plan.setdefault("exp", []).append(ep)
        for t2 in range(NTH):
            f = {}
            cnt["gl"] += 2
            f["gl"] = cnt["gl"]
            cnt["ac"] += 1
            f["ac"] = cnt["ac"]
            cnt["dv"] += 2
            f["dv"] = cnt["dv"]
            cnt["st"] += 1
            f["st"] = cnt["st"]
            plan["tiles"] = plan.get("tiles", []) + [f]
        P.append(plan)

    with nc.Block() as b:
        @b.sync
        def _(s):
            for h in range(2):
                plan = P[h]
                if h > 0:
                    s.wait_ge(mm.h, P[0]["exp"][-1]["pairs"][-1]["mm_u"])
                for bb in range(NTH):
                    ld.inc(s.dma_start(
                        bass.AP(xn_h, bb * 128, [[KCD * HT, 128], [HT, KCD], [1, 128]]),
                        io["sc_xn1"].ap()[ts(h * NTH + bb, 128), :],
                        transpose=True), 16)
                for e in range(NEXP):
                    ep = plan["exp"][e]
                    if e > 0:
                        s.wait_ge(mm.h, plan["exp"][e - 1]["m2"][-1]["mm"])
                    elif h > 0:
                        s.wait_ge(mm.h, P[0]["exp"][-1]["m2"][-1]["mm"])
                    for hc in range(KCD):
                        ld.inc(s.dma_start(
                            bass.AP(w2t, hc * 128, [[MIC * D, 128], [D, MIC], [1, 128]]),
                            io["sc_w2"].ap()[e][ts(hc, 128), :], transpose=True), 16)
                    for j in range(MIC):
                        pr = ep["pairs"][j]
                        if j >= 2:
                            s.wait_ge(mm.h, ep["pairs"][j - 2]["mm_u"])
                        for which, oc in ((0, j), (1, j + MIC)):
                            ld.inc(s.dma_start(
                                bass.AP(w1t[(j % 2) * 2 + which], 0,
                                        [[KCD * 128, 128], [128, KCD], [1, 128]]),
                                io["sc_w1"].ap()[e][ts(oc, 128), :],
                                transpose=True), 16)

        @b.gpsimd
        def _(g):
            for h in range(2):
                plan = P[h]
                for t2 in range(NTH):
                    f = plan["tiles"][t2]
                    it = h * NTH + t2
                    if it >= 1:
                        prev = P[(it - 1) // NTH]["tiles"][(it - 1) % NTH]
                        g.wait_ge(dv.h, prev["dv"])
                        g.wait_ge(st.h, 16 * prev["st"])
                    g.dma_start(xt8[:], io["sc_x1"].ap()[ts(it, 128), :]) \
                        .then_inc(gl.h, 16)
                    g.dma_start(st8[:], io["sc_s"].ap()[ts(it, 128), :]) \
                        .then_inc(gl.h, 16)
                    g.wait_ge(dv.h, f["dv"])
                    g.dma_start(io["out"].ap()[ts(it, 128), :], st8[:]) \
                        .then_inc(st.h, 16)
            g.wait_ge(st.h, 16 * cnt["st"])
        st.n = 16 * cnt["st"]
        gl.n = 16 * cnt["gl"]

        @b.tensor
        def _(t):
            for h in range(2):
                plan = P[h]
                t.wait_ge(ld.h, 16 * plan["ld_xn"])
                for e in range(NEXP):
                    ep = plan["exp"][e]
                    for j in range(MIC):
                        pr = ep["pairs"][j]
                        t.wait_ge(ld.h, 16 * pr["ld_w1"])
                        for which, banks, mmord in ((0, pr["gu"], pr["mm_g"]),
                                                    (1, pr["uu"], pr["mm_u"])):
                            for z in range(2):
                                bid, wt = banks[z]
                                if wt is not None:
                                    t.wait_ge(dv.h if wt[0] == "dv" else ac.h, wt[1])
                                for kc in range(KCD):
                                    ins = t.matmul(
                                        ps[bid % 8][:],
                                        w1t[(j % 2) * 2 + which][:, ts(kc, 128)],
                                        xn_h[:, kc * HT + z * 512: kc * HT + (z + 1) * 512],
                                        start=(kc == 0), stop=(kc == KCD - 1))
                            mm.inc(ins)
                    t.wait_ge(dv.h, ep["pairs"][-1]["dv_a"])
                    t.wait_ge(ld.h, 16 * ep["ld_w2"])
                    for n in range(4):
                        for t2 in range(NTH):
                            f = ep["m2"][n * NTH + t2]
                            if f["bu_w"] is not None:
                                wt = f["bu_w"]
                                t.wait_ge(dv.h if wt[0] == "dv" else ac.h, wt[1])
                            for ic in range(MIC):
                                ins = t.matmul(
                                    ps[f["bu"] % 8][:],
                                    act_h[:, ic * HT + t2 * 128: ic * HT + (t2 + 1) * 128],
                                    w2t[:, ic * D + n * 512: ic * D + (n + 1) * 512],
                                    start=(ic == 0), stop=(ic == MIC - 1))
                            mm.inc(ins)

        @b.scalar
        def _(sc):
            for h in range(2):
                plan = P[h]
                for e in range(NEXP):
                    ep = plan["exp"][e]
                    for j in range(MIC):
                        pr = ep["pairs"][j]
                        sc.wait_ge(mm.h, pr["mm_g"])
                        if j >= 1:
                            sc.wait_ge(dv.h, ep["pairs"][j - 1]["dv_a"])
                        elif e >= 1:
                            sc.wait_ge(dv.h, plan["exp"][e - 1]["pairs"][-1]["dv_a"])
                        elif h >= 1:
                            sc.wait_ge(dv.h, P[0]["exp"][-1]["pairs"][-1]["dv_a"])
                        for z in range(2):
                            sc.activation(gs8[:, ts(z, 512)],
                                          ps[pr["gu"][z][0] % 8][:], AF.Silu) \
                                .then_inc(ac.h, 1)
                for t2 in range(NTH):
                    f = plan["tiles"][t2]
                    sc.wait_ge(dv.h, plan["exp"][-1]["m2"][3 * NTH + t2]["dv"])
                    sc.activation(acc[:, ts(t2, D)], acc[:, ts(t2, D)], AF.Silu) \
                        .then_inc(ac.h, 1)

        @b.vector
        def _(v):
            for h in range(2):
                plan = P[h]
                if h > 0:
                    v.wait_ge(ac.h, P[0]["tiles"][-1]["ac"])
                v.memset(acc[:], 0.0)
                for e in range(NEXP):
                    ep = plan["exp"][e]
                    for j in range(MIC):
                        pr = ep["pairs"][j]
                        v.wait_ge(mm.h, pr["mm_u"])
                        v.wait_ge(ac.h, pr["ac_s"])
                        if j == 0 and (e > 0 or h > 0):
                            prev = plan["exp"][e - 1] if e > 0 else P[0]["exp"][-1]
                            v.wait_ge(mm.h, prev["m2"][-1]["mm"])
                        for z in range(2):
                            v.tensor_tensor(
                                act_h[:, j * HT + z * 512: j * HT + (z + 1) * 512],
                                ps[pr["uu"][z][0] % 8][:],
                                gs8[:, ts(z, 512)], ALU.mult).then_inc(dv.h, 1)
                    for n in range(4):
                        for t2 in range(NTH):
                            f = ep["m2"][n * NTH + t2]
                            v.wait_ge(mm.h, f["mm"])
                            if h > 0 and e == 0 and n == 0:
                                pass
                            v.scalar_tensor_tensor(
                                acc[:, t2 * D + n * 512: t2 * D + (n + 1) * 512],
                                ps[f["bu"] % 8][:],
                                combine_sb[:, (h * NTH + t2) * NEXP + e:
                                           (h * NTH + t2) * NEXP + e + 1],
                                acc[:, t2 * D + n * 512: t2 * D + (n + 1) * 512],
                                ALU.mult, ALU.add).then_inc(dv.h, 1)
                for t2 in range(NTH):
                    f = plan["tiles"][t2]
                    v.wait_ge(ac.h, f["ac"])
                    v.wait_ge(gl.h, 16 * f["gl"])
                    if h * NTH + t2 >= 1:
                        it = h * NTH + t2
                        v.wait_ge(st.h, 16 * P[(it - 1) // NTH]["tiles"][(it - 1) % NTH]["st"])
                    v.tensor_tensor(st8[:], st8[:], acc[:, ts(t2, D)], ALU.mult) \
                        .then_inc(dv.h, 1)
                    v.tensor_tensor(st8[:], st8[:], xt8[:], ALU.add).then_inc(dv.h, 1)
        dv.n = cnt["dv"]
        ac.n = cnt["ac"]



def declare_io(nc, dump):
    io = {}

    def inp(name, shape, dtype):
        io[name] = nc.dram_tensor(name, shape, dtype, kind="ExternalInput")

    def scratch(name, shape, dtype):
        kind = "ExternalOutput" if name in dump else "Internal"
        io[name] = nc.dram_tensor(name, shape, dtype, kind=kind)

    inp("x", [T, D], F32)
    inp("L0_attn_Wq", [D, D], F32)
    inp("L0_attn_Wk", [D, 512], F32)
    inp("L0_attn_Wv", [D, 512], F32)
    inp("L0_attn_Wo", [D, D], F32)
    inp("L0_ffn_gate_w", [INTER, D], F32)
    inp("L0_ffn_up_w", [INTER, D], F32)
    inp("L0_ffn_down_w", [D, INTER], F32)
    inp("c_gate_s_t", [128, 44], F32)
    inp("c_up_s_t", [128, 44], F32)
    inp("c_down_s_t", [128, 16], F32)
    inp("L1_attn_Wq", [NL1, D, D], BF16)
    inp("L1_attn_Wk", [NL1, D, D], BF16)
    inp("L1_attn_Wv", [NL1, D, D], BF16)
    inp("L1_attn_Wo", [NL1, D, D], BF16)
    inp("c_bq_t", [128, NL1 * KCD], F32)
    inp("c_bk_t", [128, NL1 * KCD], F32)
    inp("c_bo_t", [128, NL1 * KCD], F32)
    inp("c_bv_row", [1, NL1 * D], BF16)
    inp("c_bo_row", [1, D], BF16)
    inp("c_gate_wT", [D, NEXP], F32)
    inp("L1_ffn_w1", [NEXP, 2 * MI, D], F32)
    inp("L1_ffn_w2", [NEXP, D, MI], F32)
    inp("c_cos", [T, 64], F32)
    inp("c_sin", [T, 64], F32)
    inp("c_ones", [1, 128], BF16)
    inp("c_causal", [128, 128], F32)

    scratch("sc_xn", [T, D], BF16)
    scratch("sc_x1", [T, D], F32)
    scratch("sc_xn1", [T, D], BF16)
    scratch("sc_xlo", [T, D], BF16)
    scratch("sc_s", [T, D], F32)
    scratch("sc_gate_w", [INTER, D], BF16)
    scratch("sc_up_w", [INTER, D], BF16)
    scratch("sc_down_w", [D, INTER], BF16)
    scratch("sc_w1", [NEXP, 2 * MI, D], BF16)
    scratch("sc_w2", [NEXP, D, MI], BF16)
    scratch("sc_qf", [D, T], BF16)
    scratch("sc_kf", [D, T], BF16)
    scratch("sc_xa0", [D, T], BF16)
    scratch("sc_xa1", [D, T], BF16)
    scratch("sc_v", [T, D], BF16)
    scratch("sc_attn", [T, D], BF16)
    io["out"] = nc.dram_tensor("out", [T, D], F32, kind="ExternalOutput")
    return io


def build_program(dump=(), max_phase=99):
    nc = bass.Bass("TRN2", target_bir_lowering=False)
    dump = set(dump)
    io = declare_io(nc, dump)
    sb_shapes = {"sb_q_fm": [128, KCD * T], "sb_k_fm": [128, 8 * T],
                 "sb_v_tm": [128, NT * 512], "sb_attn_fm": [128, KCD * T]}
    for nm in dump:
        if nm.startswith("sb_") and nm in sb_shapes:
            io[nm] = nc.dram_tensor(nm, sb_shapes[nm], BF16, kind="ExternalOutput")
    with ExitStack() as es:
        cx = Cx(nc, es)
        sb_map = {}

        if max_phase >= 0:
            with cx.phase():
                p0_weight_prep(cx, io)
        if max_phase >= 1:
            with cx.phase():
                p1_ln0(cx, io)
        if max_phase >= 2:
            with cx.phase():
                l0scope = cx.cur
                attn_fm = None
                with cx.phase():
                    q_fm = cx.sbuf("q_fm", [128, KCD * T], BF16)
                    k_fm, v_tm = p2_l0_qkv(cx, io, q_fm)
                    sb_map.update({"sb_q_fm": q_fm, "sb_k_fm": k_fm,
                                   "sb_v_tm": v_tm})
                    if max_phase >= 3:
                        attn_fm = cx.sbuf("attn_fm", [128, KCD * T], BF16,
                                          scope=l0scope, side="right")
                        with cx.phase():
                            p3dbg = p3_l0_attn(cx, io, q_fm, k_fm, v_tm, attn_fm)
                            _emit_dumps(cx, io, dump, p3dbg)
                    _emit_dumps(cx, io, dump, sb_map)
                if max_phase >= 4:
                    with cx.phase():
                        p4_l0_proj(cx, io, attn_fm)
                if attn_fm is not None:
                    _emit_dumps(cx, io, dump, {"sb_attn_fm": attn_fm})
        if max_phase >= 5:
            with cx.phase():
                p5_l0_ffn(cx, io)
        if max_phase >= 6:
            with cx.phase():
                p6_l1_attn(cx, io)
        if max_phase >= 7:
            combine_sb = cx.sbuf("combine", [128, NT * NEXP], F32, persistent=True)
            with cx.phase():
                p7_gate(cx, io, combine_sb)
            if max_phase >= 8:
                with cx.phase():
                    p8_moe(cx, io, combine_sb)
    return nc


def _emit_dumps(cx, io, dump, sb_map):
    todo = sorted(nm for nm in dump if nm in sb_map and sb_map[nm] is not None)
    if not todo:
        return
    nc = cx.nc
    dmp = cx.sem("dmp" + todo[0])
    for nm in todo:
        if nm not in io:
            t = sb_map[nm]
            io[nm] = nc.dram_tensor(nm, list(t.shape), t.dtype,
                                    kind="ExternalOutput")
    with nc.Block() as b:
        @b.sync
        def _(s):
            for nm in todo:
                dmp.inc(s.dma_start(io[nm].ap(), sb_map[nm][:]), 16)
            s.wait_ge(dmp.h, dmp.n)


def _host_constants():
    theta = 1.0 / (10000.0 ** (np.arange(0, HD, 2, dtype=np.float64) / HD))
    pos = np.arange(T, dtype=np.float64)
    fr = np.outer(pos, theta)
    cos1 = np.cos(fr)
    sin1 = np.sin(fr)
    cos = np.concatenate([cos1, cos1], axis=1).astype(np.float32)
    sinf = np.concatenate([-sin1, sin1], axis=1).astype(np.float32)
    ones = np.ones((1, 128), dtype=ml_dtypes.bfloat16)
    causal = np.where(np.arange(128)[:, None] >= np.arange(128)[None, :],
                      np.float32(0), np.float32(-1e30))
    return cos, sinf, ones, causal


def make_inmaps(inputs, dump=()):
    cos, sinf, ones, causal = _host_constants()
    base = {}
    base["L0_attn_Wq"] = np.asarray(inputs["L0_attn_Wq"])
    base["L0_attn_Wk"] = np.asarray(inputs["L0_attn_Wk"])
    base["L0_attn_Wv"] = np.asarray(inputs["L0_attn_Wv"])
    base["L0_attn_Wo"] = np.asarray(inputs["L0_attn_Wo"])
    base["L0_ffn_gate_w"] = np.asarray(inputs["L0_ffn_gate_w"])
    base["L0_ffn_up_w"] = np.asarray(inputs["L0_ffn_up_w"])
    base["L0_ffn_down_w"] = np.asarray(inputs["L0_ffn_down_w"])
    base["c_gate_s_t"] = np.ascontiguousarray(
        np.asarray(inputs["L0_ffn_gate_s"]).reshape(44, 128).T)
    base["c_up_s_t"] = np.ascontiguousarray(
        np.asarray(inputs["L0_ffn_up_s"]).reshape(44, 128).T)
    base["c_down_s_t"] = np.ascontiguousarray(
        np.asarray(inputs["L0_ffn_down_s"]).reshape(16, 128).T)
    for nm in ["L1_attn_Wq", "L1_attn_Wk", "L1_attn_Wv", "L1_attn_Wo"]:
        base[nm] = np.asarray(inputs[nm])
    for nm, key in [("c_bq_t", "L1_attn_bq"), ("c_bk_t", "L1_attn_bk"),
                    ("c_bo_t", "L1_attn_bo")]:
        b = np.asarray(inputs[key]).astype(np.float32)
        base[nm] = np.ascontiguousarray(
            b.reshape(NL1, KCD, 128).transpose(2, 0, 1).reshape(128, NL1 * KCD))
    base["c_bv_row"] = np.ascontiguousarray(
        np.asarray(inputs["L1_attn_bv"]).reshape(1, NL1 * D))
    base["c_bo_row"] = np.ascontiguousarray(
        np.asarray(inputs["L1_attn_bo"])[NL1 - 1].reshape(1, D))
    base["c_gate_wT"] = np.ascontiguousarray(
        np.asarray(inputs["L1_ffn_gate_w"]).T)
    base["L1_ffn_w1"] = np.asarray(inputs["L1_ffn_w1"])
    base["L1_ffn_w2"] = np.asarray(inputs["L1_ffn_w2"])
    base["c_cos"] = cos
    base["c_sin"] = sinf
    base["c_ones"] = ones
    base["c_causal"] = causal
    x = np.asarray(inputs["L0_attn_x"])
    maps = []
    for b in range(8):
        m = dict(base)
        m["x"] = np.ascontiguousarray(x[b])
        maps.append(m)
    return maps


def kernel(**inputs):
    nc = build_program(dump=DUMP, max_phase=MAX_PHASE)
    maps = make_inmaps(inputs, dump=DUMP)
    res = run_bass_kernel_spmd(nc, maps, core_ids=list(range(8)))
    return np.stack([np.asarray(res.results[i]["out"]) for i in range(8)])


# version 02
# speedup vs baseline: 1.0871x; 1.0871x over previous
import math
from contextlib import ExitStack, contextmanager

import numpy as np
import ml_dtypes
import concourse.bass as bass
import concourse.mybir as mybir
from concourse.bass_utils import run_bass_kernel_spmd

F32 = mybir.dt.float32
BF16 = mybir.dt.bfloat16
AF = mybir.ActivationFunctionType
ALU = mybir.AluOpType
AX = mybir.AxisListType.X
ts = bass.ts

T = 2048
D = 2048
NT = T // 128
KCD = D // 128
HEADS, KVH, HD = 32, 8, 64
INTER = 5632
NL1 = 4
NEXP, MI = 8, 1408
MIC = MI // 128

DUMP = ()
MAX_PHASE = 99

Q_PERM = [(2 * c, 2 * c + 1) for c in range(16)]


class Sem:
    __slots__ = ("h", "n")

    def __init__(self, h):
        self.h = h
        self.n = 0

    def inc(self, ins, k=1):
        ins.then_inc(self.h, k)
        self.n += k
        return self.n


class Cx:
    def __init__(self, nc, es):
        self.nc = nc
        self.es = es
        self.cur = es

    @contextmanager
    def phase(self):
        with ExitStack() as ps:
            old = self.cur
            self.cur = ps
            try:
                yield
            finally:
                self.cur = old

    def sem(self, name):
        return Sem(self.es.enter_context(self.nc.semaphore(name)))

    def sbuf(self, name, shape, dtype, persistent=False, scope=None, side=None):
        if scope is None:
            scope = self.es if persistent else self.cur
        return scope.enter_context(self.nc.sbuf_tensor(name, shape, dtype, side=side))

    def psum(self, name, shape, dtype=F32, persistent=False):
        scope = self.es if persistent else self.cur
        return scope.enter_context(self.nc.psum_tensor(name, shape, dtype))



def build_cast_jobs(io):
    jobs = []

    def chunk(src, dst, rows, step):
        for r0 in range(0, rows, step):
            r1 = min(rows, r0 + step)
            jobs.append((dst[r0:r1, :], src[r0:r1, :]))

    chunk(io["L0_ffn_gate_w"].ap(), io["sc_gate_w"].ap(), INTER, 1024)
    chunk(io["L0_ffn_up_w"].ap(), io["sc_up_w"].ap(), INTER, 1024)
    mark_l0 = len(jobs)
    for e in range(NEXP):
        chunk(io["L1_ffn_w1"].ap()[e], io["sc_w1"].ap()[e], 2 * MI, 1024)
        chunk(io["L1_ffn_w2"].ap()[e], io["sc_w2"].ap()[e], D, 1024)
    return jobs, mark_l0




def p1_ln0(cx, io):
    nc = cx.nc
    ld = cx.sem("p1ld")
    dv = cx.sem("p1dv")
    ac = cx.sem("p1ac")
    st = cx.sem("p1st")
    dl = cx.sem("p1dl")
    dst_ = cx.sem("p1dst")
    f0 = cx.sbuf("p1f0", [128, INTER], F32)
    fb = [cx.sbuf(f"p1fb{i}", [128, INTER], BF16) for i in range(2)]
    dsc = cx.sbuf("p1dsc", [128, 16], F32)
    xs = [cx.sbuf(f"p1x{i}", [128, D], F32) for i in range(2)]
    sq = cx.sbuf("p1sq", [128, D], F32)
    ssum = cx.sbuf("p1ssum", [128, 1], F32)
    ssq = cx.sbuf("p1ssq", [128, 1], F32)
    nmean = cx.sbuf("p1nmean", [128, 1], F32)
    var = cx.sbuf("p1var", [128, 1], F32)
    rstd = cx.sbuf("p1rstd", [128, 1], F32)
    tmp = cx.sbuf("p1tmp", [128, D], F32)
    xnbs = [cx.sbuf(f"p1xnb{i}", [128, D], BF16) for i in range(2)]

    with nc.Block() as b:
        @b.sync
        def _(s):
            for i in range(NT):
                if i >= 2:
                    s.wait_ge(dv.h, 7 * (i - 1))
                ld.inc(s.dma_start(xs[i % 2][:], io["x"].ap()[ts(i, 128), :]), 16)

        @b.scalar
        def _(sc):
            for i in range(NT):
                sc.wait_ge(ld.h, 16 * (i + 1))
                sc.activation(sq[:], xs[i % 2][:], AF.Square, accum_out=ssq[:]) \
                    .then_inc(ac.h, 1)
                sc.wait_ge(dv.h, 7 * i + 6)
                sc.activation(rstd[:], rstd[:], AF.Sqrt).then_inc(ac.h, 1)
                sc.wait_ge(dv.h, 7 * (i + 1))
        ac.n = 2 * NT

        @b.vector
        def _(v):
            for i in range(NT):
                x = xs[i % 2]
                v.wait_ge(ld.h, 16 * (i + 1))
                v.reduce_sum(ssum[:], x[:], AX).then_inc(dv.h, 1)
                v.wait_ge(dv.h, 7 * i + 1)
                v.wait_ge(ac.h, 2 * i + 1)
                v.tensor_scalar_mul(nmean[:], ssum[:], -1.0 / D).then_inc(dv.h, 1)
                v.wait_ge(dv.h, 7 * i + 2)
                v.tensor_scalar_mul(var[:], ssq[:], 1.0 / D)
                v.tensor_scalar(ssq[:], nmean[:], nmean[:], None, ALU.mult) \
                    .then_inc(dv.h, 1)
                v.wait_ge(dv.h, 7 * i + 3)
                v.tensor_tensor(var[:], var[:], ssq[:], ALU.subtract).then_inc(dv.h, 1)
                v.wait_ge(dv.h, 7 * i + 4)
                v.tensor_scalar(var[:], var[:], 1e-5, None, ALU.add).then_inc(dv.h, 1)
                v.wait_ge(dv.h, 7 * i + 5)
                v.reciprocal(rstd[:], var[:]).then_inc(dv.h, 1)
                v.wait_ge(ac.h, 2 * i + 2)
                if i >= 2:
                    v.wait_ge(st.h, 16 * (i - 1))
                v.tensor_scalar(tmp[:], x[:], nmean[:], None, ALU.add)
                v.tensor_scalar(xnbs[i % 2][:], tmp[:], rstd[:], None, ALU.mult) \
                    .then_inc(dv.h, 1)
            for i in range(NT):
                v.wait_ge(dl.h, 16 * (i + 2))
                if i >= 2:
                    v.wait_ge(dst_.h, 16 * (i - 1))
                v.tensor_scalar_mul(fb[i % 2][:], f0[:], dsc[:, i:i + 1]) \
                    .then_inc(dv.h, 1)
        dv.n = 7 * NT + NT

        @b.gpsimd
        def _(g):
            g.dma_start(dsc[:], io["c_down_s_t"].ap()).then_inc(dl.h, 16)
            g.dma_start(f0[:], io["L0_ffn_down_w"].ap()[0:128, :]).then_inc(dl.h, 16)
            for i in range(NT):
                g.wait_ge(dv.h, 7 * (i + 1))
                st.inc(g.dma_start(io["sc_xn"].ap()[ts(i, 128), :], xnbs[i % 2][:]), 16)
            for i in range(NT):
                g.wait_ge(dv.h, 7 * NT + i + 1)
                if i + 1 < NT:
                    g.dma_start(f0[:], io["L0_ffn_down_w"].ap()[ts(i + 1, 128), :]) \
                        .then_inc(dl.h, 16)
                g.dma_start(io["sc_down_w"].ap()[ts(i, 128), :], fb[i % 2][:]) \
                    .then_inc(dst_.h, 16)
            g.wait_ge(st.h, st.n)
            g.wait_ge(dst_.h, 16 * NT)



def p2_l0_qkv(cx, io, q_fm):
    nc = cx.nc

    def load_xn_fm(s, ld, xn_fm):
        for bb in range(NT):
            ld.inc(s.dma_start(
                bass.AP(xn_fm, bb * 128, [[KCD * T, 128], [T, KCD], [1, 128]]),
                io["sc_xn"].ap()[ts(bb, 128), :],
                transpose=True), 16)

    wq = io["L0_attn_Wq"].ap()

    def q_cols_ap(nb):
        return (wq[:, nb * 512:(nb + 1) * 512]
                .rearrange("(c p) n -> p c n", p=128))

    with cx.phase():
        ld = cx.sem("p2ald")
        wl = cx.sem("p2awl")
        mm = cx.sem("p2amm")
        ev = cx.sem("p2aev")
        rp = cx.sem("p2arp")
        tr = cx.sem("p2atr")
        xn_fm = cx.sbuf("p2xnfm", [128, KCD * T], BF16)
        cosb = cx.sbuf("p2cos", [128, NT * 64], F32)
        sinb = cx.sbuf("p2sin", [128, NT * 64], F32)
        wbuf = cx.sbuf("p2wq", [128, KCD * 512], BF16)
        qf = [cx.sbuf(f"p2qf{i}", [128, 512], F32) for i in range(2)]
        qsw = cx.sbuf("p2qsw", [128, 512], F32)
        qb = [cx.sbuf(f"p2qb{i}", [128, 512], BF16) for i in range(2)]
        ps = [cx.psum(f"p2ps{i}", [128, 512]) for i in range(8)]

        with nc.Block() as b:
            @b.sync
            def _(s):
                load_xn_fm(s, ld, xn_fm)
                ld.inc(s.dma_start(
                    cosb[:].rearrange("p (i n) -> p i n", i=NT),
                    io["c_cos"].ap().rearrange("(i p) n -> p i n", p=128)), 16)
                ld.inc(s.dma_start(
                    sinb[:].rearrange("p (i n) -> p i n", i=NT),
                    io["c_sin"].ap().rearrange("(i p) n -> p i n", p=128)), 16)
                for nb in range(4):
                    for i in range(NT):
                        it = nb * NT + i
                        s.wait_ge(rp.h, it + 1)
                        dst = bass.AP(q_fm, 4 * nb * T + i * 128,
                                      [[KCD * T, 128], [T, 4], [1, 128]])
                        tr.inc(s.dma_start(dst, qb[it % 2][:], transpose=True), 16)

            @b.gpsimd
            def _(g):
                for nb in range(4):
                    if nb >= 1:
                        g.wait_ge(mm.h, 16 * nb)
                    wl.inc(g.dma_start(
                        wbuf[:].rearrange("p (c n) -> p c n", c=KCD),
                        q_cols_ap(nb)), 16)
                g.wait_ge(tr.h, 16 * 4 * NT)

            @b.tensor
            def _(t):
                t.wait_ge(ld.h, 16 * (KCD + 2))
                for nb in range(4):
                    t.wait_ge(wl.h, 16 * (nb + 1))
                    for i in range(NT):
                        it = nb * NT + i
                        bank = it % 8
                        if it >= 8:
                            t.wait_ge(ev.h, it - 8 + 1)
                        for kc in range(KCD):
                            ins = t.matmul(
                                ps[bank][:],
                                xn_fm[:, kc * T + i * 128: kc * T + (i + 1) * 128],
                                wbuf[:, ts(kc, 512)],
                                start=(kc == 0), stop=(kc == KCD - 1))
                        mm.inc(ins)

            @b.scalar
            def _(sc):
                for it in range(4 * NT):
                    sc.wait_ge(mm.h, it + 1)
                    sc.wait_ge(rp.h, max(0, it - 1))
                    ev.inc(sc.activation(qf[it % 2][:], ps[it % 8][:], AF.Copy,
                                         scale=0.125))

            @b.vector
            def _(v):
                for nb in range(4):
                    for i in range(NT):
                        it = nb * NT + i
                        v.wait_ge(ev.h, it + 1)
                        if it >= 2:
                            v.wait_ge(tr.h, 16 * (it - 1))
                        x = qf[it % 2]
                        co = bass.AP(cosb, i * 64, [[NT * 64, 128], [0, 8], [1, 64]])
                        si = bass.AP(sinb, i * 64, [[NT * 64, 128], [0, 8], [1, 64]])
                        sw = qsw[:].rearrange("p (h two j) -> p h two j", two=2, j=32)
                        xv = x[:].rearrange("p (h two j) -> p h two j", two=2, j=32)
                        v.tensor_copy(sw[:, :, 0, :], xv[:, :, 1, :])
                        v.tensor_copy(sw[:, :, 1, :], xv[:, :, 0, :])
                        x3 = x[:].rearrange("p (h j) -> p h j", j=64)
                        v.tensor_tensor(x3, x3, co, ALU.mult)
                        v.tensor_tensor(qsw[:].rearrange("p (h j) -> p h j", j=64),
                                        qsw[:].rearrange("p (h j) -> p h j", j=64),
                                        si, ALU.mult)
                        rp.inc(v.tensor_tensor(qb[it % 2][:], x[:], qsw[:], ALU.add))

    k_fm = cx.sbuf("k_fm", [128, 8 * T], BF16)
    with cx.phase():
        ld = cx.sem("p2bld")
        wl = cx.sem("p2bwl")
        mm = cx.sem("p2bmm")
        ev = cx.sem("p2bev")
        rp = cx.sem("p2brp")
        tr = cx.sem("p2btr")
        xn_fm = cx.sbuf("p2bxnfm", [128, KCD * T], BF16)
        cosb = cx.sbuf("p2bcos", [128, NT * 64], F32)
        sinb = cx.sbuf("p2bsin", [128, NT * 64], F32)
        wbuf = cx.sbuf("p2bw", [128, KCD * 512], BF16)
        qf = [cx.sbuf(f"p2bqf{i}", [128, 512], F32) for i in range(2)]
        qsw = cx.sbuf("p2bqsw", [128, 512], F32)
        qb = [cx.sbuf(f"p2bqb{i}", [128, 512], BF16) for i in range(2)]
        k2 = [cx.sbuf(f"p2bk2{i}", [128, 1024], BF16) for i in range(2)]
        ps = [cx.psum(f"p2bps{i}", [128, 512]) for i in range(8)]

        with nc.Block() as b:
            @b.sync
            def _(s):
                load_xn_fm(s, ld, xn_fm)
                ld.inc(s.dma_start(
                    cosb[:].rearrange("p (i n) -> p i n", i=NT),
                    io["c_cos"].ap().rearrange("(i p) n -> p i n", p=128)), 16)
                ld.inc(s.dma_start(
                    sinb[:].rearrange("p (i n) -> p i n", i=NT),
                    io["c_sin"].ap().rearrange("(i p) n -> p i n", p=128)), 16)
                for i in range(NT):
                    s.wait_ge(rp.h, 3 * (i + 1))
                    dst = bass.AP(k_fm, i * 128, [[8 * T, 128], [T, 8], [1, 128]])
                    tr.inc(s.dma_start(dst, k2[i % 2][:], transpose=True), 16)

            @b.gpsimd
            def _(g):
                wl.inc(g.dma_start(
                    wbuf[:].rearrange("p (c n) -> p c n", c=KCD),
                    io["L0_attn_Wk"].ap().rearrange("(c p) n -> p c n", p=128)), 16)
                g.wait_ge(tr.h, 16 * NT)

            @b.tensor
            def _(t):
                t.wait_ge(ld.h, 16 * (KCD + 2))
                t.wait_ge(wl.h, 16)
                for i in range(NT):
                    bank = i % 8
                    if i >= 8:
                        t.wait_ge(ev.h, i - 8 + 1)
                    for kc in range(KCD):
                        ins = t.matmul(
                            ps[bank][:],
                            xn_fm[:, kc * T + i * 128: kc * T + (i + 1) * 128],
                            wbuf[:, ts(kc, 512)],
                            start=(kc == 0), stop=(kc == KCD - 1))
                    mm.inc(ins)

            @b.scalar
            def _(sc):
                for i in range(NT):
                    sc.wait_ge(mm.h, i + 1)
                    sc.wait_ge(rp.h, max(0, 3 * (i - 1)))
                    ev.inc(sc.activation(qf[i % 2][:], ps[i % 8][:], AF.Copy))

            @b.vector
            def _(v):
                for i in range(NT):
                    v.wait_ge(ev.h, i + 1)
                    if i >= 2:
                        v.wait_ge(tr.h, 16 * (i - 1))
                    x = qf[i % 2]
                    co = bass.AP(cosb, i * 64, [[NT * 64, 128], [0, 8], [1, 64]])
                    si = bass.AP(sinb, i * 64, [[NT * 64, 128], [0, 8], [1, 64]])
                    sw = qsw[:].rearrange("p (h two j) -> p h two j", two=2, j=32)
                    xv = x[:].rearrange("p (h two j) -> p h two j", two=2, j=32)
                    v.tensor_copy(sw[:, :, 0, :], xv[:, :, 1, :])
                    v.tensor_copy(sw[:, :, 1, :], xv[:, :, 0, :])
                    x3 = x[:].rearrange("p (h j) -> p h j", j=64)
                    v.tensor_tensor(x3, x3, co, ALU.mult)
                    v.tensor_tensor(qsw[:].rearrange("p (h j) -> p h j", j=64),
                                    qsw[:].rearrange("p (h j) -> p h j", j=64),
                                    si, ALU.mult)
                    rp.inc(v.tensor_tensor(qb[i % 2][:], x[:], qsw[:], ALU.add))
                    kk = k2[i % 2][:].rearrange("p (g d x) -> p g d x", d=2, x=64)
                    kv = qb[i % 2][:].rearrange("p (g x) -> p g x", x=64)
                    rp.inc(v.tensor_copy(kk[:, :, 0, :], kv))
                    rp.inc(v.tensor_copy(kk[:, :, 1, :], kv))

    v_tm = cx.sbuf("v_tm", [128, NT * 512], BF16)
    with cx.phase():
        ld = cx.sem("p2cld")
        wl = cx.sem("p2cwl")
        mm = cx.sem("p2cmm")
        ev = cx.sem("p2cev")
        xn_fm = cx.sbuf("p2cxnfm", [128, KCD * T], BF16)
        wbuf = cx.sbuf("p2cw", [128, KCD * 512], BF16)
        ps = [cx.psum(f"p2cps{i}", [128, 512]) for i in range(8)]

        with nc.Block() as b:
            @b.sync
            def _(s):
                load_xn_fm(s, ld, xn_fm)

            @b.gpsimd
            def _(g):
                wl.inc(g.dma_start(
                    wbuf[:].rearrange("p (c n) -> p c n", c=KCD),
                    io["L0_attn_Wv"].ap().rearrange("(c p) n -> p c n", p=128)), 16)

            @b.tensor
            def _(t):
                t.wait_ge(ld.h, 16 * KCD)
                t.wait_ge(wl.h, 16)
                for i in range(NT):
                    bank = i % 8
                    if i >= 8:
                        t.wait_ge(ev.h, i - 8 + 1)
                    for kc in range(KCD):
                        ins = t.matmul(
                            ps[bank][:],
                            xn_fm[:, kc * T + i * 128: kc * T + (i + 1) * 128],
                            wbuf[:, ts(kc, 512)],
                            start=(kc == 0), stop=(kc == KCD - 1))
                    mm.inc(ins)

            @b.scalar
            def _(sc):
                for i in range(NT):
                    sc.wait_ge(mm.h, i + 1)
                    ev.inc(sc.activation(v_tm[:, ts(i, 512)], ps[i % 8][:], AF.Copy))
    return k_fm, v_tm



def p3_l0_attn(cx, io, q_fm, k_fm, v_tm, attn_fm, bg=None):
    nc = cx.nc
    pe = cx.sem("p3pe")
    dv = cx.sem("p3dv")
    ac = cx.sem("p3ac")
    ld = cx.sem("p3ld")
    trs = cx.sem("p3tr")
    mask = cx.sbuf("p3mask", [128, 128], F32)
    probs = [cx.sbuf(f"p3pr{i}", [128, 2048], BF16) for i in range(2)]
    probsT = [cx.sbuf(f"p3pt{i}", [128, 2048], BF16) for i in range(2)]
    rs = [cx.sbuf(f"p3rs{i}", [128, 1], F32) for i in range(2)]
    rinv = [cx.sbuf(f"p3ri{i}", [128, 1], F32) for i in range(2)]
    nmax = [cx.sbuf(f"p3nm{i}", [128, 1], F32) for i in range(2)]
    scp = cx.psum("p3scp", [128, 4096])

    iters = [(h, qt) for h in range(HEADS) for qt in range(NT)]
    NIT = len(iters)

    rel = [0] * 8
    pos = [0]
    pe_n = [0]
    dv_n = [0]
    ac_n = [0]

    def alloc(C):
        if pos[0] + C > 8:
            pos[0] = 0
        r = pos[0]
        pos[0] += C
        return r, max(rel[r:r + C])

    P = []
    for i, (h, qt) in enumerate(iters):
        L = (qt + 1) * 128
        C = (L + 511) // 512
        p = {"h": h, "qt": qt, "L": L, "C": C}
        p["rb"], p["rb_w"] = alloc(C)
        p["pb"], p["pb_w"] = alloc(1)
        pe_n[0] += 1
        p["pe_s"] = pe_n[0]
        pe_n[0] += 1
        p["pe_v"] = pe_n[0]
        dv_n[0] += 2
        p["dv_max"] = dv_n[0]
        dv_n[0] += 2
        p["dv_norm"] = dv_n[0]
        ac_n[0] += 1
        p["ac_exp"] = ac_n[0]
        ac_n[0] += 1
        p["ac_ev"] = ac_n[0]
        for b in range(p["rb"], p["rb"] + C):
            rel[b] = p["ac_exp"]
        rel[p["pb"]] = p["ac_ev"]
        P.append(p)
    c = 0
    for t in range(NIT + 2):
        if t - 2 >= 0:
            c += 1
            P[t - 2]["pe_v"] = c
        if t < NIT:
            c += 1
            P[t]["pe_s"] = c
    c = 0
    for t in range(NIT + 3):
        if 0 <= t - 3 < NIT:
            c += 1
            P[t - 3]["ac_ev"] = c
        if 0 <= t - 1 < NIT:
            c += 1
            P[t - 1]["ac_exp"] = c
    rel = [0] * 8
    pos = [0]
    for i, p in enumerate(P):
        rb, w = alloc(p["C"])
        assert rb == p["rb"]
        p["rb_w"] = w
        pb, w = alloc(1)
        assert pb == p["pb"]
        p["pb_w"] = w
        for b in range(rb, rb + p["C"]):
            rel[b] = p["ac_exp"]
        rel[pb] = p["ac_ev"]
    c = 0
    for t in range(NIT + 1):
        if t < NIT:
            c += 2
            P[t]["dv_max"] = c
        if t - 1 >= 0:
            c += 2
            P[t - 1]["dv_norm"] = c

    def sc_of(p):
        return scp[:, p["rb"] * 512: p["rb"] * 512 + p["L"]]

    def pv_of(p, base):
        return scp[base:base + 64, p["pb"] * 512: p["pb"] * 512 + 128]

    def attn_slice(p, base):
        h, qt = p["h"], p["qt"]
        return attn_fm[base:base + 64,
                       (h // 2) * T + qt * 128:(h // 2) * T + (qt + 1) * 128]

    with nc.Block() as b:
        @b.tensor
        def _(t):
            t.wait_ge(ld.h, 16)
            for tk in range(NIT + 2):
                if tk - 2 >= 0:
                    p = P[tk - 2]
                    h, qt, L = p["h"], p["qt"], p["L"]
                    base = (h % 2) * 64
                    g = h // 4
                    t.wait_ge(ac.h, p["pb_w"])
                    t.wait_ge(trs.h, 16 * (tk - 1))
                    ins = None
                    for kc in range(L // 128):
                        ins = t.matmul(
                            pv_of(p, base),
                            v_tm[:, kc * 512 + g * 64: kc * 512 + (g + 1) * 64],
                            probsT[(tk - 2) % 2][:, kc * 128:(kc + 1) * 128],
                            start=(kc == 0), stop=(kc == L // 128 - 1),
                            tile_position=(0, base))
                    ins.then_inc(pe.h, 1)
                if tk < NIT:
                    p = P[tk]
                    h, qt, L = p["h"], p["qt"], p["L"]
                    base = (h % 2) * 64
                    g = h // 4
                    t.wait_ge(ac.h, p["rb_w"])
                    lhs = q_fm[base:base + 64, (h // 2) * T + qt * 128:
                               (h // 2) * T + (qt + 1) * 128]
                    ins = None
                    for n0 in range(0, L, 512):
                        w = min(512, L - n0)
                        ins = t.matmul(
                            scp[:, p["rb"] * 512 + n0: p["rb"] * 512 + n0 + w],
                            lhs, k_fm[base:base + 64, g * T + n0: g * T + n0 + w],
                            start=True, stop=True, tile_position=(base, 0))
                    ins.then_inc(pe.h, 1)

        @b.vector
        def _(v):
            v.wait_ge(ld.h, 16)
            for tk in range(NIT + 1):
                if tk < NIT:
                    p = P[tk]
                    L = p["L"]
                    dcol = p["rb"] * 512 + L - 128
                    v.wait_ge(pe.h, p["pe_s"])
                    if tk >= 2:
                        v.wait_ge(ac.h, P[tk - 2]["ac_exp"])
                    v.tensor_tensor(scp[:, dcol:dcol + 128],
                                    scp[:, dcol:dcol + 128],
                                    mask[:], ALU.add).then_inc(dv.h, 1)
                    v.wait_ge(dv.h, p["dv_max"] - 1)
                    v.reduce_max(nmax[tk % 2][:], sc_of(p), AX,
                                 negate=True).then_inc(dv.h, 1)
                if tk - 1 >= 0:
                    p = P[tk - 1]
                    v.wait_ge(ac.h, p["ac_exp"])
                    v.reciprocal(rinv[(tk - 1) % 2][:], rs[(tk - 1) % 2][:]) \
                        .then_inc(dv.h, 1)
                    v.wait_ge(dv.h, p["dv_norm"] - 1)
                    v.tensor_scalar_mul(probs[(tk - 1) % 2][:, 0:p["L"]],
                                        probs[(tk - 1) % 2][:, 0:p["L"]],
                                        rinv[(tk - 1) % 2][:]).then_inc(dv.h, 1)

        @b.scalar
        def _(sc):
            for tk in range(NIT + 3):
                if 0 <= tk - 3 < NIT:
                    p = P[tk - 3]
                    base = (p["h"] % 2) * 64
                    sc.wait_ge(pe.h, p["pe_v"])
                    sc.activation(attn_slice(p, base), pv_of(p, base),
                                  AF.Copy).then_inc(ac.h, 1)
                if 0 <= tk - 1 < NIT:
                    p = P[tk - 1]
                    i = tk - 1
                    sc.wait_ge(dv.h, p["dv_max"])
                    if i >= 2:
                        sc.wait_ge(trs.h, 16 * (i - 1))
                    sc.activation(probs[i % 2][:, 0:p["L"]], sc_of(p), AF.Exp,
                                  bias=nmax[i % 2][:],
                                  accum_out=rs[i % 2][:]).then_inc(ac.h, 1)

        @b.sync
        def _(s):
            ld.inc(s.dma_start(mask[:], io["c_causal"].ap()), 16)
            for i, p in enumerate(P):
                s.wait_ge(dv.h, p["dv_norm"])
                if i >= 2:
                    s.wait_ge(pe.h, P[i - 2]["pe_v"])
                trs.inc(s.dma_start(
                    probsT[i % 2][:, 0:p["L"]].rearrange("p (c f) -> p c f", f=128),
                    probs[i % 2][:, 0:p["L"]].rearrange("p (c f) -> p c f", f=128),
                    transpose=True), 16)

        if bg is not None:
            @b.gpsimd
            def _(g):
                bg(g)
    pe.n = pe_n[0]
    dv.n = dv_n[0]
    ac.n = ac_n[0]
    return {"sb_p3_probs0": probs[0], "sb_p3_probs1": probs[1],
            "sb_p3_pt0": probsT[0], "sb_p3_pt1": probsT[1],
            "sb_p3_rinv": rinv[0], "sb_p3_nmax": nmax[0], "sb_p3_mx0": nmax[1],
            "sb_p3_rs0": rs[0], "sb_p3_mask": mask}



def p4_l0_proj(cx, io, attn_fm):
    nc = cx.nc
    wl = cx.sem("p4wl")
    mm = cx.sem("p4mm")
    ev = cx.sem("p4ev")
    st = cx.sem("p4st")
    wo = cx.sbuf("p4wo", [128, KCD * D], BF16)
    sb = [cx.sbuf(f"p4sb{i}", [128, 512], F32) for i in range(4)]
    ps = [cx.psum(f"p4ps{i}", [128, 512]) for i in range(8)]

    with nc.Block() as b:
        @b.gpsimd
        def _(g):
            wl.inc(g.dma_start(
                wo[:].rearrange("p (c n) -> p c n", c=KCD),
                io["L0_attn_Wo"].ap().rearrange("(c p) n -> p c n", p=128)), 16)

        @b.tensor
        def _(t):
            t.wait_ge(wl.h, 16)
            for it in range(NT * 4):
                i, n = divmod(it, 4)
                bank = it % 8
                if it >= 8:
                    t.wait_ge(ev.h, it - 8 + 1)
                for kc in range(KCD):
                    ins = t.matmul(ps[bank][:],
                                   attn_fm[:, kc * T + i * 128: kc * T + (i + 1) * 128],
                                   wo[:, kc * D + n * 512: kc * D + (n + 1) * 512],
                                   start=(kc == 0), stop=(kc == KCD - 1))
                mm.inc(ins)

        @b.scalar
        def _(sc):
            for it in range(NT * 4):
                sc.wait_ge(mm.h, it + 1)
                if it >= 4:
                    sc.wait_ge(st.h, 16 * (it - 4 + 1))
                ev.inc(sc.activation(sb[it % 4][:], ps[it % 8][:], AF.Copy))

        @b.sync
        def _(s):
            for it in range(NT * 4):
                i, n = divmod(it, 4)
                s.wait_ge(ev.h, it + 1)
                st.inc(s.dma_start(
                    io["sc_s"].ap()[ts(i, 128), ts(n, 512)], sb[it % 4][:]), 16)
            s.wait_ge(st.h, st.n)



def p5_l0_ffn(cx, io):
    nc = cx.nc
    ld = cx.sem("p5ld")
    gl = cx.sem("p5gl")
    mm = cx.sem("p5mm")
    dv = cx.sem("p5dv")
    ac = cx.sem("p5ac")
    st = cx.sem("p5st")

    NQ = 4
    QT = 512
    IC = INTER // 128

    xn_q = cx.sbuf("p5xn", [128, KCD * QT], BF16)
    gwb = [cx.sbuf(f"p5gw{i}", [128, KCD * 128], BF16) for i in range(2)]
    uwb = [cx.sbuf(f"p5uw{i}", [128, KCD * 128], BF16) for i in range(2)]
    act_q = cx.sbuf("p5act", [128, IC * QT], BF16)
    dw_n = cx.sbuf("p5dw", [128, IC * QT], BF16)
    gsc = cx.sbuf("p5gsc", [128, IC], F32)
    usc = cx.sbuf("p5usc", [128, IC], F32)
    g_f = cx.sbuf("p5gf", [128, 512], F32)
    gs_ = cx.sbuf("p5gs", [128, 512], F32)
    gbuf = cx.sbuf("p5gbuf", [128, 4 * D], F32)
    xn1f = cx.sbuf("p5xn1f", [128, D], F32)
    xts = [cx.sbuf("p5xt0", [128, D], F32)] * 2
    sts_ = [cx.sbuf("p5st0", [128, D], F32)] * 2
    x1b = cx.sbuf("p5x1", [128, D], F32)
    xn1b = cx.sbuf("p5xn1b", [128, D], BF16)
    xlob = cx.sbuf("p5xlob", [128, D], BF16)
    sq5 = xn1f
    ssum = cx.sbuf("p5ssum", [128, 1], F32)
    ssq = cx.sbuf("p5ssq", [128, 1], F32)
    nmean = cx.sbuf("p5nmean", [128, 1], F32)
    var = cx.sbuf("p5var", [128, 1], F32)
    rstd = cx.sbuf("p5rstd", [128, 1], F32)
    ps = [cx.psum(f"p5ps{i}", [128, 512]) for i in range(8)]

    cnt = {"ld": 2, "gl": 0, "mm": 0, "dv": 0, "ac": 0, "st": 0}
    bank_hist = []
    P = []

    def use_bank():
        i = len(bank_hist)
        w = bank_hist[i - 8] if i >= 8 else None
        bank_hist.append(None)
        return i, w

    for q in range(NQ):
        plan = {"q": q, "oc": [], "bn": [], "tiles": []}
        cnt["ld"] += 4
        plan["ld_xn"] = cnt["ld"]
        for oc in range(IC):
            e = {}
            cnt["ld"] += 2
            e["ld_w"] = cnt["ld"]
            e["gu"], e["gu_w"] = use_bank()
            e["uu"], e["uu_w"] = use_bank()
            cnt["mm"] += 1
            e["mm_g"] = cnt["mm"]
            cnt["mm"] += 1
            e["mm_u"] = cnt["mm"]
            cnt["dv"] += 1
            e["dv_g"] = cnt["dv"]
            bank_hist[e["gu"]] = ("dv", cnt["dv"])
            cnt["ac"] += 1
            e["ac_silu"] = cnt["ac"]
            cnt["dv"] += 1
            e["dv_stt"] = cnt["dv"]
            bank_hist[e["uu"]] = ("dv", cnt["dv"])
            plan["oc"].append(e)
        for n in range(4):
            e = {}
            cnt["ld"] += 4
            e["ld_dw"] = cnt["ld"]
            e["tiles"] = []
            for t in range(4):
                f = {}
                f["bu"], f["bu_w"] = use_bank()
                cnt["mm"] += 1
                f["mm"] = cnt["mm"]
                cnt["ac"] += 1
                f["ac"] = cnt["ac"]
                bank_hist[f["bu"]] = ("ac", cnt["ac"])
                e["tiles"].append(f)
            plan["bn"].append(e)
        for t in range(4):
            f = {}
            cnt["gl"] += 2
            f["gl"] = cnt["gl"]
            base = cnt["dv"]
            f["dv0"] = base
            cnt["dv"] += 9
            f["dv_x1"] = base + 2
            f["dv_red"] = base + 3
            f["dv_mean"] = base + 4
            f["dv_vs"] = base + 5
            f["dv_ve"] = base + 6
            f["dv_rec"] = base + 7
            f["dv_done"] = base + 9
            cnt["ac"] += 2
            f["ac_sq"] = cnt["ac"] - 1
            f["ac_rt"] = cnt["ac"]
            cnt["st"] += 3
            f["st"] = cnt["st"]
            plan["tiles"].append(f)
        P.append(plan)

    with nc.Block() as b:
        @b.sync
        def _(s):
            s.wait_ge(cx.p0sem.h, 16 * cx.mark_l0)
            ld.inc(s.dma_start(gsc[:], io["c_gate_s_t"].ap()), 16)
            ld.inc(s.dma_start(usc[:], io["c_up_s_t"].ap()), 16)
            for q in range(NQ):
                plan = P[q]
                if q > 0:
                    s.wait_ge(mm.h, P[q - 1]["oc"][-1]["mm_u"])
                for bb in range(4):
                    ld.inc(s.dma_start(
                        bass.AP(xn_q, bb * 128, [[KCD * QT, 128], [QT, KCD], [1, 128]]),
                        io["sc_xn"].ap()[ts(4 * q + bb, 128), :],
                        transpose=True), 16)
                for oc in range(IC):
                    e = plan["oc"][oc]
                    if oc >= 2:
                        s.wait_ge(mm.h, plan["oc"][oc - 2]["mm_u"])
                    elif q > 0 and oc < 2:
                        s.wait_ge(mm.h, P[q - 1]["oc"][oc - 2]["mm_u"])
                    ld.inc(s.dma_start(
                        bass.AP(gwb[oc % 2], 0, [[KCD * 128, 128], [128, KCD], [1, 128]]),
                        io["sc_gate_w"].ap()[ts(oc, 128), :], transpose=True), 16)
                    ld.inc(s.dma_start(
                        bass.AP(uwb[oc % 2], 0, [[KCD * 128, 128], [128, KCD], [1, 128]]),
                        io["sc_up_w"].ap()[ts(oc, 128), :], transpose=True), 16)
                for n in range(4):
                    e = plan["bn"][n]
                    if n > 0:
                        s.wait_ge(mm.h, plan["bn"][n - 1]["tiles"][-1]["mm"])
                    elif q > 0:
                        s.wait_ge(mm.h, P[q - 1]["bn"][-1]["tiles"][-1]["mm"])
                    for hb in range(4):
                        ld.inc(s.dma_start(
                            bass.AP(dw_n, hb * 128, [[IC * QT, 128], [QT, IC], [1, 128]]),
                            io["sc_down_w"].ap()[n * 512 + hb * 128:
                                                 n * 512 + (hb + 1) * 128, :],
                            transpose=True), 16)

        @b.gpsimd
        def _(g):
            for q in range(NQ):
                plan = P[q]
                for t in range(4):
                    f = plan["tiles"][t]
                    it = q * 4 + t
                    if it >= 1:
                        g.wait_ge(dv.h, P[(it - 1) // 4]["tiles"][(it - 1) % 4]["dv_x1"])
                    g.dma_start(xts[it % 2][:],
                                io["x"].ap()[ts(it, 128), :]).then_inc(gl.h, 16)
                    g.dma_start(sts_[it % 2][:],
                                io["sc_s"].ap()[ts(it, 128), :]).then_inc(gl.h, 16)
                    g.wait_ge(dv.h, f["dv_x1"])
                    g.dma_start(io["sc_x1"].ap()[ts(it, 128), :], x1b[:]).then_inc(st.h, 16)
                    g.wait_ge(dv.h, f["dv_done"])
                    g.dma_start(io["sc_xn1"].ap()[ts(it, 128), :], xn1b[:]).then_inc(st.h, 16)
                    g.dma_start(io["sc_xlo"].ap()[ts(it, 128), :], xlob[:]).then_inc(st.h, 16)
            g.wait_ge(st.h, 16 * cnt["st"])
        st.n = 16 * cnt["st"]
        gl.n = 16 * cnt["gl"]

        @b.tensor
        def _(t):
            for q in range(NQ):
                plan = P[q]
                t.wait_ge(ld.h, 16 * plan["ld_xn"])
                for oc in range(IC):
                    e = plan["oc"][oc]
                    t.wait_ge(ld.h, 16 * e["ld_w"])
                    for which, bid, wt, mmord in (
                            ("g", e["gu"], e["gu_w"], e["mm_g"]),
                            ("u", e["uu"], e["uu_w"], e["mm_u"])):
                        if wt is not None:
                            t.wait_ge(dv.h if wt[0] == "dv" else ac.h, wt[1])
                        wsrc = gwb[oc % 2] if which == "g" else uwb[oc % 2]
                        for kc in range(KCD):
                            ins = t.matmul(
                                ps[bid % 8][:],
                                wsrc[:, ts(kc, 128)],
                                xn_q[:, ts(kc, QT)],
                                start=(kc == 0), stop=(kc == KCD - 1))
                        ins.then_inc(mm.h, 1)
                t.wait_ge(dv.h, plan["oc"][-1]["dv_stt"])
                for n in range(4):
                    e = plan["bn"][n]
                    t.wait_ge(ld.h, 16 * e["ld_dw"])
                    for ti in range(4):
                        f = e["tiles"][ti]
                        if f["bu_w"] is not None:
                            wt = f["bu_w"]
                            t.wait_ge(dv.h if wt[0] == "dv" else ac.h, wt[1])
                        for ic in range(IC):
                            ins = t.matmul(
                                ps[f["bu"] % 8][:],
                                act_q[:, ic * QT + ti * 128: ic * QT + (ti + 1) * 128],
                                dw_n[:, ts(ic, QT)],
                                start=(ic == 0), stop=(ic == IC - 1))
                        ins.then_inc(mm.h, 1)
        mm.n = cnt["mm"]

        @b.vector
        def _(v):
            for q in range(NQ):
                plan = P[q]
                for oc in range(IC):
                    e = plan["oc"][oc]
                    v.wait_ge(mm.h, e["mm_g"])
                    if oc > 0:
                        v.wait_ge(ac.h, plan["oc"][oc - 1]["ac_silu"])
                    elif q > 0:
                        v.wait_ge(ac.h, P[q - 1]["oc"][-1]["ac_silu"])
                    v.tensor_scalar_mul(g_f[:], ps[e["gu"] % 8][:],
                                        gsc[:, oc:oc + 1]).then_inc(dv.h, 1)
                    v.wait_ge(mm.h, e["mm_u"])
                    v.wait_ge(ac.h, e["ac_silu"])
                    if oc == 0 and q > 0:
                        v.wait_ge(mm.h, P[q - 1]["bn"][-1]["tiles"][-1]["mm"])
                    v.scalar_tensor_tensor(
                        act_q[:, ts(oc, QT)], ps[e["uu"] % 8][:],
                        usc[:, oc:oc + 1], gs_[:], ALU.mult, ALU.mult) \
                        .then_inc(dv.h, 1)
                for t in range(4):
                    f = plan["tiles"][t]
                    it = q * 4 + t
                    v.wait_ge(ac.h, plan["bn"][3]["tiles"][t]["ac"])
                    v.wait_ge(gl.h, 16 * f["gl"])
                    if it >= 1:
                        v.wait_ge(st.h, 16 * P[(it - 1) // 4]["tiles"][(it - 1) % 4]["st"])
                    gb = gbuf[:, ts(t, D)]
                    v.tensor_tensor(x1b[:], sts_[it % 2][:], gb, ALU.mult) \
                        .then_inc(dv.h, 1)
                    v.tensor_tensor(x1b[:], x1b[:], xts[it % 2][:], ALU.add) \
                        .then_inc(dv.h, 1)
                    v.reduce_sum(ssum[:], x1b[:], AX).then_inc(dv.h, 1)
                    v.wait_ge(dv.h, f["dv_red"])
                    v.wait_ge(ac.h, f["ac_sq"])
                    v.tensor_scalar_mul(nmean[:], ssum[:], -1.0 / D).then_inc(dv.h, 1)
                    v.wait_ge(dv.h, f["dv_mean"])
                    v.tensor_scalar_mul(var[:], ssq[:], 1.0 / D)
                    v.tensor_scalar(ssq[:], nmean[:], nmean[:], None, ALU.mult) \
                        .then_inc(dv.h, 1)
                    v.wait_ge(dv.h, f["dv_vs"])
                    v.tensor_tensor(var[:], var[:], ssq[:], ALU.subtract)
                    v.tensor_scalar(var[:], var[:], 1e-5, None, ALU.add) \
                        .then_inc(dv.h, 1)
                    v.wait_ge(dv.h, f["dv_ve"])
                    v.reciprocal(rstd[:], var[:]).then_inc(dv.h, 1)
                    v.wait_ge(ac.h, f["ac_rt"])
                    v.tensor_scalar(xn1f[:], x1b[:], nmean[:], None, ALU.add)
                    v.tensor_scalar(xn1f[:], xn1f[:], rstd[:], None, ALU.mult) \
                        .then_inc(dv.h, 1)
                    v.tensor_copy(xn1b[:], xn1f[:])
                    v.tensor_tensor(xlob[:], xn1f[:], xn1b[:], ALU.subtract) \
                        .then_inc(dv.h, 1)
        dv.n = cnt["dv"]

        @b.scalar
        def _(sc):
            for q in range(NQ):
                plan = P[q]
                for oc in range(IC):
                    e = plan["oc"][oc]
                    sc.wait_ge(dv.h, e["dv_g"])
                    if oc > 0:
                        sc.wait_ge(dv.h, plan["oc"][oc - 1]["dv_stt"])
                    elif q > 0:
                        sc.wait_ge(dv.h, P[q - 1]["oc"][-1]["dv_stt"])
                    sc.activation(gs_[:], g_f[:], AF.Silu).then_inc(ac.h, 1)
                for n in range(4):
                    e = plan["bn"][n]
                    for ti in range(4):
                        f = e["tiles"][ti]
                        sc.wait_ge(mm.h, f["mm"])
                        if q > 0:
                            sc.wait_ge(dv.h, P[q - 1]["tiles"][ti]["dv_x1"])
                        sc.activation(gbuf[:, ti * D + n * 512: ti * D + (n + 1) * 512],
                                      ps[f["bu"] % 8][:], AF.Silu).then_inc(ac.h, 1)
                for t in range(4):
                    f = plan["tiles"][t]
                    sc.wait_ge(dv.h, f["dv_x1"])
                    sc.activation(sq5[:], x1b[:], AF.Square, accum_out=ssq[:]) \
                        .then_inc(ac.h, 1)
                    sc.wait_ge(dv.h, f["dv_rec"])
                    sc.activation(rstd[:], rstd[:], AF.Sqrt).then_inc(ac.h, 1)
        ac.n = cnt["ac"]



def p6_l1_attn(cx, io):
    nc = cx.nc
    ld = cx.sem("p6ld")
    gl = cx.sem("p6gl")
    mm = cx.sem("p6mm")
    dv = cx.sem("p6dv")
    ac = cx.sem("p6ac")
    st = cx.sem("p6st")
    qs = 1.0 / math.sqrt(D)

    bqt = cx.sbuf("p6bq", [128, NL1 * KCD], F32)
    bkt = cx.sbuf("p6bk", [128, NL1 * KCD], F32)
    bot = cx.sbuf("p6bo", [128, NL1 * KCD], F32)
    ones = cx.sbuf("p6ones", [1, 128], BF16)
    bvr = cx.sbuf("p6bv", [1, NL1 * D], BF16)
    with nc.Block() as b:
        @b.sync
        def _(s):
            ld.inc(s.dma_start(bqt[:], io["c_bq_t"].ap()), 16)
            ld.inc(s.dma_start(bkt[:], io["c_bk_t"].ap()), 16)
            ld.inc(s.dma_start(bot[:], io["c_bo_t"].ap()), 16)
            ld.inc(s.dma_start(ones[:], io["c_ones"].ap()), 16)
            ld.inc(s.dma_start(bvr[:], io["c_bv_row"].ap()), 16)
            s.wait_ge(ld.h, ld.n)
        @b.vector
        def _(v):
            v.wait_ge(ld.h, ld.n)

    for l in range(NL1):
        src_fm = io["sc_xn1"] if l == 0 else io["sc_xa%d" % ((l - 1) % 2)]
        src_is_tm = (l == 0)
        dst = io["sc_s"] if l == NL1 - 1 else io["sc_xa%d" % (l % 2)]

        with cx.phase():
            xa = cx.sbuf(f"p6xa{l}", [128, KCD * T], BF16)
            wcol = [cx.sbuf(f"p6wc{l}{i}", [128, KCD * 128], BF16) for i in range(2)]
            wv = cx.sbuf(f"p6wv{l}", [128, KCD * 512], BF16)
            stg = [cx.sbuf(f"p6stg{l}{i}", [128, 512], BF16) for i in range(4)]
            ps = [cx.psum(f"p6ps{l}{i}", [128, 512]) for i in range(8)]

            ld0, gl0, mm0, ac0, st0 = ld.n, gl.n, mm.n, ac.n, st.n
            with nc.Block() as b:
                @b.sync
                def _(s):
                    s.wait_ge(st.h, st0)
                    if src_is_tm:
                        for bb in range(NT):
                            ld.inc(s.dma_start(
                                bass.AP(xa, bb * 128,
                                        [[KCD * T, 128], [T, KCD], [1, 128]]),
                                src_fm.ap()[ts(bb, 128), :], transpose=True), 16)
                    else:
                        for kc in range(KCD):
                            ld.inc(s.dma_start(
                                xa[:].rearrange("p (c t) -> p c t", c=KCD)[:, kc:kc + 1, :],
                                src_fm.ap().rearrange("(c p) t -> p c t", p=128)[:, kc:kc + 1, :]), 16)
                    for j in range(192):
                        s.wait_ge(ac.h, ac0 + j + 1)
                        if j < 64:
                            m, n = divmod(j, 4)
                            dst_ap = io["sc_qf"].ap()[ts(m, 128), ts(n, 512)]
                        elif j < 128:
                            m, n = divmod(j - 64, 4)
                            dst_ap = io["sc_kf"].ap()[ts(m, 128), ts(n, 512)]
                        else:
                            n, t_ = divmod(j - 128, 16)
                            dst_ap = io["sc_v"].ap()[ts(t_, 128), ts(n, 512)]
                        st.inc(s.dma_start(dst_ap, stg[j % 4][:]), 16)

                @b.gpsimd
                def _(g):
                    moe = cx.cast_jobs[cx.mark_l0:]
                    q4 = (len(moe) + NL1 - 1) // NL1
                    for dst_ap, src_ap in moe[l * q4:(l + 1) * q4]:
                        cx.p0sem.inc(g.dma_start(dst_ap, src_ap), 16)
                    for proj in range(2):
                        w = io["L1_attn_Wq"] if proj == 0 else io["L1_attn_Wk"]
                        for m in range(KCD):
                            j = proj * 16 + m
                            if j >= 2:
                                g.wait_ge(mm.h, mm0 + (j - 1) * 4)
                            gl.inc(g.dma_start(
                                wcol[m % 2][:].rearrange("p (c f) -> p c f", c=KCD),
                                w.ap()[l].rearrange("(c p) f -> p c f", p=128)
                                 [:, :, ts(m, 128)]), 16)
                    for n in range(4):
                        g.wait_ge(mm.h, mm0 + 128 + n * 16)
                        gl.inc(g.dma_start(
                            wv[:].rearrange("p (c f) -> p c f", c=KCD),
                            io["L1_attn_Wv"].ap()[l].rearrange("(c p) f -> p c f", p=128)
                             [:, :, ts(n, 512)]), 16)

                @b.tensor
                def _(t):
                    t.wait_ge(ld.h, ld0 + 16 * KCD)
                    u = 0
                    for proj in range(2):
                        for m in range(KCD):
                            t.wait_ge(gl.h, gl0 + 16 * (proj * 16 + m + 1))
                            for n in range(4):
                                if u >= 8:
                                    t.wait_ge(ac.h, ac0 + u - 8 + 1)
                                for kc in range(KCD):
                                    ins = t.matmul(
                                        ps[u % 8][:],
                                        wcol[m % 2][:, ts(kc, 128)],
                                        xa[:, kc * T + n * 512: kc * T + (n + 1) * 512],
                                        start=(kc == 0), stop=(kc == KCD - 1))
                                mm.inc(ins)
                                u += 1
                    for n in range(4):
                        t.wait_ge(gl.h, gl0 + 16 * (32 + n + 1))
                        for t_ in range(NT):
                            if u >= 8:
                                t.wait_ge(ac.h, ac0 + u - 8 + 1)
                            for kc in range(KCD):
                                t.matmul(
                                    ps[u % 8][:],
                                    xa[:, kc * T + t_ * 128: kc * T + (t_ + 1) * 128],
                                    wv[:, ts(kc, 512)],
                                    start=(kc == 0), stop=False)
                            ins = t.matmul(ps[u % 8][:], ones[:],
                                           bvr[0:1, l * D + n * 512: l * D + (n + 1) * 512],
                                           start=False, stop=True)
                            mm.inc(ins)
                            u += 1

                @b.vector
                def _(v):
                    for j in range(192):
                        v.wait_ge(mm.h, mm0 + j + 1)
                        if j >= 4:
                            v.wait_ge(st.h, st0 + 16 * (j - 4 + 1))
                        if j < 64:
                            m = j // 4
                            v.tensor_scalar(stg[j % 4][:], ps[j % 8][:],
                                            bqt[:, l * KCD + m: l * KCD + m + 1],
                                            None, ALU.add).then_inc(ac.h, 1)
                        elif j < 128:
                            m = (j - 64) // 4
                            v.tensor_scalar(stg[j % 4][:], ps[j % 8][:],
                                            bkt[:, l * KCD + m: l * KCD + m + 1],
                                            None, ALU.add).then_inc(ac.h, 1)
                        else:
                            v.tensor_copy(stg[j % 4][:], ps[j % 8][:]) \
                                .then_inc(ac.h, 1)
                    ac.n += 192

        with cx.phase():
            k_sb = cx.sbuf(f"p6k{l}", [128, KCD * T], BF16)
            v_sb = cx.sbuf(f"p6v{l}", [128, KCD * D], BF16)
            qt_b = [cx.sbuf(f"p6qt{l}{i}", [128, KCD * 128], BF16) for i in range(2)]
            probs = [cx.sbuf(f"p6pr{l}{i}", [128, T], BF16) for i in range(2)]
            probsT = [cx.sbuf(f"p6pt{l}{i}", [128, T], BF16) for i in range(2)]
            astg = [cx.sbuf(f"p6as{l}{i}", [128, 512], BF16) for i in range(4)]
            nmax = [cx.sbuf(f"p6nm{l}{i}", [128, 1], F32) for i in range(2)]
            rs = [cx.sbuf(f"p6rs{l}{i}", [128, 1], F32) for i in range(2)]
            rinv = [cx.sbuf(f"p6ri{l}{i}", [128, 1], F32) for i in range(2)]
            psA = cx.psum(f"p6A{l}", [128, T])
            psB = cx.psum(f"p6B{l}", [128, T])

            ld0, gl0, mm0, dv0, ac0, st0, tr0 = ld.n, gl.n, mm.n, dv.n, ac.n, st.n, 0
            trs = cx.sem(f"p6tr{l}")
            with nc.Block() as b:
                @b.sync
                def _(s):
                    s.wait_ge(st.h, st0)
                    for kc in range(KCD):
                        ld.inc(s.dma_start(
                            k_sb[:].rearrange("p (c t) -> p c t", c=KCD)[:, kc:kc + 1, :],
                            io["sc_kf"].ap().rearrange("(c p) t -> p c t", p=128)
                             [:, kc:kc + 1, :]), 16)
                    for kc in range(KCD):
                        ld.inc(s.dma_start(
                            v_sb[:].rearrange("p (c t) -> p c t", c=KCD)[:, kc:kc + 1, :],
                            io["sc_v"].ap().rearrange("(c p) t -> p c t", p=128)
                             [:, kc:kc + 1, :]), 16)
                    for qt in range(NT):
                        if qt >= 2:
                            s.wait_ge(mm.h, mm0 + 5 * (qt - 2) + 1)
                        ld.inc(s.dma_start(
                            qt_b[qt % 2][:].rearrange("p (c f) -> p c f", c=KCD),
                            io["sc_qf"].ap().rearrange("(c p) t -> p c t", p=128)
                             [:, :, ts(qt, 128)]), 16)
                        s.wait_ge(ac.h, ac0 + qt + 1)
                        if qt >= 2:
                            s.wait_ge(mm.h, mm0 + 5 * (qt - 2) + 5)
                        trs.inc(s.dma_start(
                            probsT[qt % 2][:].rearrange("p (c f) -> p c f", f=128),
                            probs[qt % 2][:].rearrange("p (c f) -> p c f", f=128),
                            transpose=True), 16)

                @b.tensor
                def _(t):
                    t.wait_ge(ld.h, ld0 + 16 * 2 * KCD)
                    for qt in range(NT):
                        t.wait_ge(ld.h, ld0 + 16 * (2 * KCD + qt + 1))
                        if qt >= 1:
                            t.wait_ge(ac.h, ac0 + qt)
                        for n in range(4):
                            for kc in range(KCD):
                                ins = t.matmul(
                                    psA[:, ts(n, 512)],
                                    qt_b[qt % 2][:, ts(kc, 128)],
                                    k_sb[:, kc * T + n * 512: kc * T + (n + 1) * 512],
                                    start=(kc == 0), stop=(kc == KCD - 1))
                        ins.then_inc(mm.h, 1)
                        mm.n += 1
                        t.wait_ge(trs.h, 16 * (qt + 1))
                        if qt >= 1:
                            t.wait_ge(dv.h, dv0 + 7 * qt)
                        for n in range(4):
                            for kc in range(KCD):
                                ins = t.matmul(
                                    psB[:, ts(n, 512)],
                                    probsT[qt % 2][:, ts(kc, 128)],
                                    v_sb[:, kc * D + n * 512: kc * D + (n + 1) * 512],
                                    start=(kc == 0), stop=(kc == KCD - 1))
                            mm.inc(ins)

                @b.vector
                def _(v):
                    for qt in range(NT):
                        v.wait_ge(mm.h, mm0 + 5 * qt + 1)
                        v.reduce_max(nmax[qt % 2][:], psA[:], AX, negate=True) \
                            .then_inc(dv.h, 1)
                        v.wait_ge(dv.h, dv0 + 7 * qt + 1)
                        v.tensor_scalar_mul(nmax[qt % 2][:], nmax[qt % 2][:], qs) \
                            .then_inc(dv.h, 1)
                        v.wait_ge(ac.h, ac0 + qt + 1)
                        v.reciprocal(rinv[qt % 2][:], rs[qt % 2][:]).then_inc(dv.h, 1)
                        dv.n += 3
                        v.wait_ge(mm.h, mm0 + 5 * qt + 5)
                        v.wait_ge(dv.h, dv0 + 7 * qt + 3)
                        for n in range(4):
                            if 4 * qt + n >= 4:
                                v.wait_ge(st.h, st0 + 16 * (4 * qt + n - 4 + 1))
                            v.tensor_scalar_mul(astg[n % 4][:], psB[:, ts(n, 512)],
                                                rinv[qt % 2][:]).then_inc(dv.h, 1)
                            dv.n += 1

                @b.scalar
                def _(sc):
                    for qt in range(NT):
                        sc.wait_ge(dv.h, dv0 + 7 * qt + 2)
                        if qt >= 2:
                            sc.wait_ge(dv.h, dv0 + 7 * (qt - 1))
                            sc.wait_ge(trs.h, 16 * (qt - 1))
                        sc.activation(probs[qt % 2][:], psA[:], AF.Exp,
                                      bias=nmax[qt % 2][:], scale=qs,
                                      accum_out=rs[qt % 2][:]).then_inc(ac.h, 1)
                        ac.n += 1

                @b.gpsimd
                def _(g):
                    for qt in range(NT):
                        for n in range(4):
                            g.wait_ge(dv.h, dv0 + 7 * qt + 3 + n + 1)
                            st.inc(g.dma_start(
                                io["sc_attn"].ap()[ts(qt, 128), ts(n, 512)],
                                astg[n % 4][:]), 16)
                    g.wait_ge(st.h, st.n)

        last = (l == NL1 - 1)
        with cx.phase():
            at_sb = cx.sbuf(f"p6at{l}", [128, KCD * T], BF16)
            if last:
                wcol = [cx.sbuf(f"p6oc{l}{i}", [128, KCD * 512], BF16)
                        for i in range(2)]
                bor = cx.sbuf(f"p6bor{l}", [1, D], BF16)
            else:
                wcol = [cx.sbuf(f"p6oc{l}{i}", [128, KCD * 128], BF16)
                        for i in range(2)]
            ostg = [cx.sbuf(f"p6os{l}{i}", [128, 512],
                            F32 if last else BF16) for i in range(4)]
            ps = [cx.psum(f"p6cps{l}{i}", [128, 512]) for i in range(8)]
            ld0, gl0, mm0, ac0, st0 = ld.n, gl.n, mm.n, ac.n, st.n
            with nc.Block() as b:
                @b.sync
                def _(s):
                    s.wait_ge(st.h, st0)
                    if last:
                        ld.inc(s.dma_start(bor[:], io["c_bo_row"].ap()), 16)
                    for bb in range(NT):
                        ld.inc(s.dma_start(
                            bass.AP(at_sb, bb * 128,
                                    [[KCD * T, 128], [T, KCD], [1, 128]]),
                            io["sc_attn"].ap()[ts(bb, 128), :], transpose=True), 16)
                    for j in range(64):
                        if last:
                            n, m = divmod(j, 16)
                        else:
                            m, n = divmod(j, 4)
                        dap = dst.ap()[ts(m, 128), ts(n, 512)]
                        s.wait_ge(ac.h, ac0 + j + 1)
                        st.inc(s.dma_start(dap, ostg[j % 4][:]), 16)
                    s.wait_ge(st.h, st.n)

                @b.gpsimd
                def _(g):
                    if last:
                        for n in range(4):
                            if n >= 2:
                                g.wait_ge(mm.h, mm0 + (n - 1) * 16)
                            gl.inc(g.dma_start(
                                wcol[n % 2][:].rearrange("p (c f) -> p c f", c=KCD),
                                io["L1_attn_Wo"].ap()[l]
                                 .rearrange("(c p) f -> p c f", p=128)
                                 [:, :, ts(n, 512)]), 16)
                    else:
                        for m in range(KCD):
                            if m >= 2:
                                g.wait_ge(mm.h, mm0 + (m - 1) * 4)
                            gl.inc(g.dma_start(
                                wcol[m % 2][:].rearrange("p (c f) -> p c f", c=KCD),
                                io["L1_attn_Wo"].ap()[l]
                                 .rearrange("(c p) f -> p c f", p=128)
                                 [:, :, ts(m, 128)]), 16)

                @b.tensor
                def _(t):
                    t.wait_ge(ld.h, ld0 + 16 * (NT + (1 if last else 0)))
                    u = 0
                    if last:
                        for n in range(4):
                            t.wait_ge(gl.h, gl0 + 16 * (n + 1))
                            for t_ in range(NT):
                                if u >= 8:
                                    t.wait_ge(ac.h, ac0 + u - 8 + 1)
                                for kc in range(KCD):
                                    t.matmul(
                                        ps[u % 8][:],
                                        at_sb[:, kc * T + t_ * 128: kc * T + (t_ + 1) * 128],
                                        wcol[n % 2][:, ts(kc, 512)],
                                        start=(kc == 0), stop=False)
                                ins = t.matmul(ps[u % 8][:], ones[:],
                                               bor[0:1, ts(n, 512)],
                                               start=False, stop=True)
                                mm.inc(ins)
                                u += 1
                    else:
                        for m in range(KCD):
                            t.wait_ge(gl.h, gl0 + 16 * (m + 1))
                            for n in range(4):
                                if u >= 8:
                                    t.wait_ge(ac.h, ac0 + u - 8 + 1)
                                for kc in range(KCD):
                                    ins = t.matmul(
                                        ps[u % 8][:],
                                        wcol[m % 2][:, ts(kc, 128)],
                                        at_sb[:, kc * T + n * 512: kc * T + (n + 1) * 512],
                                        start=(kc == 0), stop=(kc == KCD - 1))
                                mm.inc(ins)
                                u += 1

                @b.vector
                def _(v):
                    for j in range(64):
                        v.wait_ge(mm.h, mm0 + j + 1)
                        if j >= 4:
                            v.wait_ge(st.h, st0 + 16 * (j - 4 + 1))
                        if last:
                            v.tensor_copy(ostg[j % 4][:], ps[j % 8][:]) \
                                .then_inc(ac.h, 1)
                        else:
                            m = j // 4
                            v.tensor_scalar(ostg[j % 4][:], ps[j % 8][:],
                                            bot[:, l * KCD + m: l * KCD + m + 1],
                                            None, ALU.add).then_inc(ac.h, 1)
                    ac.n += 64



def p7_gate(cx, io, combine_sb):
    nc = cx.nc
    ld = cx.sem("p7ld")
    dv = cx.sem("p7dv")
    ac = cx.sem("p7ac")
    mm = cx.sem("p7mm")
    gwf = cx.sbuf("p7gwf", [128, KCD * NEXP], F32)
    gwh = cx.sbuf("p7gwh", [128, KCD * NEXP], BF16)
    gwl = cx.sbuf("p7gwl", [128, KCD * NEXP], BF16)
    xh = [cx.sbuf(f"p7xh{i}", [128, KCD * 128], BF16) for i in range(2)]
    xl = [cx.sbuf(f"p7xl{i}", [128, KCD * 128], BF16) for i in range(2)]
    pr = cx.sbuf("p7pr", [128, NEXP], F32)
    pn = cx.sbuf("p7pn", [128, NEXP], F32)
    msk1 = cx.sbuf("p7m1k", [128, NEXP], F32)
    tmp8 = cx.sbuf("p7t8", [128, NEXP], F32)
    nm = cx.sbuf("p7nm", [128, 1], F32)
    rsum = cx.sbuf("p7rs", [128, 1], F32)
    rinv = cx.sbuf("p7ri", [128, 1], F32)
    m1 = cx.sbuf("p7m1", [128, 1], F32)
    m2 = cx.sbuf("p7m2", [128, 1], F32)
    den = cx.sbuf("p7dn", [128, 1], F32)
    w1_ = cx.sbuf("p7w1", [128, 1], F32)
    w2_ = cx.sbuf("p7w2", [128, 1], F32)
    psg = cx.psum("p7ps", [128, NEXP])

    with nc.Block() as b:
        @b.sync
        def _(s):
            ld.inc(s.dma_start(
                gwf[:].rearrange("p (c e) -> p c e", c=KCD),
                io["c_gate_wT"].ap().rearrange("(c p) e -> p c e", p=128)), 16)
            for i in range(NT):
                if i >= 2:
                    s.wait_ge(mm.h, i - 1)
                ld.inc(s.dma_start(
                    bass.AP(xh[i % 2], 0, [[KCD * 128, 128], [128, KCD], [1, 128]]),
                    io["sc_xn1"].ap()[ts(i, 128), :], transpose=True), 16)
                ld.inc(s.dma_start(
                    bass.AP(xl[i % 2], 0, [[KCD * 128, 128], [128, KCD], [1, 128]]),
                    io["sc_xlo"].ap()[ts(i, 128), :], transpose=True), 16)

        @b.vector
        def _(v):
            v.wait_ge(ld.h, 16)
            dv.inc(v.tensor_copy(gwh[:], gwf[:]))
            v.wait_ge(dv.h, dv.n)
            dv.inc(v.tensor_tensor(gwl[:], gwf[:], gwh[:], ALU.subtract))
            for i in range(NT):
                v.wait_ge(mm.h, i + 1)
                dv.inc(v.reduce_max(nm[:], psg[:], AX, negate=True))
                v.wait_ge(dv.h, dv.n)
                v.wait_ge(ac.h, i + 1)
                dv.inc(v.reciprocal(rinv[:], rsum[:]))
                v.wait_ge(dv.h, dv.n)
                dv.inc(v.tensor_scalar_mul(pn[:], pr[:], rinv[:]))
                v.wait_ge(dv.h, dv.n)
                dv.inc(v.reduce_max(m1[:], pn[:], AX))
                v.wait_ge(dv.h, dv.n)
                dv.inc(v.tensor_scalar(msk1[:], pn[:], m1[:], None, ALU.is_equal))
                v.wait_ge(dv.h, dv.n)
                dv.inc(v.scalar_tensor_tensor(tmp8[:], msk1[:], -1e30, pn[:],
                                              ALU.mult, ALU.add))
                v.wait_ge(dv.h, dv.n)
                dv.inc(v.reduce_max(m2[:], tmp8[:], AX))
                v.wait_ge(dv.h, dv.n)
                dv.inc(v.tensor_tensor(den[:], m1[:], m2[:], ALU.add))
                v.wait_ge(dv.h, dv.n)
                dv.inc(v.reciprocal(w1_[:], den[:]))
                v.wait_ge(dv.h, dv.n)
                dv.inc(v.tensor_scalar(w2_[:], m2[:], w1_[:], None, ALU.mult))
                dv.inc(v.tensor_scalar(w1_[:], m1[:], w1_[:], None, ALU.mult))
                v.wait_ge(dv.h, dv.n)
                dv.inc(v.tensor_scalar(msk1[:], msk1[:], w1_[:], None, ALU.mult))
                dv.inc(v.tensor_scalar(tmp8[:], pn[:], m2[:], None, ALU.is_equal))
                v.wait_ge(dv.h, dv.n)
                dv.inc(v.scalar_tensor_tensor(
                    combine_sb[:, ts(i, NEXP)], tmp8[:], w2_[:], msk1[:],
                    ALU.mult, ALU.add))
                v.wait_ge(dv.h, dv.n)

        @b.tensor
        def _(t):
            t.wait_ge(dv.h, 2)
            for i in range(NT):
                t.wait_ge(ld.h, 16 * (2 * i + 3))
                if i >= 1:
                    t.wait_ge(ac.h, i)
                for kc in range(KCD):
                    last = (kc == KCD - 1)
                    t.matmul(psg[:], xh[i % 2][:, ts(kc, 128)],
                             gwh[:, ts(kc, NEXP)], start=(kc == 0), stop=False)
                    t.matmul(psg[:], xl[i % 2][:, ts(kc, 128)],
                             gwh[:, ts(kc, NEXP)], start=False, stop=False)
                    ins = t.matmul(psg[:], xh[i % 2][:, ts(kc, 128)],
                                   gwl[:, ts(kc, NEXP)], start=False, stop=last)
                mm.inc(ins)

        @b.scalar
        def _(sc):
            for i in range(NT):
                sc.wait_ge(dv.h, 2 + 14 * i + 1)
                sc.activation(pr[:], psg[:], AF.Exp, bias=nm[:],
                              accum_out=rsum[:]).then_inc(ac.h, 1)
        ac.n = NT



def p8_moe(cx, io, combine_sb):
    nc = cx.nc
    ld = cx.sem("p8ld")
    gl = cx.sem("p8gl")
    mm = cx.sem("p8mm")
    dv = cx.sem("p8dv")
    ac = cx.sem("p8ac")
    st = cx.sem("p8st")
    HT = 1024
    NTH = HT // 128

    xn_h = cx.sbuf("p8xn", [128, KCD * HT], BF16)
    acc = cx.sbuf("p8acc", [128, NTH * D], F32)
    act_h = cx.sbuf("p8act", [128, MIC * HT], BF16)
    w2t = cx.sbuf("p8w2", [128, MIC * D], BF16)
    w1t = [cx.sbuf(f"p8w1{i}", [128, KCD * 128], BF16) for i in range(4)]
    gs8 = cx.sbuf("p8gs", [128, HT], F32)
    xt8 = cx.sbuf("p8xt", [128, D], F32)
    st8 = cx.sbuf("p8st_", [128, D], F32)
    ps = [cx.psum(f"p8ps{i}", [128, 512]) for i in range(8)]

    cnt = {"ld": 0, "mm": 0, "dv": 0, "ac": 0, "st": 0, "gl": 0}
    bank_hist = []
    P = []

    def use_bank():
        i = len(bank_hist)
        w = bank_hist[i - 8] if i >= 8 else None
        bank_hist.append(None)
        return i, w

    for h in range(2):
        plan = {"pairs": [], "m2": []}
        cnt["ld"] += NTH
        plan["ld_xn"] = cnt["ld"]
        for e in range(NEXP):
            ep = {"pairs": [], "m2": []}
            cnt["ld"] += KCD
            ep["ld_w2"] = cnt["ld"]
            for j in range(MIC):
                pr = {}
                cnt["ld"] += 2
                pr["ld_w1"] = cnt["ld"]
                pr["gu"] = [use_bank() for _ in range(2)]
                cnt["mm"] += 1
                pr["mm_g"] = cnt["mm"]
                pr["uu"] = [use_bank() for _ in range(2)]
                cnt["mm"] += 1
                pr["mm_u"] = cnt["mm"]
                cnt["ac"] += 2
                pr["ac_s"] = cnt["ac"]
                for z in range(2):
                    bank_hist[pr["gu"][z][0]] = ("ac", cnt["ac"] - 1 + z)
                cnt["dv"] += 2
                pr["dv_a"] = cnt["dv"]
                for z in range(2):
                    bank_hist[pr["uu"][z][0]] = ("dv", cnt["dv"] - 1 + z)
                ep["pairs"].append(pr)
            for n in range(4):
                for t2 in range(NTH):
                    f = {}
                    f["bu"], f["bu_w"] = use_bank()
                    cnt["mm"] += 1
                    f["mm"] = cnt["mm"]
                    cnt["dv"] += 1
                    f["dv"] = cnt["dv"]
                    bank_hist[f["bu"]] = ("dv", cnt["dv"])
                    ep["m2"].append(f)
            plan.setdefault("exp", []).append(ep)
        for t2 in range(NTH):
            f = {}
            cnt["gl"] += 2
            f["gl"] = cnt["gl"]
            cnt["ac"] += 1
            f["ac"] = cnt["ac"]
            cnt["dv"] += 2
            f["dv"] = cnt["dv"]
            cnt["st"] += 1
            f["st"] = cnt["st"]
            plan["tiles"] = plan.get("tiles", []) + [f]
        P.append(plan)

    with nc.Block() as b:
        @b.sync
        def _(s):
            s.wait_ge(cx.p0sem.h, cx.p0sem.n)
            for h in range(2):
                plan = P[h]
                if h > 0:
                    s.wait_ge(mm.h, P[0]["exp"][-1]["pairs"][-1]["mm_u"])
                for bb in range(NTH):
                    ld.inc(s.dma_start(
                        bass.AP(xn_h, bb * 128, [[KCD * HT, 128], [HT, KCD], [1, 128]]),
                        io["sc_xn1"].ap()[ts(h * NTH + bb, 128), :],
                        transpose=True), 16)
                for e in range(NEXP):
                    ep = plan["exp"][e]
                    if e > 0:
                        s.wait_ge(mm.h, plan["exp"][e - 1]["m2"][-1]["mm"])
                    elif h > 0:
                        s.wait_ge(mm.h, P[0]["exp"][-1]["m2"][-1]["mm"])
                    for hc in range(KCD):
                        ld.inc(s.dma_start(
                            bass.AP(w2t, hc * 128, [[MIC * D, 128], [D, MIC], [1, 128]]),
                            io["sc_w2"].ap()[e][ts(hc, 128), :], transpose=True), 16)
                    for j in range(MIC):
                        pr = ep["pairs"][j]
                        if j >= 2:
                            s.wait_ge(mm.h, ep["pairs"][j - 2]["mm_u"])
                        for which, oc in ((0, j), (1, j + MIC)):
                            ld.inc(s.dma_start(
                                bass.AP(w1t[(j % 2) * 2 + which], 0,
                                        [[KCD * 128, 128], [128, KCD], [1, 128]]),
                                io["sc_w1"].ap()[e][ts(oc, 128), :],
                                transpose=True), 16)

        @b.gpsimd
        def _(g):
            for h in range(2):
                plan = P[h]
                for t2 in range(NTH):
                    f = plan["tiles"][t2]
                    it = h * NTH + t2
                    if it >= 1:
                        prev = P[(it - 1) // NTH]["tiles"][(it - 1) % NTH]
                        g.wait_ge(dv.h, prev["dv"])
                        g.wait_ge(st.h, 16 * prev["st"])
                    g.dma_start(xt8[:], io["sc_x1"].ap()[ts(it, 128), :]) \
                        .then_inc(gl.h, 16)
                    g.dma_start(st8[:], io["sc_s"].ap()[ts(it, 128), :]) \
                        .then_inc(gl.h, 16)
                    g.wait_ge(dv.h, f["dv"])
                    g.dma_start(io["out"].ap()[ts(it, 128), :], st8[:]) \
                        .then_inc(st.h, 16)
            g.wait_ge(st.h, 16 * cnt["st"])
        st.n = 16 * cnt["st"]
        gl.n = 16 * cnt["gl"]

        @b.tensor
        def _(t):
            for h in range(2):
                plan = P[h]
                t.wait_ge(ld.h, 16 * plan["ld_xn"])
                for e in range(NEXP):
                    ep = plan["exp"][e]
                    for j in range(MIC):
                        pr = ep["pairs"][j]
                        t.wait_ge(ld.h, 16 * pr["ld_w1"])
                        for which, banks, mmord in ((0, pr["gu"], pr["mm_g"]),
                                                    (1, pr["uu"], pr["mm_u"])):
                            for z in range(2):
                                bid, wt = banks[z]
                                if wt is not None:
                                    t.wait_ge(dv.h if wt[0] == "dv" else ac.h, wt[1])
                                for kc in range(KCD):
                                    ins = t.matmul(
                                        ps[bid % 8][:],
                                        w1t[(j % 2) * 2 + which][:, ts(kc, 128)],
                                        xn_h[:, kc * HT + z * 512: kc * HT + (z + 1) * 512],
                                        start=(kc == 0), stop=(kc == KCD - 1))
                            mm.inc(ins)
                    t.wait_ge(dv.h, ep["pairs"][-1]["dv_a"])
                    t.wait_ge(ld.h, 16 * ep["ld_w2"])
                    for n in range(4):
                        for t2 in range(NTH):
                            f = ep["m2"][n * NTH + t2]
                            if f["bu_w"] is not None:
                                wt = f["bu_w"]
                                t.wait_ge(dv.h if wt[0] == "dv" else ac.h, wt[1])
                            for ic in range(MIC):
                                ins = t.matmul(
                                    ps[f["bu"] % 8][:],
                                    act_h[:, ic * HT + t2 * 128: ic * HT + (t2 + 1) * 128],
                                    w2t[:, ic * D + n * 512: ic * D + (n + 1) * 512],
                                    start=(ic == 0), stop=(ic == MIC - 1))
                            mm.inc(ins)

        @b.scalar
        def _(sc):
            for h in range(2):
                plan = P[h]
                for e in range(NEXP):
                    ep = plan["exp"][e]
                    for j in range(MIC):
                        pr = ep["pairs"][j]
                        sc.wait_ge(mm.h, pr["mm_g"])
                        if j >= 1:
                            sc.wait_ge(dv.h, ep["pairs"][j - 1]["dv_a"])
                        elif e >= 1:
                            sc.wait_ge(dv.h, plan["exp"][e - 1]["pairs"][-1]["dv_a"])
                        elif h >= 1:
                            sc.wait_ge(dv.h, P[0]["exp"][-1]["pairs"][-1]["dv_a"])
                        for z in range(2):
                            sc.activation(gs8[:, ts(z, 512)],
                                          ps[pr["gu"][z][0] % 8][:], AF.Silu) \
                                .then_inc(ac.h, 1)
                for t2 in range(NTH):
                    f = plan["tiles"][t2]
                    sc.wait_ge(dv.h, plan["exp"][-1]["m2"][3 * NTH + t2]["dv"])
                    sc.activation(acc[:, ts(t2, D)], acc[:, ts(t2, D)], AF.Silu) \
                        .then_inc(ac.h, 1)

        @b.vector
        def _(v):
            for h in range(2):
                plan = P[h]
                if h > 0:
                    v.wait_ge(ac.h, P[0]["tiles"][-1]["ac"])
                v.memset(acc[:], 0.0)
                for e in range(NEXP):
                    ep = plan["exp"][e]
                    for j in range(MIC):
                        pr = ep["pairs"][j]
                        v.wait_ge(mm.h, pr["mm_u"])
                        v.wait_ge(ac.h, pr["ac_s"])
                        if j == 0 and (e > 0 or h > 0):
                            prev = plan["exp"][e - 1] if e > 0 else P[0]["exp"][-1]
                            v.wait_ge(mm.h, prev["m2"][-1]["mm"])
                        for z in range(2):
                            v.tensor_tensor(
                                act_h[:, j * HT + z * 512: j * HT + (z + 1) * 512],
                                ps[pr["uu"][z][0] % 8][:],
                                gs8[:, ts(z, 512)], ALU.mult).then_inc(dv.h, 1)
                    for n in range(4):
                        for t2 in range(NTH):
                            f = ep["m2"][n * NTH + t2]
                            v.wait_ge(mm.h, f["mm"])
                            if h > 0 and e == 0 and n == 0:
                                pass
                            v.scalar_tensor_tensor(
                                acc[:, t2 * D + n * 512: t2 * D + (n + 1) * 512],
                                ps[f["bu"] % 8][:],
                                combine_sb[:, (h * NTH + t2) * NEXP + e:
                                           (h * NTH + t2) * NEXP + e + 1],
                                acc[:, t2 * D + n * 512: t2 * D + (n + 1) * 512],
                                ALU.mult, ALU.add).then_inc(dv.h, 1)
                for t2 in range(NTH):
                    f = plan["tiles"][t2]
                    v.wait_ge(ac.h, f["ac"])
                    v.wait_ge(gl.h, 16 * f["gl"])
                    if h * NTH + t2 >= 1:
                        it = h * NTH + t2
                        v.wait_ge(st.h, 16 * P[(it - 1) // NTH]["tiles"][(it - 1) % NTH]["st"])
                    v.tensor_tensor(st8[:], st8[:], acc[:, ts(t2, D)], ALU.mult) \
                        .then_inc(dv.h, 1)
                    v.tensor_tensor(st8[:], st8[:], xt8[:], ALU.add).then_inc(dv.h, 1)
        dv.n = cnt["dv"]
        ac.n = cnt["ac"]



def declare_io(nc, dump):
    io = {}

    def inp(name, shape, dtype):
        io[name] = nc.dram_tensor(name, shape, dtype, kind="ExternalInput")

    def scratch(name, shape, dtype):
        kind = "ExternalOutput" if name in dump else "Internal"
        io[name] = nc.dram_tensor(name, shape, dtype, kind=kind)

    inp("x", [T, D], F32)
    inp("L0_attn_Wq", [D, D], F32)
    inp("L0_attn_Wk", [D, 512], F32)
    inp("L0_attn_Wv", [D, 512], F32)
    inp("L0_attn_Wo", [D, D], F32)
    inp("L0_ffn_gate_w", [INTER, D], F32)
    inp("L0_ffn_up_w", [INTER, D], F32)
    inp("L0_ffn_down_w", [D, INTER], F32)
    inp("c_gate_s_t", [128, 44], F32)
    inp("c_up_s_t", [128, 44], F32)
    inp("c_down_s_t", [128, 16], F32)
    inp("L1_attn_Wq", [NL1, D, D], BF16)
    inp("L1_attn_Wk", [NL1, D, D], BF16)
    inp("L1_attn_Wv", [NL1, D, D], BF16)
    inp("L1_attn_Wo", [NL1, D, D], BF16)
    inp("c_bq_t", [128, NL1 * KCD], F32)
    inp("c_bk_t", [128, NL1 * KCD], F32)
    inp("c_bo_t", [128, NL1 * KCD], F32)
    inp("c_bv_row", [1, NL1 * D], BF16)
    inp("c_bo_row", [1, D], BF16)
    inp("c_gate_wT", [D, NEXP], F32)
    inp("L1_ffn_w1", [NEXP, 2 * MI, D], F32)
    inp("L1_ffn_w2", [NEXP, D, MI], F32)
    inp("c_cos", [T, 64], F32)
    inp("c_sin", [T, 64], F32)
    inp("c_ones", [1, 128], BF16)
    inp("c_causal", [128, 128], F32)

    scratch("sc_xn", [T, D], BF16)
    scratch("sc_x1", [T, D], F32)
    scratch("sc_xn1", [T, D], BF16)
    scratch("sc_xlo", [T, D], BF16)
    scratch("sc_s", [T, D], F32)
    scratch("sc_gate_w", [INTER, D], BF16)
    scratch("sc_up_w", [INTER, D], BF16)
    scratch("sc_down_w", [D, INTER], BF16)
    scratch("sc_w1", [NEXP, 2 * MI, D], BF16)
    scratch("sc_w2", [NEXP, D, MI], BF16)
    scratch("sc_qf", [D, T], BF16)
    scratch("sc_kf", [D, T], BF16)
    scratch("sc_xa0", [D, T], BF16)
    scratch("sc_xa1", [D, T], BF16)
    scratch("sc_v", [T, D], BF16)
    scratch("sc_attn", [T, D], BF16)
    io["out"] = nc.dram_tensor("out", [T, D], F32, kind="ExternalOutput")
    return io


def build_program(dump=(), max_phase=99):
    nc = bass.Bass("TRN2", target_bir_lowering=False)
    dump = set(dump)
    io = declare_io(nc, dump)
    sb_shapes = {"sb_q_fm": [128, KCD * T], "sb_k_fm": [128, 8 * T],
                 "sb_v_tm": [128, NT * 512], "sb_attn_fm": [128, KCD * T]}
    for nm in dump:
        if nm.startswith("sb_") and nm in sb_shapes:
            io[nm] = nc.dram_tensor(nm, sb_shapes[nm], BF16, kind="ExternalOutput")
    with ExitStack() as es:
        cx = Cx(nc, es)
        sb_map = {}

        cx.cast_jobs, cx.mark_l0 = build_cast_jobs(io)
        cx.p0sem = cx.sem("p0cast")
        if max_phase >= 1:
            with cx.phase(), nc.named_scope("p1"):
                p1_ln0(cx, io)
        if max_phase >= 2:
            with cx.phase():
                l0scope = cx.cur
                attn_fm = None
                with cx.phase():
                    q_fm = cx.sbuf("q_fm", [128, KCD * T], BF16)
                    with nc.named_scope("p2"):
                        k_fm, v_tm = p2_l0_qkv(cx, io, q_fm)
                    sb_map.update({"sb_q_fm": q_fm, "sb_k_fm": k_fm,
                                   "sb_v_tm": v_tm})
                    if max_phase >= 3:
                        attn_fm = cx.sbuf("attn_fm", [128, KCD * T], BF16,
                                          scope=l0scope, side="right")
                        def _bg(g):
                            for dst_ap, src_ap in cx.cast_jobs[:cx.mark_l0]:
                                cx.p0sem.inc(g.dma_start(dst_ap, src_ap), 16)
                        with cx.phase(), nc.named_scope("p3"):
                            p3dbg = p3_l0_attn(cx, io, q_fm, k_fm, v_tm, attn_fm,
                                               bg=_bg)
                            _emit_dumps(cx, io, dump, p3dbg)
                    _emit_dumps(cx, io, dump, sb_map)
                if max_phase >= 4:
                    with cx.phase(), nc.named_scope("p4"):
                        p4_l0_proj(cx, io, attn_fm)
                if attn_fm is not None:
                    _emit_dumps(cx, io, dump, {"sb_attn_fm": attn_fm})
        if max_phase >= 5:
            with cx.phase(), nc.named_scope("p5"):
                p5_l0_ffn(cx, io)
        if max_phase >= 6:
            with cx.phase(), nc.named_scope("p6"):
                p6_l1_attn(cx, io)
        if max_phase >= 7:
            combine_sb = cx.sbuf("combine", [128, NT * NEXP], F32, persistent=True)
            with cx.phase(), nc.named_scope("p7"):
                p7_gate(cx, io, combine_sb)
            if max_phase >= 8:
                with cx.phase(), nc.named_scope("p8"):
                    p8_moe(cx, io, combine_sb)
    return nc


def _emit_dumps(cx, io, dump, sb_map):
    todo = sorted(nm for nm in dump if nm in sb_map and sb_map[nm] is not None)
    if not todo:
        return
    nc = cx.nc
    dmp = cx.sem("dmp" + todo[0])
    for nm in todo:
        if nm not in io:
            t = sb_map[nm]
            io[nm] = nc.dram_tensor(nm, list(t.shape), t.dtype,
                                    kind="ExternalOutput")
    with nc.Block() as b:
        @b.sync
        def _(s):
            for nm in todo:
                dmp.inc(s.dma_start(io[nm].ap(), sb_map[nm][:]), 16)
            s.wait_ge(dmp.h, dmp.n)


def _host_constants():
    theta = 1.0 / (10000.0 ** (np.arange(0, HD, 2, dtype=np.float64) / HD))
    pos = np.arange(T, dtype=np.float64)
    fr = np.outer(pos, theta)
    cos1 = np.cos(fr)
    sin1 = np.sin(fr)
    cos = np.concatenate([cos1, cos1], axis=1).astype(np.float32)
    sinf = np.concatenate([-sin1, sin1], axis=1).astype(np.float32)
    ones = np.ones((1, 128), dtype=ml_dtypes.bfloat16)
    causal = np.where(np.arange(128)[:, None] >= np.arange(128)[None, :],
                      np.float32(0), np.float32(-1e30))
    return cos, sinf, ones, causal


def make_inmaps(inputs, dump=()):
    cos, sinf, ones, causal = _host_constants()
    base = {}
    base["L0_attn_Wq"] = np.asarray(inputs["L0_attn_Wq"])
    base["L0_attn_Wk"] = np.asarray(inputs["L0_attn_Wk"])
    base["L0_attn_Wv"] = np.asarray(inputs["L0_attn_Wv"])
    base["L0_attn_Wo"] = np.asarray(inputs["L0_attn_Wo"])
    base["L0_ffn_gate_w"] = np.asarray(inputs["L0_ffn_gate_w"])
    base["L0_ffn_up_w"] = np.asarray(inputs["L0_ffn_up_w"])
    base["L0_ffn_down_w"] = np.asarray(inputs["L0_ffn_down_w"])
    base["c_gate_s_t"] = np.ascontiguousarray(
        np.asarray(inputs["L0_ffn_gate_s"]).reshape(44, 128).T)
    base["c_up_s_t"] = np.ascontiguousarray(
        np.asarray(inputs["L0_ffn_up_s"]).reshape(44, 128).T)
    base["c_down_s_t"] = np.ascontiguousarray(
        np.asarray(inputs["L0_ffn_down_s"]).reshape(16, 128).T)
    for nm in ["L1_attn_Wq", "L1_attn_Wk", "L1_attn_Wv", "L1_attn_Wo"]:
        base[nm] = np.asarray(inputs[nm])
    for nm, key in [("c_bq_t", "L1_attn_bq"), ("c_bk_t", "L1_attn_bk"),
                    ("c_bo_t", "L1_attn_bo")]:
        b = np.asarray(inputs[key]).astype(np.float32)
        base[nm] = np.ascontiguousarray(
            b.reshape(NL1, KCD, 128).transpose(2, 0, 1).reshape(128, NL1 * KCD))
    base["c_bv_row"] = np.ascontiguousarray(
        np.asarray(inputs["L1_attn_bv"]).reshape(1, NL1 * D))
    base["c_bo_row"] = np.ascontiguousarray(
        np.asarray(inputs["L1_attn_bo"])[NL1 - 1].reshape(1, D))
    base["c_gate_wT"] = np.ascontiguousarray(
        np.asarray(inputs["L1_ffn_gate_w"]).T)
    base["L1_ffn_w1"] = np.asarray(inputs["L1_ffn_w1"])
    base["L1_ffn_w2"] = np.asarray(inputs["L1_ffn_w2"])
    base["c_cos"] = cos
    base["c_sin"] = sinf
    base["c_ones"] = ones
    base["c_causal"] = causal
    x = np.asarray(inputs["L0_attn_x"])
    maps = []
    for b in range(8):
        m = dict(base)
        m["x"] = np.ascontiguousarray(x[b])
        maps.append(m)
    return maps


def kernel(**inputs):
    nc = build_program(dump=DUMP, max_phase=MAX_PHASE)
    maps = make_inmaps(inputs, dump=DUMP)
    res = run_bass_kernel_spmd(nc, maps, core_ids=list(range(8)))
    return np.stack([np.asarray(res.results[i]["out"]) for i in range(8)])


# version 04 (final)
# speedup vs baseline: 1.3221x; 1.2162x over previous
import math
from contextlib import ExitStack, contextmanager

import numpy as np
import ml_dtypes
import concourse.bass as bass
import concourse.mybir as mybir
from concourse.bass_utils import run_bass_kernel_spmd

F32 = mybir.dt.float32
BF16 = mybir.dt.bfloat16
AF = mybir.ActivationFunctionType
ALU = mybir.AluOpType
AX = mybir.AxisListType.X
ts = bass.ts

T = 2048
D = 2048
NT = T // 128
KCD = D // 128
HEADS, KVH, HD = 32, 8, 64
INTER = 5632
NL1 = 4
NEXP, MI = 8, 1408
MIC = MI // 128

DUMP = ()
MAX_PHASE = 99

Q_PERM = [(2 * c, 2 * c + 1) for c in range(16)]


class Sem:
    __slots__ = ("h", "n")

    def __init__(self, h):
        self.h = h
        self.n = 0

    def inc(self, ins, k=1):
        ins.then_inc(self.h, k)
        self.n += k
        return self.n


class Cx:
    def __init__(self, nc, es):
        self.nc = nc
        self.es = es
        self.cur = es

    @contextmanager
    def phase(self):
        with ExitStack() as ps:
            old = self.cur
            self.cur = ps
            try:
                yield
            finally:
                self.cur = old

    def sem(self, name):
        return Sem(self.es.enter_context(self.nc.semaphore(name)))

    def sbuf(self, name, shape, dtype, persistent=False, scope=None, side=None):
        if scope is None:
            scope = self.es if persistent else self.cur
        return scope.enter_context(self.nc.sbuf_tensor(name, shape, dtype, side=side))

    def psum(self, name, shape, dtype=F32, persistent=False):
        scope = self.es if persistent else self.cur
        return scope.enter_context(self.nc.psum_tensor(name, shape, dtype))



def build_cast_jobs(io):
    jobs = []

    def chunk(src, dst, rows, step):
        for r0 in range(0, rows, step):
            r1 = min(rows, r0 + step)
            jobs.append((dst[r0:r1, :], src[r0:r1, :]))

    chunk(io["L0_ffn_gate_w"].ap(), io["sc_gate_w"].ap(), INTER, 1024)
    chunk(io["L0_ffn_up_w"].ap(), io["sc_up_w"].ap(), INTER, 1024)
    mark_l0 = len(jobs)
    for e in range(NEXP):
        chunk(io["L1_ffn_w1"].ap()[e], io["sc_w1"].ap()[e], 2 * MI, 1024)
        chunk(io["L1_ffn_w2"].ap()[e], io["sc_w2"].ap()[e], D, 1024)
    return jobs, mark_l0




def p1_ln0(cx, io):
    nc = cx.nc
    ld = cx.sem("p1ld")
    dv = cx.sem("p1dv")
    ac = cx.sem("p1ac")
    st = cx.sem("p1st")
    dl = cx.sem("p1dl")
    dst_ = cx.sem("p1dst")
    f0 = cx.sbuf("p1f0", [128, INTER], F32)
    fb = [cx.sbuf(f"p1fb{i}", [128, INTER], BF16) for i in range(2)]
    dsc = cx.sbuf("p1dsc", [128, 16], F32)
    xs = [cx.sbuf(f"p1x{i}", [128, D], F32) for i in range(2)]
    sq = cx.sbuf("p1sq", [128, D], F32)
    ssum = cx.sbuf("p1ssum", [128, 1], F32)
    ssq = cx.sbuf("p1ssq", [128, 1], F32)
    nmean = cx.sbuf("p1nmean", [128, 1], F32)
    var = cx.sbuf("p1var", [128, 1], F32)
    rstd = cx.sbuf("p1rstd", [128, 1], F32)
    tmp = cx.sbuf("p1tmp", [128, D], F32)
    xnbs = [cx.sbuf(f"p1xnb{i}", [128, D], BF16) for i in range(2)]

    with nc.Block() as b:
        @b.sync
        def _(s):
            for i in range(NT):
                if i >= 2:
                    s.wait_ge(dv.h, 7 * (i - 1))
                ld.inc(s.dma_start(xs[i % 2][:], io["x"].ap()[ts(i, 128), :]), 16)

        @b.scalar
        def _(sc):
            for i in range(NT):
                sc.wait_ge(ld.h, 16 * (i + 1))
                sc.activation(sq[:], xs[i % 2][:], AF.Square, accum_out=ssq[:]) \
                    .then_inc(ac.h, 1)
                sc.wait_ge(dv.h, 7 * i + 6)
                sc.activation(rstd[:], rstd[:], AF.Sqrt).then_inc(ac.h, 1)
                sc.wait_ge(dv.h, 7 * (i + 1))
        ac.n = 2 * NT

        @b.vector
        def _(v):
            for i in range(NT):
                x = xs[i % 2]
                v.wait_ge(ld.h, 16 * (i + 1))
                v.reduce_sum(ssum[:], x[:], AX).then_inc(dv.h, 1)
                v.wait_ge(dv.h, 7 * i + 1)
                v.wait_ge(ac.h, 2 * i + 1)
                v.tensor_scalar_mul(nmean[:], ssum[:], -1.0 / D).then_inc(dv.h, 1)
                v.wait_ge(dv.h, 7 * i + 2)
                v.tensor_scalar_mul(var[:], ssq[:], 1.0 / D)
                v.tensor_scalar(ssq[:], nmean[:], nmean[:], None, ALU.mult) \
                    .then_inc(dv.h, 1)
                v.wait_ge(dv.h, 7 * i + 3)
                v.tensor_tensor(var[:], var[:], ssq[:], ALU.subtract).then_inc(dv.h, 1)
                v.wait_ge(dv.h, 7 * i + 4)
                v.tensor_scalar(var[:], var[:], 1e-5, None, ALU.add).then_inc(dv.h, 1)
                v.wait_ge(dv.h, 7 * i + 5)
                v.reciprocal(rstd[:], var[:]).then_inc(dv.h, 1)
                v.wait_ge(ac.h, 2 * i + 2)
                if i >= 2:
                    v.wait_ge(st.h, 16 * (i - 1))
                v.tensor_scalar(tmp[:], x[:], nmean[:], None, ALU.add)
                v.tensor_scalar(xnbs[i % 2][:], tmp[:], rstd[:], None, ALU.mult) \
                    .then_inc(dv.h, 1)
            for i in range(NT):
                v.wait_ge(dl.h, 16 * (i + 2))
                if i >= 2:
                    v.wait_ge(dst_.h, 16 * (i - 1))
                v.tensor_scalar_mul(fb[i % 2][:], f0[:], dsc[:, i:i + 1]) \
                    .then_inc(dv.h, 1)
        dv.n = 7 * NT + NT

        @b.gpsimd
        def _(g):
            g.dma_start(dsc[:], io["c_down_s_t"].ap()).then_inc(dl.h, 16)
            g.dma_start(f0[:], io["L0_ffn_down_w"].ap()[0:128, :]).then_inc(dl.h, 16)
            for i in range(NT):
                g.wait_ge(dv.h, 7 * (i + 1))
                st.inc(g.dma_start(io["sc_xn"].ap()[ts(i, 128), :], xnbs[i % 2][:]), 16)
            for i in range(NT):
                g.wait_ge(dv.h, 7 * NT + i + 1)
                if i + 1 < NT:
                    g.dma_start(f0[:], io["L0_ffn_down_w"].ap()[ts(i + 1, 128), :]) \
                        .then_inc(dl.h, 16)
                g.dma_start(io["sc_down_w"].ap()[ts(i, 128), :], fb[i % 2][:]) \
                    .then_inc(dst_.h, 16)
            g.wait_ge(st.h, st.n)
            g.wait_ge(dst_.h, 16 * NT)



def p2_l0_qkv(cx, io, q_fm):
    nc = cx.nc

    def load_xn_fm(s, ld, xn_fm):
        for bb in range(NT):
            ld.inc(s.dma_start(
                bass.AP(xn_fm, bb * 128, [[KCD * T, 128], [T, KCD], [1, 128]]),
                io["sc_xn"].ap()[ts(bb, 128), :],
                transpose=True), 16)

    wq = io["L0_attn_Wq"].ap()

    def q_cols_ap(nb):
        return (wq[:, nb * 512:(nb + 1) * 512]
                .rearrange("(c p) n -> p c n", p=128))

    with cx.phase():
        ld = cx.sem("p2ald")
        wl = cx.sem("p2awl")
        mm = cx.sem("p2amm")
        ev = cx.sem("p2aev")
        rp = cx.sem("p2arp")
        tr = cx.sem("p2atr")
        xn_fm = cx.sbuf("p2xnfm", [128, KCD * T], BF16)
        cosb = cx.sbuf("p2cos", [128, NT * 64], F32)
        sinb = cx.sbuf("p2sin", [128, NT * 64], F32)
        wbuf = cx.sbuf("p2wq", [128, KCD * 512], BF16)
        qf = [cx.sbuf(f"p2qf{i}", [128, 512], F32) for i in range(2)]
        qsw = cx.sbuf("p2qsw", [128, 512], F32)
        qb = [cx.sbuf(f"p2qb{i}", [128, 512], BF16) for i in range(2)]
        ps = [cx.psum(f"p2ps{i}", [128, 512]) for i in range(8)]

        with nc.Block() as b:
            @b.sync
            def _(s):
                load_xn_fm(s, ld, xn_fm)
                ld.inc(s.dma_start(
                    cosb[:].rearrange("p (i n) -> p i n", i=NT),
                    io["c_cos"].ap().rearrange("(i p) n -> p i n", p=128)), 16)
                ld.inc(s.dma_start(
                    sinb[:].rearrange("p (i n) -> p i n", i=NT),
                    io["c_sin"].ap().rearrange("(i p) n -> p i n", p=128)), 16)
                for nb in range(4):
                    for i in range(NT):
                        it = nb * NT + i
                        s.wait_ge(rp.h, it + 1)
                        dst = bass.AP(q_fm, 4 * nb * T + i * 128,
                                      [[KCD * T, 128], [T, 4], [1, 128]])
                        tr.inc(s.dma_start(dst, qb[it % 2][:], transpose=True), 16)

            @b.gpsimd
            def _(g):
                for nb in range(4):
                    if nb >= 1:
                        g.wait_ge(mm.h, 16 * nb)
                    wl.inc(g.dma_start(
                        wbuf[:].rearrange("p (c n) -> p c n", c=KCD),
                        q_cols_ap(nb)), 16)
                g.wait_ge(tr.h, 16 * 4 * NT)

            @b.tensor
            def _(t):
                t.wait_ge(ld.h, 16 * (KCD + 2))
                for nb in range(4):
                    t.wait_ge(wl.h, 16 * (nb + 1))
                    for i in range(NT):
                        it = nb * NT + i
                        bank = it % 8
                        if it >= 8:
                            t.wait_ge(ev.h, it - 8 + 1)
                        for kc in range(KCD):
                            ins = t.matmul(
                                ps[bank][:],
                                xn_fm[:, kc * T + i * 128: kc * T + (i + 1) * 128],
                                wbuf[:, ts(kc, 512)],
                                start=(kc == 0), stop=(kc == KCD - 1))
                        mm.inc(ins)

            @b.scalar
            def _(sc):
                for it in range(4 * NT):
                    sc.wait_ge(mm.h, it + 1)
                    sc.wait_ge(rp.h, max(0, it - 1))
                    ev.inc(sc.activation(qf[it % 2][:], ps[it % 8][:], AF.Copy,
                                         scale=0.125))

            @b.vector
            def _(v):
                for nb in range(4):
                    for i in range(NT):
                        it = nb * NT + i
                        v.wait_ge(ev.h, it + 1)
                        if it >= 2:
                            v.wait_ge(tr.h, 16 * (it - 1))
                        x = qf[it % 2]
                        co = bass.AP(cosb, i * 64, [[NT * 64, 128], [0, 8], [1, 64]])
                        si = bass.AP(sinb, i * 64, [[NT * 64, 128], [0, 8], [1, 64]])
                        sw = qsw[:].rearrange("p (h two j) -> p h two j", two=2, j=32)
                        xv = x[:].rearrange("p (h two j) -> p h two j", two=2, j=32)
                        v.tensor_copy(sw[:, :, 0, :], xv[:, :, 1, :])
                        v.tensor_copy(sw[:, :, 1, :], xv[:, :, 0, :])
                        x3 = x[:].rearrange("p (h j) -> p h j", j=64)
                        v.tensor_tensor(x3, x3, co, ALU.mult)
                        v.tensor_tensor(qsw[:].rearrange("p (h j) -> p h j", j=64),
                                        qsw[:].rearrange("p (h j) -> p h j", j=64),
                                        si, ALU.mult)
                        rp.inc(v.tensor_tensor(qb[it % 2][:], x[:], qsw[:], ALU.add))

    k_fm = cx.sbuf("k_fm", [128, 8 * T], BF16)
    with cx.phase():
        ld = cx.sem("p2bld")
        wl = cx.sem("p2bwl")
        mm = cx.sem("p2bmm")
        ev = cx.sem("p2bev")
        rp = cx.sem("p2brp")
        tr = cx.sem("p2btr")
        xn_fm = cx.sbuf("p2bxnfm", [128, KCD * T], BF16)
        cosb = cx.sbuf("p2bcos", [128, NT * 64], F32)
        sinb = cx.sbuf("p2bsin", [128, NT * 64], F32)
        wbuf = cx.sbuf("p2bw", [128, KCD * 512], BF16)
        qf = [cx.sbuf(f"p2bqf{i}", [128, 512], F32) for i in range(2)]
        qsw = cx.sbuf("p2bqsw", [128, 512], F32)
        qb = [cx.sbuf(f"p2bqb{i}", [128, 512], BF16) for i in range(2)]
        k2 = [cx.sbuf(f"p2bk2{i}", [128, 1024], BF16) for i in range(2)]
        ps = [cx.psum(f"p2bps{i}", [128, 512]) for i in range(8)]

        with nc.Block() as b:
            @b.sync
            def _(s):
                load_xn_fm(s, ld, xn_fm)
                ld.inc(s.dma_start(
                    cosb[:].rearrange("p (i n) -> p i n", i=NT),
                    io["c_cos"].ap().rearrange("(i p) n -> p i n", p=128)), 16)
                ld.inc(s.dma_start(
                    sinb[:].rearrange("p (i n) -> p i n", i=NT),
                    io["c_sin"].ap().rearrange("(i p) n -> p i n", p=128)), 16)
                for i in range(NT):
                    s.wait_ge(rp.h, 3 * (i + 1))
                    dst = bass.AP(k_fm, i * 128, [[8 * T, 128], [T, 8], [1, 128]])
                    tr.inc(s.dma_start(dst, k2[i % 2][:], transpose=True), 16)

            @b.gpsimd
            def _(g):
                wl.inc(g.dma_start(
                    wbuf[:].rearrange("p (c n) -> p c n", c=KCD),
                    io["L0_attn_Wk"].ap().rearrange("(c p) n -> p c n", p=128)), 16)
                g.wait_ge(tr.h, 16 * NT)

            @b.tensor
            def _(t):
                t.wait_ge(ld.h, 16 * (KCD + 2))
                t.wait_ge(wl.h, 16)
                for i in range(NT):
                    bank = i % 8
                    if i >= 8:
                        t.wait_ge(ev.h, i - 8 + 1)
                    for kc in range(KCD):
                        ins = t.matmul(
                            ps[bank][:],
                            xn_fm[:, kc * T + i * 128: kc * T + (i + 1) * 128],
                            wbuf[:, ts(kc, 512)],
                            start=(kc == 0), stop=(kc == KCD - 1))
                    mm.inc(ins)

            @b.scalar
            def _(sc):
                for i in range(NT):
                    sc.wait_ge(mm.h, i + 1)
                    sc.wait_ge(rp.h, max(0, 3 * (i - 1)))
                    ev.inc(sc.activation(qf[i % 2][:], ps[i % 8][:], AF.Copy))

            @b.vector
            def _(v):
                for i in range(NT):
                    v.wait_ge(ev.h, i + 1)
                    if i >= 2:
                        v.wait_ge(tr.h, 16 * (i - 1))
                    x = qf[i % 2]
                    co = bass.AP(cosb, i * 64, [[NT * 64, 128], [0, 8], [1, 64]])
                    si = bass.AP(sinb, i * 64, [[NT * 64, 128], [0, 8], [1, 64]])
                    sw = qsw[:].rearrange("p (h two j) -> p h two j", two=2, j=32)
                    xv = x[:].rearrange("p (h two j) -> p h two j", two=2, j=32)
                    v.tensor_copy(sw[:, :, 0, :], xv[:, :, 1, :])
                    v.tensor_copy(sw[:, :, 1, :], xv[:, :, 0, :])
                    x3 = x[:].rearrange("p (h j) -> p h j", j=64)
                    v.tensor_tensor(x3, x3, co, ALU.mult)
                    v.tensor_tensor(qsw[:].rearrange("p (h j) -> p h j", j=64),
                                    qsw[:].rearrange("p (h j) -> p h j", j=64),
                                    si, ALU.mult)
                    rp.inc(v.tensor_tensor(qb[i % 2][:], x[:], qsw[:], ALU.add))
                    kk = k2[i % 2][:].rearrange("p (g d x) -> p g d x", d=2, x=64)
                    kv = qb[i % 2][:].rearrange("p (g x) -> p g x", x=64)
                    rp.inc(v.tensor_copy(kk[:, :, 0, :], kv))
                    rp.inc(v.tensor_copy(kk[:, :, 1, :], kv))

    v_tm = cx.sbuf("v_tm", [128, NT * 512], BF16)
    with cx.phase():
        ld = cx.sem("p2cld")
        wl = cx.sem("p2cwl")
        mm = cx.sem("p2cmm")
        ev = cx.sem("p2cev")
        xn_fm = cx.sbuf("p2cxnfm", [128, KCD * T], BF16)
        wbuf = cx.sbuf("p2cw", [128, KCD * 512], BF16)
        ps = [cx.psum(f"p2cps{i}", [128, 512]) for i in range(8)]

        with nc.Block() as b:
            @b.sync
            def _(s):
                load_xn_fm(s, ld, xn_fm)

            @b.gpsimd
            def _(g):
                wl.inc(g.dma_start(
                    wbuf[:].rearrange("p (c n) -> p c n", c=KCD),
                    io["L0_attn_Wv"].ap().rearrange("(c p) n -> p c n", p=128)), 16)

            @b.tensor
            def _(t):
                t.wait_ge(ld.h, 16 * KCD)
                t.wait_ge(wl.h, 16)
                for i in range(NT):
                    bank = i % 8
                    if i >= 8:
                        t.wait_ge(ev.h, i - 8 + 1)
                    for kc in range(KCD):
                        ins = t.matmul(
                            ps[bank][:],
                            xn_fm[:, kc * T + i * 128: kc * T + (i + 1) * 128],
                            wbuf[:, ts(kc, 512)],
                            start=(kc == 0), stop=(kc == KCD - 1))
                    mm.inc(ins)

            @b.scalar
            def _(sc):
                for i in range(NT):
                    sc.wait_ge(mm.h, i + 1)
                    ev.inc(sc.activation(v_tm[:, ts(i, 512)], ps[i % 8][:], AF.Copy))
    return k_fm, v_tm



def p3_l0_attn(cx, io, q_fm, k_fm, v_tm, attn_fm, bg=None):
    nc = cx.nc
    pe = cx.sem("p3pe")
    dv = cx.sem("p3dv")
    ac = cx.sem("p3ac")
    ld = cx.sem("p3ld")
    trs = cx.sem("p3tr")
    mask = cx.sbuf("p3mask", [128, 128], F32)
    probs = [cx.sbuf(f"p3pr{i}", [128, 2048], BF16) for i in range(3)]
    probsT = [cx.sbuf(f"p3pt{i}", [128, 2048], BF16) for i in range(3)]
    rs = [cx.sbuf(f"p3rs{i}", [128, 1], F32) for i in range(4)]
    rinv = [cx.sbuf(f"p3ri{i}", [128, 1], F32) for i in range(4)]
    nmax = [cx.sbuf(f"p3nm{i}", [128, 1], F32) for i in range(4)]
    scp = cx.psum("p3scp", [128, 4096])

    iters = [(h, qt) for h in range(HEADS) for qt in range(NT)]
    NIT = len(iters)

    rel = [0] * 8
    pos = [0]
    pvp = [0]
    pe_n = [0]
    dv_n = [0]
    ac_n = [0]

    def alloc(C, pv=False):
        if pv:
            r = 6 + (pvp[0] % 2)
            pvp[0] += 1
            return r, rel[r]
        if pos[0] + C > 6:
            pos[0] = 0
        r = pos[0]
        pos[0] += C
        return r, max(rel[r:r + C])

    P = []
    for i, (h, qt) in enumerate(iters):
        L = (qt + 1) * 128
        C = (L + 511) // 512
        p = {"h": h, "qt": qt, "L": L, "C": C}
        p["rb"], p["rb_w"] = alloc(C)
        p["pb"], p["pb_w"] = alloc(1, pv=True)
        pe_n[0] += 1
        p["pe_s"] = pe_n[0]
        pe_n[0] += 1
        p["pe_v"] = pe_n[0]
        dv_n[0] += 2
        p["dv_max"] = dv_n[0]
        dv_n[0] += 2
        p["dv_norm"] = dv_n[0]
        ac_n[0] += 1
        p["ac_exp"] = ac_n[0]
        ac_n[0] += 1
        p["ac_ev"] = ac_n[0]
        for b in range(p["rb"], p["rb"] + C):
            rel[b] = p["ac_exp"]
        rel[p["pb"]] = p["ac_ev"]
        P.append(p)
    c = 0
    for t in range(NIT + 4):
        if t - 4 >= 0:
            c += 1
            P[t - 4]["pe_v"] = c
        if t < NIT:
            c += 1
            P[t]["pe_s"] = c
    c = 0
    for t in range(NIT + 5):
        if 0 <= t - 5 < NIT:
            c += 1
            P[t - 5]["ac_ev"] = c
        if 0 <= t - 1 < NIT:
            c += 1
            P[t - 1]["ac_exp"] = c
    rel = [0] * 8
    pos = [0]
    pvp[0] = 0
    for i, p in enumerate(P):
        rb, w = alloc(p["C"])
        assert rb == p["rb"]
        p["rb_w"] = w
        pb, w = alloc(1, pv=True)
        assert pb == p["pb"]
        p["pb_w"] = w
        for b in range(rb, rb + p["C"]):
            rel[b] = p["ac_exp"]
        rel[pb] = p["ac_ev"]
    c = 0
    for t in range(NIT + 2):
        if t < NIT:
            c += 2
            P[t]["dv_max"] = c
        if t - 2 >= 0:
            c += 2
            P[t - 2]["dv_norm"] = c

    def sc_of(p):
        return scp[:, p["rb"] * 512: p["rb"] * 512 + p["L"]]

    def pv_of(p, base):
        return scp[base:base + 64, p["pb"] * 512: p["pb"] * 512 + 128]

    def attn_slice(p, base):
        h, qt = p["h"], p["qt"]
        return attn_fm[base:base + 64,
                       (h // 2) * T + qt * 128:(h // 2) * T + (qt + 1) * 128]

    with nc.Block() as b:
        @b.tensor
        def _(t):
            t.wait_ge(ld.h, 16)
            for tk in range(NIT + 4):
                if tk - 4 >= 0:
                    p = P[tk - 4]
                    h, qt, L = p["h"], p["qt"], p["L"]
                    base = (h % 2) * 64
                    g = h // 4
                    t.wait_ge(ac.h, p["pb_w"])
                    t.wait_ge(trs.h, 16 * (tk - 3))
                    ins = None
                    for kc in range(L // 128):
                        ins = t.matmul(
                            pv_of(p, base),
                            v_tm[:, kc * 512 + g * 64: kc * 512 + (g + 1) * 64],
                            probsT[(tk - 4) % 3][:, kc * 128:(kc + 1) * 128],
                            start=(kc == 0), stop=(kc == L // 128 - 1),
                            tile_position=(0, base))
                    ins.then_inc(pe.h, 1)
                if tk < NIT:
                    p = P[tk]
                    h, qt, L = p["h"], p["qt"], p["L"]
                    base = (h % 2) * 64
                    g = h // 4
                    t.wait_ge(ac.h, p["rb_w"])
                    lhs = q_fm[base:base + 64, (h // 2) * T + qt * 128:
                               (h // 2) * T + (qt + 1) * 128]
                    ins = None
                    for n0 in range(0, L, 512):
                        w = min(512, L - n0)
                        ins = t.matmul(
                            scp[:, p["rb"] * 512 + n0: p["rb"] * 512 + n0 + w],
                            lhs, k_fm[base:base + 64, g * T + n0: g * T + n0 + w],
                            start=True, stop=True, tile_position=(base, 0))
                    ins.then_inc(pe.h, 1)

        @b.vector
        def _(v):
            v.wait_ge(ld.h, 16)
            for tk in range(NIT + 2):
                if tk < NIT:
                    p = P[tk]
                    L = p["L"]
                    dcol = p["rb"] * 512 + L - 128
                    v.wait_ge(pe.h, p["pe_s"])
                    if tk >= 4:
                        v.wait_ge(ac.h, P[tk - 4]["ac_exp"])
                    v.tensor_tensor(scp[:, dcol:dcol + 128],
                                    scp[:, dcol:dcol + 128],
                                    mask[:], ALU.add).then_inc(dv.h, 1)
                    v.wait_ge(dv.h, p["dv_max"] - 1)
                    v.reduce_max(nmax[tk % 4][:], sc_of(p), AX,
                                 negate=True).then_inc(dv.h, 1)
                if tk - 2 >= 0:
                    p = P[tk - 2]
                    v.wait_ge(ac.h, p["ac_exp"])
                    v.reciprocal(rinv[(tk - 2) % 4][:], rs[(tk - 2) % 4][:]) \
                        .then_inc(dv.h, 1)
                    v.wait_ge(dv.h, p["dv_norm"] - 1)
                    v.tensor_scalar_mul(probs[(tk - 2) % 3][:, 0:p["L"]],
                                        probs[(tk - 2) % 3][:, 0:p["L"]],
                                        rinv[(tk - 2) % 4][:]).then_inc(dv.h, 1)

        @b.scalar
        def _(sc):
            for tk in range(NIT + 5):
                if 0 <= tk - 5 < NIT:
                    p = P[tk - 5]
                    base = (p["h"] % 2) * 64
                    sc.wait_ge(pe.h, p["pe_v"])
                    sc.activation(attn_slice(p, base), pv_of(p, base),
                                  AF.Copy).then_inc(ac.h, 1)
                if 0 <= tk - 1 < NIT:
                    p = P[tk - 1]
                    i = tk - 1
                    sc.wait_ge(dv.h, p["dv_max"])
                    if i >= 3:
                        sc.wait_ge(trs.h, 16 * (i - 2))
                    sc.activation(probs[i % 3][:, 0:p["L"]], sc_of(p), AF.Exp,
                                  bias=nmax[i % 4][:],
                                  accum_out=rs[i % 4][:]).then_inc(ac.h, 1)

        @b.sync
        def _(s):
            ld.inc(s.dma_start(mask[:], io["c_causal"].ap()), 16)
            for i, p in enumerate(P):
                s.wait_ge(dv.h, p["dv_norm"])
                if i >= 3:
                    s.wait_ge(pe.h, P[i - 3]["pe_v"])
                trs.inc(s.dma_start(
                    probsT[i % 3][:, 0:p["L"]].rearrange("p (c f) -> p c f", f=128),
                    probs[i % 3][:, 0:p["L"]].rearrange("p (c f) -> p c f", f=128),
                    transpose=True), 16)

        if bg is not None:
            @b.gpsimd
            def _(g):
                bg(g)
    pe.n = pe_n[0]
    dv.n = dv_n[0]
    ac.n = ac_n[0]
    return {"sb_p3_probs0": probs[0], "sb_p3_probs1": probs[1],
            "sb_p3_pt0": probsT[0], "sb_p3_pt1": probsT[1],
            "sb_p3_rinv": rinv[0], "sb_p3_nmax": nmax[0], "sb_p3_mx0": nmax[1],
            "sb_p3_rs0": rs[0], "sb_p3_mask": mask}



def p4_l0_proj(cx, io, attn_fm):
    nc = cx.nc
    wl = cx.sem("p4wl")
    mm = cx.sem("p4mm")
    ev = cx.sem("p4ev")
    st = cx.sem("p4st")
    wo = cx.sbuf("p4wo", [128, KCD * D], BF16)
    sb = [cx.sbuf(f"p4sb{i}", [128, 512], F32) for i in range(4)]
    ps = [cx.psum(f"p4ps{i}", [128, 512]) for i in range(8)]

    with nc.Block() as b:
        @b.gpsimd
        def _(g):
            wl.inc(g.dma_start(
                wo[:].rearrange("p (c n) -> p c n", c=KCD),
                io["L0_attn_Wo"].ap().rearrange("(c p) n -> p c n", p=128)), 16)

        @b.tensor
        def _(t):
            t.wait_ge(wl.h, 16)
            for it in range(NT * 4):
                i, n = divmod(it, 4)
                bank = it % 8
                if it >= 8:
                    t.wait_ge(ev.h, it - 8 + 1)
                for kc in range(KCD):
                    ins = t.matmul(ps[bank][:],
                                   attn_fm[:, kc * T + i * 128: kc * T + (i + 1) * 128],
                                   wo[:, kc * D + n * 512: kc * D + (n + 1) * 512],
                                   start=(kc == 0), stop=(kc == KCD - 1))
                mm.inc(ins)

        @b.scalar
        def _(sc):
            for it in range(NT * 4):
                sc.wait_ge(mm.h, it + 1)
                if it >= 4:
                    sc.wait_ge(st.h, 16 * (it - 4 + 1))
                ev.inc(sc.activation(sb[it % 4][:], ps[it % 8][:], AF.Copy))

        @b.sync
        def _(s):
            for it in range(NT * 4):
                i, n = divmod(it, 4)
                s.wait_ge(ev.h, it + 1)
                st.inc(s.dma_start(
                    io["sc_s"].ap()[ts(i, 128), ts(n, 512)], sb[it % 4][:]), 16)
            s.wait_ge(st.h, st.n)



def p5_l0_ffn(cx, io):
    nc = cx.nc
    ld = cx.sem("p5ld")
    ul = cx.sem("p5ul")
    gl = cx.sem("p5gl")
    mm = cx.sem("p5mm")
    dv = cx.sem("p5dv")
    ac = cx.sem("p5ac")
    st = cx.sem("p5st")

    NQ = 4
    QT = 512
    IC = INTER // 128

    xn_q = cx.sbuf("p5xn", [128, KCD * QT], BF16)
    gwb = [cx.sbuf(f"p5gw{i}", [128, KCD * 128], BF16) for i in range(2)]
    uwb = [cx.sbuf(f"p5uw{i}", [128, KCD * 128], BF16) for i in range(2)]
    act_q = cx.sbuf("p5act", [128, IC * QT], BF16)
    dw_n = cx.sbuf("p5dw", [128, IC * QT], BF16)
    gsc = cx.sbuf("p5gsc", [128, IC], F32)
    usc = cx.sbuf("p5usc", [128, IC], F32)
    g_f = cx.sbuf("p5gf", [128, 512], F32)
    gs_ = cx.sbuf("p5gs", [128, 512], F32)
    gbuf = cx.sbuf("p5gbuf", [128, 4 * D], F32)
    xn1f = cx.sbuf("p5xn1f", [128, D], F32)
    xts = [cx.sbuf("p5xt0", [128, D], F32)] * 2
    sts_ = [cx.sbuf("p5st0", [128, D], F32)] * 2
    x1b = cx.sbuf("p5x1", [128, D], F32)
    xn1b = cx.sbuf("p5xn1b", [128, D], BF16)
    xlob = cx.sbuf("p5xlob", [128, D], BF16)
    sq5 = xn1f
    ssum = cx.sbuf("p5ssum", [128, 1], F32)
    ssq = cx.sbuf("p5ssq", [128, 1], F32)
    nmean = cx.sbuf("p5nmean", [128, 1], F32)
    var = cx.sbuf("p5var", [128, 1], F32)
    rstd = cx.sbuf("p5rstd", [128, 1], F32)
    ps = [cx.psum(f"p5ps{i}", [128, 512]) for i in range(8)]

    cnt = {"ld": 2, "gl": 0, "mm": 0, "dv": 0, "ac": 0, "st": 0}
    bank_hist = []
    P = []

    def use_bank():
        i = len(bank_hist)
        w = bank_hist[i - 8] if i >= 8 else None
        bank_hist.append(None)
        return i, w

    for q in range(NQ):
        plan = {"q": q, "oc": [], "bn": [], "tiles": []}
        cnt["ld"] += 4
        plan["ld_xn"] = cnt["ld"]
        for oc in range(IC):
            e = {}
            cnt["ld"] += 1
            e["ld_w"] = cnt["ld"]
            cnt["ul"] = cnt.get("ul", 0) + 1
            e["ul_w"] = cnt["ul"]
            e["gu"], e["gu_w"] = use_bank()
            e["uu"], e["uu_w"] = use_bank()
            cnt["mm"] += 1
            e["mm_g"] = cnt["mm"]
            cnt["mm"] += 1
            e["mm_u"] = cnt["mm"]
            cnt["dv"] += 1
            e["dv_g"] = cnt["dv"]
            bank_hist[e["gu"]] = ("dv", cnt["dv"])
            cnt["ac"] += 1
            e["ac_silu"] = cnt["ac"]
            cnt["dv"] += 1
            e["dv_stt"] = cnt["dv"]
            bank_hist[e["uu"]] = ("dv", cnt["dv"])
            plan["oc"].append(e)
        for n in range(4):
            e = {}
            cnt["ld"] += 4
            e["ld_dw"] = cnt["ld"]
            e["tiles"] = []
            for t in range(4):
                f = {}
                f["bu"], f["bu_w"] = use_bank()
                cnt["mm"] += 1
                f["mm"] = cnt["mm"]
                cnt["ac"] += 1
                f["ac"] = cnt["ac"]
                bank_hist[f["bu"]] = ("ac", cnt["ac"])
                e["tiles"].append(f)
            plan["bn"].append(e)
        for t in range(4):
            f = {}
            cnt["gl"] += 2
            f["gl"] = cnt["gl"]
            base = cnt["dv"]
            f["dv0"] = base
            cnt["dv"] += 9
            f["dv_x1"] = base + 2
            f["dv_red"] = base + 3
            f["dv_mean"] = base + 4
            f["dv_vs"] = base + 5
            f["dv_ve"] = base + 6
            f["dv_rec"] = base + 7
            f["dv_done"] = base + 9
            cnt["ac"] += 2
            f["ac_sq"] = cnt["ac"] - 1
            f["ac_rt"] = cnt["ac"]
            cnt["st"] += 3
            f["st"] = cnt["st"]
            plan["tiles"].append(f)
        P.append(plan)

    with nc.Block() as b:
        @b.sync
        def _(s):
            s.wait_ge(cx.p0sem.h, 16 * cx.mark_l0)
            ld.inc(s.dma_start(gsc[:], io["c_gate_s_t"].ap()), 16)
            ld.inc(s.dma_start(usc[:], io["c_up_s_t"].ap()), 16)
            for q in range(NQ):
                plan = P[q]
                if q > 0:
                    s.wait_ge(mm.h, P[q - 1]["oc"][-1]["mm_u"])
                for bb in range(4):
                    ld.inc(s.dma_start(
                        bass.AP(xn_q, bb * 128, [[KCD * QT, 128], [QT, KCD], [1, 128]]),
                        io["sc_xn"].ap()[ts(4 * q + bb, 128), :],
                        transpose=True), 16)
                for oc in range(IC):
                    e = plan["oc"][oc]
                    if oc >= 2:
                        s.wait_ge(mm.h, plan["oc"][oc - 2]["mm_u"])
                    elif q > 0 and oc < 2:
                        s.wait_ge(mm.h, P[q - 1]["oc"][oc - 2]["mm_u"])
                    ld.inc(s.dma_start(
                        bass.AP(gwb[oc % 2], 0, [[KCD * 128, 128], [128, KCD], [1, 128]]),
                        io["sc_gate_w"].ap()[ts(oc, 128), :], transpose=True), 16)
                    ul.inc(s.dma_start(
                        bass.AP(uwb[oc % 2], 0, [[KCD * 128, 128], [128, KCD], [1, 128]]),
                        io["sc_up_w"].ap()[ts(oc, 128), :], transpose=True), 16)
                for n in range(4):
                    e = plan["bn"][n]
                    if n > 0:
                        s.wait_ge(mm.h, plan["bn"][n - 1]["tiles"][-1]["mm"])
                    elif q > 0:
                        s.wait_ge(mm.h, P[q - 1]["bn"][-1]["tiles"][-1]["mm"])
                    for hb in range(4):
                        ld.inc(s.dma_start(
                            bass.AP(dw_n, hb * 128, [[IC * QT, 128], [QT, IC], [1, 128]]),
                            io["sc_down_w"].ap()[n * 512 + hb * 128:
                                                 n * 512 + (hb + 1) * 128, :],
                            transpose=True), 16)

        @b.gpsimd
        def _(g):
            for q in range(NQ):
                plan = P[q]
                for t in range(4):
                    f = plan["tiles"][t]
                    it = q * 4 + t
                    if it >= 1:
                        g.wait_ge(dv.h, P[(it - 1) // 4]["tiles"][(it - 1) % 4]["dv_x1"])
                    g.dma_start(xts[it % 2][:],
                                io["x"].ap()[ts(it, 128), :]).then_inc(gl.h, 16)
                    g.dma_start(sts_[it % 2][:],
                                io["sc_s"].ap()[ts(it, 128), :]).then_inc(gl.h, 16)
                    g.wait_ge(dv.h, f["dv_x1"])
                    g.dma_start(io["sc_x1"].ap()[ts(it, 128), :], x1b[:]).then_inc(st.h, 16)
                    g.wait_ge(dv.h, f["dv_done"])
                    g.dma_start(io["sc_xn1"].ap()[ts(it, 128), :], xn1b[:]).then_inc(st.h, 16)
                    g.dma_start(io["sc_xlo"].ap()[ts(it, 128), :], xlob[:]).then_inc(st.h, 16)
            g.wait_ge(st.h, 16 * cnt["st"])
        st.n = 16 * cnt["st"]
        gl.n = 16 * cnt["gl"]

        @b.tensor
        def _(t):
            for q in range(NQ):
                plan = P[q]
                t.wait_ge(ld.h, 16 * plan["ld_xn"])
                for oc in range(IC):
                    e = plan["oc"][oc]
                    for which, bid, wt, mmord in (
                            ("g", e["gu"], e["gu_w"], e["mm_g"]),
                            ("u", e["uu"], e["uu_w"], e["mm_u"])):
                        t.wait_ge((ld.h if which == "g" else ul.h),
                                  16 * (e["ld_w"] if which == "g" else e["ul_w"]))
                        if wt is not None:
                            t.wait_ge(dv.h if wt[0] == "dv" else ac.h, wt[1])
                        wsrc = gwb[oc % 2] if which == "g" else uwb[oc % 2]
                        for kc in range(KCD):
                            ins = t.matmul(
                                ps[bid % 8][:],
                                wsrc[:, ts(kc, 128)],
                                xn_q[:, ts(kc, QT)],
                                start=(kc == 0), stop=(kc == KCD - 1))
                        ins.then_inc(mm.h, 1)
                t.wait_ge(dv.h, plan["oc"][-1]["dv_stt"])
                for n in range(4):
                    e = plan["bn"][n]
                    t.wait_ge(ld.h, 16 * e["ld_dw"])
                    for ti in range(4):
                        f = e["tiles"][ti]
                        if f["bu_w"] is not None:
                            wt = f["bu_w"]
                            t.wait_ge(dv.h if wt[0] == "dv" else ac.h, wt[1])
                        for ic in range(IC):
                            ins = t.matmul(
                                ps[f["bu"] % 8][:],
                                act_q[:, ic * QT + ti * 128: ic * QT + (ti + 1) * 128],
                                dw_n[:, ts(ic, QT)],
                                start=(ic == 0), stop=(ic == IC - 1))
                        ins.then_inc(mm.h, 1)
        mm.n = cnt["mm"]

        @b.vector
        def _(v):
            for q in range(NQ):
                plan = P[q]
                for oc in range(IC):
                    e = plan["oc"][oc]
                    v.wait_ge(mm.h, e["mm_g"])
                    if oc > 0:
                        v.wait_ge(ac.h, plan["oc"][oc - 1]["ac_silu"])
                    elif q > 0:
                        v.wait_ge(ac.h, P[q - 1]["oc"][-1]["ac_silu"])
                    v.tensor_scalar_mul(g_f[:], ps[e["gu"] % 8][:],
                                        gsc[:, oc:oc + 1]).then_inc(dv.h, 1)
                    v.wait_ge(mm.h, e["mm_u"])
                    v.wait_ge(ac.h, e["ac_silu"])
                    if oc == 0 and q > 0:
                        v.wait_ge(mm.h, P[q - 1]["bn"][-1]["tiles"][-1]["mm"])
                    v.scalar_tensor_tensor(
                        act_q[:, ts(oc, QT)], ps[e["uu"] % 8][:],
                        usc[:, oc:oc + 1], gs_[:], ALU.mult, ALU.mult) \
                        .then_inc(dv.h, 1)
                for t in range(4):
                    f = plan["tiles"][t]
                    it = q * 4 + t
                    v.wait_ge(ac.h, plan["bn"][3]["tiles"][t]["ac"])
                    v.wait_ge(gl.h, 16 * f["gl"])
                    if it >= 1:
                        v.wait_ge(st.h, 16 * P[(it - 1) // 4]["tiles"][(it - 1) % 4]["st"])
                    gb = gbuf[:, ts(t, D)]
                    v.tensor_tensor(x1b[:], sts_[it % 2][:], gb, ALU.mult) \
                        .then_inc(dv.h, 1)
                    v.tensor_tensor(x1b[:], x1b[:], xts[it % 2][:], ALU.add) \
                        .then_inc(dv.h, 1)
                    v.reduce_sum(ssum[:], x1b[:], AX).then_inc(dv.h, 1)
                    v.wait_ge(dv.h, f["dv_red"])
                    v.wait_ge(ac.h, f["ac_sq"])
                    v.tensor_scalar_mul(nmean[:], ssum[:], -1.0 / D).then_inc(dv.h, 1)
                    v.wait_ge(dv.h, f["dv_mean"])
                    v.tensor_scalar_mul(var[:], ssq[:], 1.0 / D)
                    v.tensor_scalar(ssq[:], nmean[:], nmean[:], None, ALU.mult) \
                        .then_inc(dv.h, 1)
                    v.wait_ge(dv.h, f["dv_vs"])
                    v.tensor_tensor(var[:], var[:], ssq[:], ALU.subtract)
                    v.tensor_scalar(var[:], var[:], 1e-5, None, ALU.add) \
                        .then_inc(dv.h, 1)
                    v.wait_ge(dv.h, f["dv_ve"])
                    v.reciprocal(rstd[:], var[:]).then_inc(dv.h, 1)
                    v.wait_ge(ac.h, f["ac_rt"])
                    v.tensor_scalar(xn1f[:], x1b[:], nmean[:], None, ALU.add)
                    v.tensor_scalar(xn1f[:], xn1f[:], rstd[:], None, ALU.mult) \
                        .then_inc(dv.h, 1)
                    v.tensor_copy(xn1b[:], xn1f[:])
                    v.tensor_tensor(xlob[:], xn1f[:], xn1b[:], ALU.subtract) \
                        .then_inc(dv.h, 1)
        dv.n = cnt["dv"]

        @b.scalar
        def _(sc):
            for q in range(NQ):
                plan = P[q]
                for oc in range(IC):
                    e = plan["oc"][oc]
                    sc.wait_ge(dv.h, e["dv_g"])
                    if oc > 0:
                        sc.wait_ge(dv.h, plan["oc"][oc - 1]["dv_stt"])
                    elif q > 0:
                        sc.wait_ge(dv.h, P[q - 1]["oc"][-1]["dv_stt"])
                    sc.activation(gs_[:], g_f[:], AF.Silu).then_inc(ac.h, 1)
                for n in range(4):
                    e = plan["bn"][n]
                    for ti in range(4):
                        f = e["tiles"][ti]
                        sc.wait_ge(mm.h, f["mm"])
                        if q > 0:
                            sc.wait_ge(dv.h, P[q - 1]["tiles"][ti]["dv_x1"])
                        sc.activation(gbuf[:, ti * D + n * 512: ti * D + (n + 1) * 512],
                                      ps[f["bu"] % 8][:], AF.Silu).then_inc(ac.h, 1)
                for t in range(4):
                    f = plan["tiles"][t]
                    sc.wait_ge(dv.h, f["dv_x1"])
                    sc.activation(sq5[:], x1b[:], AF.Square, accum_out=ssq[:]) \
                        .then_inc(ac.h, 1)
                    sc.wait_ge(dv.h, f["dv_rec"])
                    sc.activation(rstd[:], rstd[:], AF.Sqrt).then_inc(ac.h, 1)
        ac.n = cnt["ac"]



def p6_l1_attn(cx, io):
    nc = cx.nc
    ld = cx.sem("p6ld")
    gl = cx.sem("p6gl")
    mm = cx.sem("p6mm")
    dv = cx.sem("p6dv")
    ac = cx.sem("p6ac")
    st = cx.sem("p6st")
    qs = 1.0 / math.sqrt(D)

    bqt = cx.sbuf("p6bq", [128, NL1 * KCD], F32)
    bkt = cx.sbuf("p6bk", [128, NL1 * KCD], F32)
    bot = cx.sbuf("p6bo", [128, NL1 * KCD], F32)
    ones = cx.sbuf("p6ones", [1, 128], BF16)
    bvr = cx.sbuf("p6bv", [1, NL1 * D], BF16)
    with nc.Block() as b:
        @b.sync
        def _(s):
            ld.inc(s.dma_start(bqt[:], io["c_bq_t"].ap()), 16)
            ld.inc(s.dma_start(bkt[:], io["c_bk_t"].ap()), 16)
            ld.inc(s.dma_start(bot[:], io["c_bo_t"].ap()), 16)
            ld.inc(s.dma_start(ones[:], io["c_ones"].ap()), 16)
            ld.inc(s.dma_start(bvr[:], io["c_bv_row"].ap()), 16)
            s.wait_ge(ld.h, ld.n)
        @b.vector
        def _(v):
            v.wait_ge(ld.h, ld.n)

    k_sb = cx.sbuf("p6ksb", [128, KCD * T], BF16)
    for l in range(NL1):
        src_fm = io["sc_xn1"] if l == 0 else io["sc_xa%d" % ((l - 1) % 2)]
        src_is_tm = (l == 0)
        dst = io["sc_s"] if l == NL1 - 1 else io["sc_xa%d" % (l % 2)]

        with cx.phase():
            xa = cx.sbuf(f"p6xa{l}", [128, KCD * T], BF16)
            wcol = [cx.sbuf(f"p6wc{l}{i}", [128, KCD * 128], BF16) for i in range(2)]
            wv = cx.sbuf(f"p6wv{l}", [128, KCD * 512], BF16)
            stg = [cx.sbuf(f"p6stg{l}{i}", [128, 512], BF16) for i in range(4)]
            ps = [cx.psum(f"p6ps{l}{i}", [128, 512]) for i in range(8)]

            ld0, gl0, mm0, ac0, st0 = ld.n, gl.n, mm.n, ac.n, st.n
            with nc.Block() as b:
                @b.sync
                def _(s):
                    s.wait_ge(st.h, st0)
                    if src_is_tm:
                        for bb in range(NT):
                            ld.inc(s.dma_start(
                                bass.AP(xa, bb * 128,
                                        [[KCD * T, 128], [T, KCD], [1, 128]]),
                                src_fm.ap()[ts(bb, 128), :], transpose=True), 16)
                    else:
                        for kc in range(KCD):
                            ld.inc(s.dma_start(
                                xa[:].rearrange("p (c t) -> p c t", c=KCD)[:, kc:kc + 1, :],
                                src_fm.ap().rearrange("(c p) t -> p c t", p=128)[:, kc:kc + 1, :]), 16)
                    js = 0
                    for j in list(range(64)) + list(range(128, 192)):
                        s.wait_ge(ac.h, ac0 + j + 1)
                        if j < 64:
                            m, n = divmod(j, 4)
                            dst_ap = io["sc_qf"].ap()[ts(m, 128), ts(n, 512)]
                        else:
                            n, t_ = divmod(j - 128, 16)
                            dst_ap = io["sc_v"].ap()[ts(t_, 128), ts(n, 512)]
                        st.inc(s.dma_start(dst_ap, stg[js % 4][:]), 16)
                        js += 1

                @b.gpsimd
                def _(g):
                    moe = cx.cast_jobs[cx.mark_l0:]
                    q4 = (len(moe) + NL1 - 1) // NL1
                    slice_ = list(moe[l * q4:(l + 1) * q4])
                    for proj in range(2):
                        w = io["L1_attn_Wq"] if proj == 0 else io["L1_attn_Wk"]
                        for m in range(KCD):
                            j = proj * 16 + m
                            if j >= 2:
                                g.wait_ge(mm.h, mm0 + (j - 1) * 4)
                            gl.inc(g.dma_start(
                                wcol[m % 2][:].rearrange("p (c f) -> p c f", c=KCD),
                                w.ap()[l].rearrange("(c p) f -> p c f", p=128)
                                 [:, :, ts(m, 128)]), 16)
                            if j % 2 == 1 and slice_:
                                dst_ap, src_ap = slice_.pop(0)
                                cx.p0sem.inc(g.dma_start(dst_ap, src_ap), 16)
                    for dst_ap, src_ap in slice_:
                        cx.p0sem.inc(g.dma_start(dst_ap, src_ap), 16)
                    for n in range(4):
                        g.wait_ge(mm.h, mm0 + 128 + n * 16)
                        gl.inc(g.dma_start(
                            wv[:].rearrange("p (c f) -> p c f", c=KCD),
                            io["L1_attn_Wv"].ap()[l].rearrange("(c p) f -> p c f", p=128)
                             [:, :, ts(n, 512)]), 16)

                @b.tensor
                def _(t):
                    t.wait_ge(ld.h, ld0 + 16 * KCD)
                    u = 0
                    for proj in range(2):
                        for m in range(KCD):
                            t.wait_ge(gl.h, gl0 + 16 * (proj * 16 + m + 1))
                            for n in range(4):
                                if u >= 8:
                                    t.wait_ge(ac.h, ac0 + u - 8 + 1)
                                for kc in range(KCD):
                                    ins = t.matmul(
                                        ps[u % 8][:],
                                        wcol[m % 2][:, ts(kc, 128)],
                                        xa[:, kc * T + n * 512: kc * T + (n + 1) * 512],
                                        start=(kc == 0), stop=(kc == KCD - 1))
                                mm.inc(ins)
                                u += 1
                    for n in range(4):
                        t.wait_ge(gl.h, gl0 + 16 * (32 + n + 1))
                        for t_ in range(NT):
                            if u >= 8:
                                t.wait_ge(ac.h, ac0 + u - 8 + 1)
                            for kc in range(KCD):
                                t.matmul(
                                    ps[u % 8][:],
                                    xa[:, kc * T + t_ * 128: kc * T + (t_ + 1) * 128],
                                    wv[:, ts(kc, 512)],
                                    start=(kc == 0), stop=False)
                            ins = t.matmul(ps[u % 8][:], ones[:],
                                           bvr[0:1, l * D + n * 512: l * D + (n + 1) * 512],
                                           start=False, stop=True)
                            mm.inc(ins)
                            u += 1

                @b.vector
                def _(v):
                    js = 0
                    for j in range(192):
                        v.wait_ge(mm.h, mm0 + j + 1)
                        if j < 64:
                            if js >= 4:
                                v.wait_ge(st.h, st0 + 16 * (js - 4 + 1))
                            m = j // 4
                            n = j % 4
                            v.tensor_scalar(stg[js % 4][:], ps[j % 8][:],
                                            bqt[:, l * KCD + m: l * KCD + m + 1],
                                            None, ALU.add).then_inc(ac.h, 1)
                            js += 1
                        elif j < 128:
                            m, n = divmod(j - 64, 4)
                            v.tensor_scalar(
                                k_sb[:, m * T + n * 512: m * T + (n + 1) * 512],
                                ps[j % 8][:],
                                bkt[:, l * KCD + m: l * KCD + m + 1],
                                None, ALU.add).then_inc(ac.h, 1)
                        else:
                            if js >= 4:
                                v.wait_ge(st.h, st0 + 16 * (js - 4 + 1))
                            v.tensor_copy(stg[js % 4][:], ps[j % 8][:]) \
                                .then_inc(ac.h, 1)
                            js += 1
                    ac.n += 192

        with cx.phase():
            v_sb = cx.sbuf(f"p6v{l}", [128, KCD * D], BF16)
            qt_b = [cx.sbuf(f"p6qt{l}{i}", [128, KCD * 128], BF16) for i in range(2)]
            probs = [cx.sbuf(f"p6pr{l}{i}", [128, T], BF16) for i in range(2)]
            probsT = [cx.sbuf(f"p6pt{l}{i}", [128, T], BF16) for i in range(2)]
            astg = [cx.sbuf(f"p6as{l}{i}", [128, 512], BF16) for i in range(4)]
            nmax = [cx.sbuf(f"p6nm{l}{i}", [128, 1], F32) for i in range(2)]
            rs = [cx.sbuf(f"p6rs{l}{i}", [128, 1], F32) for i in range(2)]
            rinv = [cx.sbuf(f"p6ri{l}{i}", [128, 1], F32) for i in range(2)]
            psA = cx.psum(f"p6A{l}", [128, T])
            psB = cx.psum(f"p6B{l}", [128, T])

            ld0, gl0, mm0, dv0, ac0, st0, tr0 = ld.n, gl.n, mm.n, dv.n, ac.n, st.n, 0
            trs = cx.sem(f"p6tr{l}")
            with nc.Block() as b:
                @b.sync
                def _(s):
                    s.wait_ge(st.h, st0)
                    ld.inc(s.dma_start(
                        qt_b[0][:].rearrange("p (c f) -> p c f", c=KCD),
                        io["sc_qf"].ap().rearrange("(c p) t -> p c t", p=128)
                         [:, :, ts(0, 128)]), 16)
                    for kc in range(KCD):
                        ld.inc(s.dma_start(
                            v_sb[:].rearrange("p (c t) -> p c t", c=KCD)[:, kc:kc + 1, :],
                            io["sc_v"].ap().rearrange("(c p) t -> p c t", p=128)
                             [:, kc:kc + 1, :]), 16)
                    for qt in range(NT):
                        if qt >= 1:
                            if qt >= 2:
                                s.wait_ge(mm.h, mm0 + 5 * (qt - 2) + 1)
                            ld.inc(s.dma_start(
                                qt_b[qt % 2][:].rearrange("p (c f) -> p c f", c=KCD),
                                io["sc_qf"].ap().rearrange("(c p) t -> p c t", p=128)
                                 [:, :, ts(qt, 128)]), 16)
                        s.wait_ge(ac.h, ac0 + qt + 1)
                        if qt >= 2:
                            s.wait_ge(mm.h, mm0 + 5 * (qt - 2) + 5)
                        trs.inc(s.dma_start(
                            probsT[qt % 2][:].rearrange("p (c f) -> p c f", f=128),
                            probs[qt % 2][:].rearrange("p (c f) -> p c f", f=128),
                            transpose=True), 16)

                @b.tensor
                def _(t):
                    for qt in range(NT):
                        t.wait_ge(ld.h, ld0 + 16 * (1 if qt == 0 else KCD + qt + 1))
                        if qt >= 1:
                            t.wait_ge(ac.h, ac0 + qt)
                        for n in range(4):
                            for kc in range(KCD):
                                ins = t.matmul(
                                    psA[:, ts(n, 512)],
                                    qt_b[qt % 2][:, ts(kc, 128)],
                                    k_sb[:, kc * T + n * 512: kc * T + (n + 1) * 512],
                                    start=(kc == 0), stop=(kc == KCD - 1))
                        ins.then_inc(mm.h, 1)
                        mm.n += 1
                        if qt == 0:
                            t.wait_ge(ld.h, ld0 + 16 * (KCD + 1))
                        t.wait_ge(trs.h, 16 * (qt + 1))
                        if qt >= 1:
                            t.wait_ge(dv.h, dv0 + 7 * qt)
                        for n in range(4):
                            for kc in range(KCD):
                                ins = t.matmul(
                                    psB[:, ts(n, 512)],
                                    probsT[qt % 2][:, ts(kc, 128)],
                                    v_sb[:, kc * D + n * 512: kc * D + (n + 1) * 512],
                                    start=(kc == 0), stop=(kc == KCD - 1))
                            mm.inc(ins)

                @b.vector
                def _(v):
                    for qt in range(NT):
                        v.wait_ge(mm.h, mm0 + 5 * qt + 1)
                        v.reduce_max(nmax[qt % 2][:], psA[:], AX, negate=True) \
                            .then_inc(dv.h, 1)
                        v.wait_ge(dv.h, dv0 + 7 * qt + 1)
                        v.tensor_scalar_mul(nmax[qt % 2][:], nmax[qt % 2][:], qs) \
                            .then_inc(dv.h, 1)
                        v.wait_ge(ac.h, ac0 + qt + 1)
                        v.reciprocal(rinv[qt % 2][:], rs[qt % 2][:]).then_inc(dv.h, 1)
                        dv.n += 3
                        v.wait_ge(mm.h, mm0 + 5 * qt + 5)
                        v.wait_ge(dv.h, dv0 + 7 * qt + 3)
                        for n in range(4):
                            if 4 * qt + n >= 4:
                                v.wait_ge(st.h, st0 + 16 * (4 * qt + n - 4 + 1))
                            v.tensor_scalar_mul(astg[n % 4][:], psB[:, ts(n, 512)],
                                                rinv[qt % 2][:]).then_inc(dv.h, 1)
                            dv.n += 1

                @b.scalar
                def _(sc):
                    for qt in range(NT):
                        sc.wait_ge(dv.h, dv0 + 7 * qt + 2)
                        if qt >= 2:
                            sc.wait_ge(dv.h, dv0 + 7 * (qt - 1))
                            sc.wait_ge(trs.h, 16 * (qt - 1))
                        sc.activation(probs[qt % 2][:], psA[:], AF.Exp,
                                      bias=nmax[qt % 2][:], scale=qs,
                                      accum_out=rs[qt % 2][:]).then_inc(ac.h, 1)
                        ac.n += 1

                @b.gpsimd
                def _(g):
                    for qt in range(NT):
                        for n in range(4):
                            g.wait_ge(dv.h, dv0 + 7 * qt + 3 + n + 1)
                            st.inc(g.dma_start(
                                io["sc_attn"].ap()[ts(qt, 128), ts(n, 512)],
                                astg[n % 4][:]), 16)
                    g.wait_ge(st.h, st.n)

        last = (l == NL1 - 1)
        with cx.phase():
            at_sb = cx.sbuf(f"p6at{l}", [128, KCD * T], BF16)
            if last:
                wcol = [cx.sbuf(f"p6oc{l}{i}", [128, KCD * 512], BF16)
                        for i in range(2)]
                bor = cx.sbuf(f"p6bor{l}", [1, D], BF16)
            else:
                wcol = [cx.sbuf(f"p6oc{l}{i}", [128, KCD * 128], BF16)
                        for i in range(2)]
            ostg = [cx.sbuf(f"p6os{l}{i}", [128, 512],
                            F32 if last else BF16) for i in range(4)]
            ps = [cx.psum(f"p6cps{l}{i}", [128, 512]) for i in range(8)]
            ld0, gl0, mm0, ac0, st0 = ld.n, gl.n, mm.n, ac.n, st.n
            with nc.Block() as b:
                @b.sync
                def _(s):
                    s.wait_ge(st.h, st0)
                    if last:
                        ld.inc(s.dma_start(bor[:], io["c_bo_row"].ap()), 16)
                    for bb in range(NT):
                        ld.inc(s.dma_start(
                            bass.AP(at_sb, bb * 128,
                                    [[KCD * T, 128], [T, KCD], [1, 128]]),
                            io["sc_attn"].ap()[ts(bb, 128), :], transpose=True), 16)
                    for j in range(64):
                        if last:
                            n, m = divmod(j, 16)
                        else:
                            m, n = divmod(j, 4)
                        dap = dst.ap()[ts(m, 128), ts(n, 512)]
                        s.wait_ge(ac.h, ac0 + j + 1)
                        st.inc(s.dma_start(dap, ostg[j % 4][:]), 16)
                    s.wait_ge(st.h, st.n)

                @b.gpsimd
                def _(g):
                    if last:
                        for n in range(4):
                            if n >= 2:
                                g.wait_ge(mm.h, mm0 + (n - 1) * 16)
                            gl.inc(g.dma_start(
                                wcol[n % 2][:].rearrange("p (c f) -> p c f", c=KCD),
                                io["L1_attn_Wo"].ap()[l]
                                 .rearrange("(c p) f -> p c f", p=128)
                                 [:, :, ts(n, 512)]), 16)
                    else:
                        for m in range(KCD):
                            if m >= 2:
                                g.wait_ge(mm.h, mm0 + (m - 1) * 4)
                            gl.inc(g.dma_start(
                                wcol[m % 2][:].rearrange("p (c f) -> p c f", c=KCD),
                                io["L1_attn_Wo"].ap()[l]
                                 .rearrange("(c p) f -> p c f", p=128)
                                 [:, :, ts(m, 128)]), 16)

                @b.tensor
                def _(t):
                    t.wait_ge(ld.h, ld0 + 16 * (NT + (1 if last else 0)))
                    u = 0
                    if last:
                        for n in range(4):
                            t.wait_ge(gl.h, gl0 + 16 * (n + 1))
                            for t_ in range(NT):
                                if u >= 8:
                                    t.wait_ge(ac.h, ac0 + u - 8 + 1)
                                for kc in range(KCD):
                                    t.matmul(
                                        ps[u % 8][:],
                                        at_sb[:, kc * T + t_ * 128: kc * T + (t_ + 1) * 128],
                                        wcol[n % 2][:, ts(kc, 512)],
                                        start=(kc == 0), stop=False)
                                ins = t.matmul(ps[u % 8][:], ones[:],
                                               bor[0:1, ts(n, 512)],
                                               start=False, stop=True)
                                mm.inc(ins)
                                u += 1
                    else:
                        for m in range(KCD):
                            t.wait_ge(gl.h, gl0 + 16 * (m + 1))
                            for n in range(4):
                                if u >= 8:
                                    t.wait_ge(ac.h, ac0 + u - 8 + 1)
                                for kc in range(KCD):
                                    ins = t.matmul(
                                        ps[u % 8][:],
                                        wcol[m % 2][:, ts(kc, 128)],
                                        at_sb[:, kc * T + n * 512: kc * T + (n + 1) * 512],
                                        start=(kc == 0), stop=(kc == KCD - 1))
                                mm.inc(ins)
                                u += 1

                @b.vector
                def _(v):
                    for j in range(64):
                        v.wait_ge(mm.h, mm0 + j + 1)
                        if j >= 4:
                            v.wait_ge(st.h, st0 + 16 * (j - 4 + 1))
                        if last:
                            v.tensor_copy(ostg[j % 4][:], ps[j % 8][:]) \
                                .then_inc(ac.h, 1)
                        else:
                            m = j // 4
                            v.tensor_scalar(ostg[j % 4][:], ps[j % 8][:],
                                            bot[:, l * KCD + m: l * KCD + m + 1],
                                            None, ALU.add).then_inc(ac.h, 1)
                    ac.n += 64



def p7_gate(cx, io, combine_sb):
    nc = cx.nc
    ld = cx.sem("p7ld")
    dv = cx.sem("p7dv")
    ac = cx.sem("p7ac")
    mm = cx.sem("p7mm")
    gwf = cx.sbuf("p7gwf", [128, KCD * NEXP], F32)
    gwh = cx.sbuf("p7gwh", [128, KCD * NEXP], BF16)
    gwl = cx.sbuf("p7gwl", [128, KCD * NEXP], BF16)
    xh = [cx.sbuf(f"p7xh{i}", [128, KCD * 128], BF16) for i in range(2)]
    xl = [cx.sbuf(f"p7xl{i}", [128, KCD * 128], BF16) for i in range(2)]
    pr = cx.sbuf("p7pr", [128, NEXP], F32)
    pn = cx.sbuf("p7pn", [128, NEXP], F32)
    msk1 = cx.sbuf("p7m1k", [128, NEXP], F32)
    tmp8 = cx.sbuf("p7t8", [128, NEXP], F32)
    nm = cx.sbuf("p7nm", [128, 1], F32)
    rsum = cx.sbuf("p7rs", [128, 1], F32)
    rinv = cx.sbuf("p7ri", [128, 1], F32)
    m1 = cx.sbuf("p7m1", [128, 1], F32)
    m2 = cx.sbuf("p7m2", [128, 1], F32)
    den = cx.sbuf("p7dn", [128, 1], F32)
    w1_ = cx.sbuf("p7w1", [128, 1], F32)
    w2_ = cx.sbuf("p7w2", [128, 1], F32)
    psg = cx.psum("p7ps", [128, NEXP])

    with nc.Block() as b:
        @b.sync
        def _(s):
            ld.inc(s.dma_start(
                gwf[:].rearrange("p (c e) -> p c e", c=KCD),
                io["c_gate_wT"].ap().rearrange("(c p) e -> p c e", p=128)), 16)
            for i in range(NT):
                if i >= 2:
                    s.wait_ge(mm.h, i - 1)
                ld.inc(s.dma_start(
                    bass.AP(xh[i % 2], 0, [[KCD * 128, 128], [128, KCD], [1, 128]]),
                    io["sc_xn1"].ap()[ts(i, 128), :], transpose=True), 16)
                ld.inc(s.dma_start(
                    bass.AP(xl[i % 2], 0, [[KCD * 128, 128], [128, KCD], [1, 128]]),
                    io["sc_xlo"].ap()[ts(i, 128), :], transpose=True), 16)

        @b.vector
        def _(v):
            v.wait_ge(ld.h, 16)
            dv.inc(v.tensor_copy(gwh[:], gwf[:]))
            v.wait_ge(dv.h, dv.n)
            dv.inc(v.tensor_tensor(gwl[:], gwf[:], gwh[:], ALU.subtract))
            for i in range(NT):
                v.wait_ge(mm.h, i + 1)
                dv.inc(v.reduce_max(nm[:], psg[:], AX, negate=True))
                v.wait_ge(dv.h, dv.n)
                v.wait_ge(ac.h, i + 1)
                dv.inc(v.reciprocal(rinv[:], rsum[:]))
                v.wait_ge(dv.h, dv.n)
                dv.inc(v.tensor_scalar_mul(pn[:], pr[:], rinv[:]))
                v.wait_ge(dv.h, dv.n)
                dv.inc(v.reduce_max(m1[:], pn[:], AX))
                v.wait_ge(dv.h, dv.n)
                dv.inc(v.tensor_scalar(msk1[:], pn[:], m1[:], None, ALU.is_equal))
                v.wait_ge(dv.h, dv.n)
                dv.inc(v.scalar_tensor_tensor(tmp8[:], msk1[:], -1e30, pn[:],
                                              ALU.mult, ALU.add))
                v.wait_ge(dv.h, dv.n)
                dv.inc(v.reduce_max(m2[:], tmp8[:], AX))
                v.wait_ge(dv.h, dv.n)
                dv.inc(v.tensor_tensor(den[:], m1[:], m2[:], ALU.add))
                v.wait_ge(dv.h, dv.n)
                dv.inc(v.reciprocal(w1_[:], den[:]))
                v.wait_ge(dv.h, dv.n)
                dv.inc(v.tensor_scalar(w2_[:], m2[:], w1_[:], None, ALU.mult))
                dv.inc(v.tensor_scalar(w1_[:], m1[:], w1_[:], None, ALU.mult))
                v.wait_ge(dv.h, dv.n)
                dv.inc(v.tensor_scalar(msk1[:], msk1[:], w1_[:], None, ALU.mult))
                dv.inc(v.tensor_scalar(tmp8[:], pn[:], m2[:], None, ALU.is_equal))
                v.wait_ge(dv.h, dv.n)
                dv.inc(v.scalar_tensor_tensor(
                    combine_sb[:, ts(i, NEXP)], tmp8[:], w2_[:], msk1[:],
                    ALU.mult, ALU.add))
                v.wait_ge(dv.h, dv.n)

        @b.tensor
        def _(t):
            t.wait_ge(dv.h, 2)
            for i in range(NT):
                t.wait_ge(ld.h, 16 * (2 * i + 3))
                if i >= 1:
                    t.wait_ge(ac.h, i)
                for kc in range(KCD):
                    last = (kc == KCD - 1)
                    t.matmul(psg[:], xh[i % 2][:, ts(kc, 128)],
                             gwh[:, ts(kc, NEXP)], start=(kc == 0), stop=False)
                    t.matmul(psg[:], xl[i % 2][:, ts(kc, 128)],
                             gwh[:, ts(kc, NEXP)], start=False, stop=False)
                    ins = t.matmul(psg[:], xh[i % 2][:, ts(kc, 128)],
                                   gwl[:, ts(kc, NEXP)], start=False, stop=last)
                mm.inc(ins)

        @b.scalar
        def _(sc):
            for i in range(NT):
                sc.wait_ge(dv.h, 2 + 14 * i + 1)
                sc.activation(pr[:], psg[:], AF.Exp, bias=nm[:],
                              accum_out=rsum[:]).then_inc(ac.h, 1)
        ac.n = NT



def p8_moe(cx, io, combine_sb):
    nc = cx.nc
    ld = cx.sem("p8ld")
    w2l = cx.sem("p8w2l")
    gl = cx.sem("p8gl")
    mm = cx.sem("p8mm")
    dv = cx.sem("p8dv")
    ac = cx.sem("p8ac")
    st = cx.sem("p8st")
    HT = 1024
    NTH = HT // 128

    xn_h = cx.sbuf("p8xn", [128, KCD * HT], BF16)
    acc = cx.sbuf("p8acc", [128, NTH * D], F32)
    act_h = cx.sbuf("p8act", [128, MIC * HT], BF16)
    w2t = cx.sbuf("p8w2", [128, MIC * D], BF16)
    w1t = [cx.sbuf(f"p8w1{i}", [128, KCD * 128], BF16) for i in range(4)]
    gs8 = cx.sbuf("p8gs", [128, HT], F32)
    xt8 = cx.sbuf("p8xt", [128, D], F32)
    st8 = cx.sbuf("p8st_", [128, D], F32)
    ps = [cx.psum(f"p8ps{i}", [128, 512]) for i in range(8)]

    cnt = {"ld": 0, "mm": 0, "dv": 0, "ac": 0, "st": 0, "gl": 0}
    bank_hist = []
    P = []

    def use_bank():
        i = len(bank_hist)
        w = bank_hist[i - 8] if i >= 8 else None
        bank_hist.append(None)
        return i, w

    for h in range(2):
        plan = {"pairs": [], "m2": []}
        cnt["ld"] += NTH
        plan["ld_xn"] = cnt["ld"]
        for e in range(NEXP):
            ep = {"pairs": [], "m2": []}
            cnt["w2"] = cnt.get("w2", 0) + KCD
            ep["ld_w2"] = cnt["w2"]
            for j in range(MIC):
                pr = {}
                cnt["ld"] += 2
                pr["ld_w1"] = cnt["ld"]
                pr["gu"] = [use_bank() for _ in range(2)]
                cnt["mm"] += 1
                pr["mm_g"] = cnt["mm"]
                pr["uu"] = [use_bank() for _ in range(2)]
                cnt["mm"] += 1
                pr["mm_u"] = cnt["mm"]
                cnt["ac"] += 2
                pr["ac_s"] = cnt["ac"]
                for z in range(2):
                    bank_hist[pr["gu"][z][0]] = ("ac", cnt["ac"] - 1 + z)
                cnt["dv"] += 2
                pr["dv_a"] = cnt["dv"]
                for z in range(2):
                    bank_hist[pr["uu"][z][0]] = ("dv", cnt["dv"] - 1 + z)
                ep["pairs"].append(pr)
            for n in range(4):
                for t2 in range(NTH):
                    f = {}
                    f["bu"], f["bu_w"] = use_bank()
                    cnt["mm"] += 1
                    f["mm"] = cnt["mm"]
                    cnt["dv"] += 1
                    f["dv"] = cnt["dv"]
                    bank_hist[f["bu"]] = ("dv", cnt["dv"])
                    ep["m2"].append(f)
            plan.setdefault("exp", []).append(ep)
        for t2 in range(NTH):
            f = {}
            cnt["gl"] += 2
            f["gl"] = cnt["gl"]
            cnt["ac"] += 1
            f["ac"] = cnt["ac"]
            cnt["dv"] += 2
            f["dv"] = cnt["dv"]
            cnt["st"] += 1
            f["st"] = cnt["st"]
            plan["tiles"] = plan.get("tiles", []) + [f]
        P.append(plan)

    with nc.Block() as b:
        @b.sync
        def _(s):
            s.wait_ge(cx.p0sem.h, cx.p0sem.n)
            for h in range(2):
                plan = P[h]
                if h > 0:
                    s.wait_ge(mm.h, P[0]["exp"][-1]["pairs"][-1]["mm_u"])
                for bb in range(NTH):
                    ld.inc(s.dma_start(
                        bass.AP(xn_h, bb * 128, [[KCD * HT, 128], [HT, KCD], [1, 128]]),
                        io["sc_xn1"].ap()[ts(h * NTH + bb, 128), :],
                        transpose=True), 16)
                for e in range(NEXP):
                    ep = plan["exp"][e]
                    for j in range(MIC):
                        pr = ep["pairs"][j]
                        if j >= 2:
                            s.wait_ge(mm.h, ep["pairs"][j - 2]["mm_u"])
                        for which, oc in ((0, j), (1, j + MIC)):
                            ld.inc(s.dma_start(
                                bass.AP(w1t[(j % 2) * 2 + which], 0,
                                        [[KCD * 128, 128], [128, KCD], [1, 128]]),
                                io["sc_w1"].ap()[e][ts(oc, 128), :],
                                transpose=True), 16)
                        if 1 <= j <= 8:
                            if j == 1:
                                if e > 0:
                                    s.wait_ge(mm.h, plan["exp"][e - 1]["m2"][-1]["mm"])
                                elif h > 0:
                                    s.wait_ge(mm.h, P[0]["exp"][-1]["m2"][-1]["mm"])
                            for hc in (2 * (j - 1), 2 * (j - 1) + 1):
                                w2l.inc(s.dma_start(
                                    bass.AP(w2t, hc * 128,
                                            [[MIC * D, 128], [D, MIC], [1, 128]]),
                                    io["sc_w2"].ap()[e][ts(hc, 128), :],
                                    transpose=True), 16)

        @b.gpsimd
        def _(g):
            for h in range(2):
                plan = P[h]
                for t2 in range(NTH):
                    f = plan["tiles"][t2]
                    it = h * NTH + t2
                    if it >= 1:
                        prev = P[(it - 1) // NTH]["tiles"][(it - 1) % NTH]
                        g.wait_ge(dv.h, prev["dv"])
                        g.wait_ge(st.h, 16 * prev["st"])
                    g.dma_start(xt8[:], io["sc_x1"].ap()[ts(it, 128), :]) \
                        .then_inc(gl.h, 16)
                    g.dma_start(st8[:], io["sc_s"].ap()[ts(it, 128), :]) \
                        .then_inc(gl.h, 16)
                    g.wait_ge(dv.h, f["dv"])
                    g.dma_start(io["out"].ap()[ts(it, 128), :], st8[:]) \
                        .then_inc(st.h, 16)
            g.wait_ge(st.h, 16 * cnt["st"])
        st.n = 16 * cnt["st"]
        gl.n = 16 * cnt["gl"]

        @b.tensor
        def _(t):
            for h in range(2):
                plan = P[h]
                t.wait_ge(ld.h, 16 * plan["ld_xn"])
                for e in range(NEXP):
                    ep = plan["exp"][e]
                    for j in range(MIC):
                        pr = ep["pairs"][j]
                        t.wait_ge(ld.h, 16 * pr["ld_w1"])
                        for which, banks, mmord in ((0, pr["gu"], pr["mm_g"]),
                                                    (1, pr["uu"], pr["mm_u"])):
                            for z in range(2):
                                bid, wt = banks[z]
                                if wt is not None:
                                    t.wait_ge(dv.h if wt[0] == "dv" else ac.h, wt[1])
                                for kc in range(KCD):
                                    ins = t.matmul(
                                        ps[bid % 8][:],
                                        w1t[(j % 2) * 2 + which][:, ts(kc, 128)],
                                        xn_h[:, kc * HT + z * 512: kc * HT + (z + 1) * 512],
                                        start=(kc == 0), stop=(kc == KCD - 1))
                            mm.inc(ins)
                    t.wait_ge(dv.h, ep["pairs"][-1]["dv_a"])
                    t.wait_ge(w2l.h, 16 * ep["ld_w2"])
                    for n in range(4):
                        for t2 in range(NTH):
                            f = ep["m2"][n * NTH + t2]
                            if f["bu_w"] is not None:
                                wt = f["bu_w"]
                                t.wait_ge(dv.h if wt[0] == "dv" else ac.h, wt[1])
                            for ic in range(MIC):
                                ins = t.matmul(
                                    ps[f["bu"] % 8][:],
                                    act_h[:, ic * HT + t2 * 128: ic * HT + (t2 + 1) * 128],
                                    w2t[:, ic * D + n * 512: ic * D + (n + 1) * 512],
                                    start=(ic == 0), stop=(ic == MIC - 1))
                            mm.inc(ins)

        @b.scalar
        def _(sc):
            for h in range(2):
                plan = P[h]
                for e in range(NEXP):
                    ep = plan["exp"][e]
                    for j in range(MIC):
                        pr = ep["pairs"][j]
                        sc.wait_ge(mm.h, pr["mm_g"])
                        if j >= 1:
                            sc.wait_ge(dv.h, ep["pairs"][j - 1]["dv_a"])
                        elif e >= 1:
                            sc.wait_ge(dv.h, plan["exp"][e - 1]["pairs"][-1]["dv_a"])
                        elif h >= 1:
                            sc.wait_ge(dv.h, P[0]["exp"][-1]["pairs"][-1]["dv_a"])
                        for z in range(2):
                            sc.activation(gs8[:, ts(z, 512)],
                                          ps[pr["gu"][z][0] % 8][:], AF.Silu) \
                                .then_inc(ac.h, 1)
                for t2 in range(NTH):
                    f = plan["tiles"][t2]
                    sc.wait_ge(dv.h, plan["exp"][-1]["m2"][3 * NTH + t2]["dv"])
                    sc.activation(acc[:, ts(t2, D)], acc[:, ts(t2, D)], AF.Silu) \
                        .then_inc(ac.h, 1)

        @b.vector
        def _(v):
            for h in range(2):
                plan = P[h]
                if h > 0:
                    v.wait_ge(ac.h, P[0]["tiles"][-1]["ac"])
                v.memset(acc[:], 0.0)
                for e in range(NEXP):
                    ep = plan["exp"][e]
                    for j in range(MIC):
                        pr = ep["pairs"][j]
                        v.wait_ge(mm.h, pr["mm_u"])
                        v.wait_ge(ac.h, pr["ac_s"])
                        if j == 0 and (e > 0 or h > 0):
                            prev = plan["exp"][e - 1] if e > 0 else P[0]["exp"][-1]
                            v.wait_ge(mm.h, prev["m2"][-1]["mm"])
                        for z in range(2):
                            v.tensor_tensor(
                                act_h[:, j * HT + z * 512: j * HT + (z + 1) * 512],
                                ps[pr["uu"][z][0] % 8][:],
                                gs8[:, ts(z, 512)], ALU.mult).then_inc(dv.h, 1)
                    for n in range(4):
                        for t2 in range(NTH):
                            f = ep["m2"][n * NTH + t2]
                            v.wait_ge(mm.h, f["mm"])
                            if h > 0 and e == 0 and n == 0:
                                pass
                            v.scalar_tensor_tensor(
                                acc[:, t2 * D + n * 512: t2 * D + (n + 1) * 512],
                                ps[f["bu"] % 8][:],
                                combine_sb[:, (h * NTH + t2) * NEXP + e:
                                           (h * NTH + t2) * NEXP + e + 1],
                                acc[:, t2 * D + n * 512: t2 * D + (n + 1) * 512],
                                ALU.mult, ALU.add).then_inc(dv.h, 1)
                for t2 in range(NTH):
                    f = plan["tiles"][t2]
                    v.wait_ge(ac.h, f["ac"])
                    v.wait_ge(gl.h, 16 * f["gl"])
                    if h * NTH + t2 >= 1:
                        it = h * NTH + t2
                        v.wait_ge(st.h, 16 * P[(it - 1) // NTH]["tiles"][(it - 1) % NTH]["st"])
                    v.tensor_tensor(st8[:], st8[:], acc[:, ts(t2, D)], ALU.mult) \
                        .then_inc(dv.h, 1)
                    v.tensor_tensor(st8[:], st8[:], xt8[:], ALU.add).then_inc(dv.h, 1)
        dv.n = cnt["dv"]
        ac.n = cnt["ac"]



def declare_io(nc, dump):
    io = {}

    def inp(name, shape, dtype):
        io[name] = nc.dram_tensor(name, shape, dtype, kind="ExternalInput")

    def scratch(name, shape, dtype):
        kind = "ExternalOutput" if name in dump else "Internal"
        io[name] = nc.dram_tensor(name, shape, dtype, kind=kind)

    inp("x", [T, D], F32)
    inp("L0_attn_Wq", [D, D], F32)
    inp("L0_attn_Wk", [D, 512], F32)
    inp("L0_attn_Wv", [D, 512], F32)
    inp("L0_attn_Wo", [D, D], F32)
    inp("L0_ffn_gate_w", [INTER, D], F32)
    inp("L0_ffn_up_w", [INTER, D], F32)
    inp("L0_ffn_down_w", [D, INTER], F32)
    inp("c_gate_s_t", [128, 44], F32)
    inp("c_up_s_t", [128, 44], F32)
    inp("c_down_s_t", [128, 16], F32)
    inp("L1_attn_Wq", [NL1, D, D], BF16)
    inp("L1_attn_Wk", [NL1, D, D], BF16)
    inp("L1_attn_Wv", [NL1, D, D], BF16)
    inp("L1_attn_Wo", [NL1, D, D], BF16)
    inp("c_bq_t", [128, NL1 * KCD], F32)
    inp("c_bk_t", [128, NL1 * KCD], F32)
    inp("c_bo_t", [128, NL1 * KCD], F32)
    inp("c_bv_row", [1, NL1 * D], BF16)
    inp("c_bo_row", [1, D], BF16)
    inp("c_gate_wT", [D, NEXP], F32)
    inp("L1_ffn_w1", [NEXP, 2 * MI, D], F32)
    inp("L1_ffn_w2", [NEXP, D, MI], F32)
    inp("c_cos", [T, 64], F32)
    inp("c_sin", [T, 64], F32)
    inp("c_ones", [1, 128], BF16)
    inp("c_causal", [128, 128], F32)

    scratch("sc_xn", [T, D], BF16)
    scratch("sc_x1", [T, D], F32)
    scratch("sc_xn1", [T, D], BF16)
    scratch("sc_xlo", [T, D], BF16)
    scratch("sc_s", [T, D], F32)
    scratch("sc_gate_w", [INTER, D], BF16)
    scratch("sc_up_w", [INTER, D], BF16)
    scratch("sc_down_w", [D, INTER], BF16)
    scratch("sc_w1", [NEXP, 2 * MI, D], BF16)
    scratch("sc_w2", [NEXP, D, MI], BF16)
    scratch("sc_qf", [D, T], BF16)
    scratch("sc_kf", [D, T], BF16)
    scratch("sc_xa0", [D, T], BF16)
    scratch("sc_xa1", [D, T], BF16)
    scratch("sc_v", [T, D], BF16)
    scratch("sc_attn", [T, D], BF16)
    io["out"] = nc.dram_tensor("out", [T, D], F32, kind="ExternalOutput")
    return io


def build_program(dump=(), max_phase=99):
    nc = bass.Bass("TRN2", target_bir_lowering=False)
    dump = set(dump)
    io = declare_io(nc, dump)
    sb_shapes = {"sb_q_fm": [128, KCD * T], "sb_k_fm": [128, 8 * T],
                 "sb_v_tm": [128, NT * 512], "sb_attn_fm": [128, KCD * T]}
    for nm in dump:
        if nm.startswith("sb_") and nm in sb_shapes:
            io[nm] = nc.dram_tensor(nm, sb_shapes[nm], BF16, kind="ExternalOutput")
    with ExitStack() as es:
        cx = Cx(nc, es)
        sb_map = {}

        cx.cast_jobs, cx.mark_l0 = build_cast_jobs(io)
        cx.p0sem = cx.sem("p0cast")
        if max_phase >= 1:
            with cx.phase(), nc.named_scope("p1"):
                p1_ln0(cx, io)
        if max_phase >= 2:
            with cx.phase():
                l0scope = cx.cur
                attn_fm = None
                with cx.phase():
                    q_fm = cx.sbuf("q_fm", [128, KCD * T], BF16)
                    with nc.named_scope("p2"):
                        k_fm, v_tm = p2_l0_qkv(cx, io, q_fm)
                    sb_map.update({"sb_q_fm": q_fm, "sb_k_fm": k_fm,
                                   "sb_v_tm": v_tm})
                    if max_phase >= 3:
                        attn_fm = cx.sbuf("attn_fm", [128, KCD * T], BF16,
                                          scope=l0scope, side="right")
                        def _bg(g):
                            for dst_ap, src_ap in cx.cast_jobs[:cx.mark_l0]:
                                cx.p0sem.inc(g.dma_start(dst_ap, src_ap), 16)
                        with cx.phase(), nc.named_scope("p3"):
                            p3dbg = p3_l0_attn(cx, io, q_fm, k_fm, v_tm, attn_fm,
                                               bg=_bg)
                            _emit_dumps(cx, io, dump, p3dbg)
                    _emit_dumps(cx, io, dump, sb_map)
                if max_phase >= 4:
                    with cx.phase(), nc.named_scope("p4"):
                        p4_l0_proj(cx, io, attn_fm)
                if attn_fm is not None:
                    _emit_dumps(cx, io, dump, {"sb_attn_fm": attn_fm})
        if max_phase >= 5:
            with cx.phase(), nc.named_scope("p5"):
                p5_l0_ffn(cx, io)
        if max_phase >= 6:
            with cx.phase(), nc.named_scope("p6"):
                p6_l1_attn(cx, io)
        if max_phase >= 7:
            combine_sb = cx.sbuf("combine", [128, NT * NEXP], F32, persistent=True)
            with cx.phase(), nc.named_scope("p7"):
                p7_gate(cx, io, combine_sb)
            if max_phase >= 8:
                with cx.phase(), nc.named_scope("p8"):
                    p8_moe(cx, io, combine_sb)
    return nc


def _emit_dumps(cx, io, dump, sb_map):
    todo = sorted(nm for nm in dump if nm in sb_map and sb_map[nm] is not None)
    if not todo:
        return
    nc = cx.nc
    dmp = cx.sem("dmp" + todo[0])
    for nm in todo:
        if nm not in io:
            t = sb_map[nm]
            io[nm] = nc.dram_tensor(nm, list(t.shape), t.dtype,
                                    kind="ExternalOutput")
    with nc.Block() as b:
        @b.sync
        def _(s):
            for nm in todo:
                dmp.inc(s.dma_start(io[nm].ap(), sb_map[nm][:]), 16)
            s.wait_ge(dmp.h, dmp.n)


def _host_constants():
    theta = 1.0 / (10000.0 ** (np.arange(0, HD, 2, dtype=np.float64) / HD))
    pos = np.arange(T, dtype=np.float64)
    fr = np.outer(pos, theta)
    cos1 = np.cos(fr)
    sin1 = np.sin(fr)
    cos = np.concatenate([cos1, cos1], axis=1).astype(np.float32)
    sinf = np.concatenate([-sin1, sin1], axis=1).astype(np.float32)
    ones = np.ones((1, 128), dtype=ml_dtypes.bfloat16)
    causal = np.where(np.arange(128)[:, None] >= np.arange(128)[None, :],
                      np.float32(0), np.float32(-1e30))
    return cos, sinf, ones, causal


def make_inmaps(inputs, dump=()):
    cos, sinf, ones, causal = _host_constants()
    base = {}
    base["L0_attn_Wq"] = np.asarray(inputs["L0_attn_Wq"])
    base["L0_attn_Wk"] = np.asarray(inputs["L0_attn_Wk"])
    base["L0_attn_Wv"] = np.asarray(inputs["L0_attn_Wv"])
    base["L0_attn_Wo"] = np.asarray(inputs["L0_attn_Wo"])
    base["L0_ffn_gate_w"] = np.asarray(inputs["L0_ffn_gate_w"])
    base["L0_ffn_up_w"] = np.asarray(inputs["L0_ffn_up_w"])
    base["L0_ffn_down_w"] = np.asarray(inputs["L0_ffn_down_w"])
    base["c_gate_s_t"] = np.ascontiguousarray(
        np.asarray(inputs["L0_ffn_gate_s"]).reshape(44, 128).T)
    base["c_up_s_t"] = np.ascontiguousarray(
        np.asarray(inputs["L0_ffn_up_s"]).reshape(44, 128).T)
    base["c_down_s_t"] = np.ascontiguousarray(
        np.asarray(inputs["L0_ffn_down_s"]).reshape(16, 128).T)
    for nm in ["L1_attn_Wq", "L1_attn_Wk", "L1_attn_Wv", "L1_attn_Wo"]:
        base[nm] = np.asarray(inputs[nm])
    for nm, key in [("c_bq_t", "L1_attn_bq"), ("c_bk_t", "L1_attn_bk"),
                    ("c_bo_t", "L1_attn_bo")]:
        b = np.asarray(inputs[key]).astype(np.float32)
        base[nm] = np.ascontiguousarray(
            b.reshape(NL1, KCD, 128).transpose(2, 0, 1).reshape(128, NL1 * KCD))
    base["c_bv_row"] = np.ascontiguousarray(
        np.asarray(inputs["L1_attn_bv"]).reshape(1, NL1 * D))
    base["c_bo_row"] = np.ascontiguousarray(
        np.asarray(inputs["L1_attn_bo"])[NL1 - 1].reshape(1, D))
    base["c_gate_wT"] = np.ascontiguousarray(
        np.asarray(inputs["L1_ffn_gate_w"]).T)
    base["L1_ffn_w1"] = np.asarray(inputs["L1_ffn_w1"])
    base["L1_ffn_w2"] = np.asarray(inputs["L1_ffn_w2"])
    base["c_cos"] = cos
    base["c_sin"] = sinf
    base["c_ones"] = ones
    base["c_causal"] = causal
    x = np.asarray(inputs["L0_attn_x"])
    maps = []
    for b in range(8):
        m = dict(base)
        m["x"] = np.ascontiguousarray(x[b])
        maps.append(m)
    return maps


def kernel(**inputs):
    nc = build_program(dump=DUMP, max_phase=MAX_PHASE)
    maps = make_inmaps(inputs, dump=DUMP)
    res = run_bass_kernel_spmd(nc, maps, core_ids=list(range(8)))
    return np.stack([np.asarray(res.results[i]["out"]) for i in range(8)])
